# version 07
# baseline (speedup 1.0000x reference)
import numpy as np
from contextlib import ExitStack

import concourse.bass as bass
import concourse.mybir as mybir
from concourse.bass_utils import run_bass_kernel_spmd

F32 = mybir.dt.float32
BF16 = mybir.dt.bfloat16
AF = mybir.ActivationFunctionType
ALU = mybir.AluOpType
P = 128


class Cfg:
    def __init__(self, T=4096, D=1024, DI=1024, S=16, DFF=4096, NCH=512, CC=512):
        self.T = T
        self.D = D
        self.DI = DI
        self.S = S
        self.DFF = DFF
        self.NCH = NCH
        self.CC = CC
        self.NJ = T // NCH
        self.ND = D // P
        self.NI = DI // P
        self.NWF = 2 * self.NI
        self.NF = DFF // P
        self.TH = T // 2
        self.TC = T // 2
        self.NPI = self.NI // 2
        self.NCC = self.TC // CC
        self.NTS = CC // P
        self.XD = 1 + 2 * S
        self.NW2 = min(512, D)
        self.NQS = self.NW2 // P
        self.NF2N = D // self.NW2


def build_program(cfg, dbg=False):
    c = cfg
    T, D, DI, S, DFF = c.T, c.D, c.DI, c.S, c.DFF
    NCH, NJ, ND, NI, NWF, NF = c.NCH, c.NJ, c.ND, c.NI, c.NWF, c.NF
    TH, TC, NPI, CC, NCC, NTS, XD = c.TH, c.TC, c.NPI, c.CC, c.NCC, c.NTS, c.XD
    NW2, NQS, NF2N = c.NW2, c.NQS, c.NF2N
    NK2 = 2 * NI
    NBC = 3
    NW = min(512, TH)
    NTHC = TH // NW
    NWC = min(512, TC)
    NCCH = TC // NWC
    WOC = min(512, TC)
    NWOC = TC // WOC
    NWIN_CS = 2 * DI // NCH
    NST = 2 * NPI * S
    NTP = NTS // 2
    PH_A, PH_C = 5, 10

    nc = bass.Bass("TRN2", target_bir_lowering=False, debug=False,
                   detect_race_conditions=False)

    xT = nc.dram_tensor("xT", [D, T], F32, kind="ExternalInput")
    xTh = nc.dram_tensor("xTh", [D, TC], F32, kind="ExternalInput")
    w_in = nc.dram_tensor("w_in", [D, 2 * DI], F32, kind="ExternalInput")
    conv_w = nc.dram_tensor("conv_w", [DI, 4], F32, kind="ExternalInput")
    conv_b = nc.dram_tensor("conv_b", [P, DI // P], F32, kind="ExternalInput")
    w_xp = nc.dram_tensor("w_xp", [DI, XD], F32, kind="ExternalInput")
    w_dt = nc.dram_tensor("w_dt", [P, DI // P], F32, kind="ExternalInput")
    b_dt = nc.dram_tensor("b_dt", [P, DI // P], F32, kind="ExternalInput")
    a_log = nc.dram_tensor("a_log", [P, (DI // P) * S], F32, kind="ExternalInput")
    dp = nc.dram_tensor("dp", [P, DI // P], F32, kind="ExternalInput")
    w_out = nc.dram_tensor("w_out", [2 * DI, D], F32, kind="ExternalInput")
    w1t = nc.dram_tensor("w1t", [D, DFF], F32, kind="ExternalInput")
    b1 = nc.dram_tensor("b1", [P, DFF // P], F32, kind="ExternalInput")
    w2t = nc.dram_tensor("w2t", [DFF, D], F32, kind="ExternalInput")
    b2 = nc.dram_tensor("b2", [1, D], F32, kind="ExternalInput")
    sel = nc.dram_tensor("sel", [P, 2], F32, kind="ExternalInput")
    w2g = nc.dram_tensor("w2g", [32 * P, 1024], F32, kind="ExternalInput")
    out = nc.dram_tensor("out", [TC, D], F32, kind="ExternalOutput")

    xc_dram = nc.dram_tensor("xc_dram", [DI, T], BF16, kind="Internal")
    z_dram = nc.dram_tensor("z_dram", [DI, T], BF16, kind="Internal")
    rowscr = nc.dram_tensor("rowscr", [1, T], F32, kind="Internal")
    cc1_in = nc.dram_tensor("cc1_in", [XD, T], F32, kind="Internal")
    cc1_out = nc.dram_tensor("cc1_out", [XD, T], F32, kind="Internal")
    xdbl_bf = nc.dram_tensor("xdbl_bf", [XD, T], BF16, kind="Internal")
    ygin = nc.dram_tensor("ygin", [DI, T], BF16, kind="Internal")
    ygout = nc.dram_tensor("ygout", [2 * DI, T], BF16, kind="Internal")
    wq2_dram = nc.dram_tensor("wq2_dram", [DFF, D], BF16, kind="Internal")
    g1d = nc.dram_tensor("g1d", [P, ND], F32, kind="Internal")
    g2D2 = nc.dram_tensor("g2D2", [P, 32], F32, kind="Internal")
    g2d3 = nc.dram_tensor("g2d3", [1, 4], F32, kind="Internal")
    chl_d = nc.dram_tensor("chl_d", [1, 3], F32, kind="Internal")
    thr_d = nc.dram_tensor("thr_d", [1, 2], F32, kind="Internal")
    row2_d = nc.dram_tensor("row2_d", [2, TC], F32, kind="Internal")

    if dbg:
        dbg_xc = nc.dram_tensor("dbg_xc", [DI, T], BF16, kind="ExternalOutput")
        dbg_z = nc.dram_tensor("dbg_z", [DI, T], BF16, kind="ExternalOutput")
        dbg_xdbl = nc.dram_tensor("dbg_xdbl", [XD, T], F32, kind="ExternalOutput")
        dbg_yg = nc.dram_tensor("dbg_yg", [2 * DI, T], BF16, kind="ExternalOutput")
        dbg_xres = nc.dram_tensor("dbg_xres", [D, TC], F32, kind="ExternalOutput")
        dbg_thr = nc.dram_tensor("dbg_thr", [1, 2], F32, kind="ExternalOutput")
        dbg_dl = nc.dram_tensor("dbg_dl", [P, 2 * TH], F32, kind="ExternalOutput")
        dbg_cr = nc.dram_tensor("dbg_cr", [P, NI * S * 2], F32, kind="ExternalOutput")
        dbg_xcp = nc.dram_tensor("dbg_xcp", [P, 2 * TH], F32, kind="ExternalOutput")
        dbg_zp = nc.dram_tensor("dbg_zp", [P, 2 * TH], F32, kind="ExternalOutput")

    w2g_v = w2g.ap().rearrange("(g p) t -> p g t", p=P)
    xT_v = xT.ap().rearrange("(i p) t -> p i t", p=P)
    xTh_v = xTh.ap().rearrange("(i p) t -> p i t", p=P)
    w_in_v = w_in.ap().rearrange("(i p) f -> p i f", p=P)
    w_xp_v = w_xp.ap().rearrange("(i p) f -> p i f", p=P)
    a_log_v = a_log.ap().rearrange("p (i s) -> p i s", i=DI // P)
    conv_w_v = conv_w.ap().rearrange("(i p) k -> p i k", p=P)
    xc_v = xc_dram.ap().rearrange("(i p) t -> p i t", p=P)
    z_v = z_dram.ap().rearrange("(i p) t -> p i t", p=P)
    w_out_v = w_out.ap().rearrange("(i p) f -> p i f", p=P)
    w1t_v = w1t.ap().rearrange("(i p) f -> p i f", p=P)
    w2t_v = w2t.ap().rearrange("(i p) f -> p i f", p=P)
    ygin_v = ygin.ap().rearrange("(i p) t -> p i t", p=P)
    ygout_v = ygout.ap().rearrange("(i p) t -> p i t", p=P)
    wq2_v = wq2_dram.ap().rearrange("(i p) f -> p i f", p=P)

    with ExitStack() as top:
        E = top.enter_context
        sem_names = [
            "sWL", "sIO", "sPRE", "sCSm", "sCSe", "sNA",
            "sXL0", "sXL1", "sSQ", "sST", "sEV1", "sVAR", "sRST", "sRW",
            "sRB0", "sRB1",
            "sWIN", "sEVC", "sZS0", "sZS1", "sCV", "sXCE", "sXCS0", "sXCS1",
            "sXD", "sXDE",
            "sGW0", "sGW1", "sGA", "sGR", "sTH", "sCD0", "sCD1", "sCC1",
            "sXBF", "sPH",
            "sBC0", "sBC1", "sBC2", "sSC", "sABC", "sZL0", "sZL1",
            "sA1", "sDL", "sDA", "sPR", "sYA", "sYG",
            "sSZ", "sYS0", "sYS1", "sAT", "sU",
            "sCL", "sWO", "sXR", "sSQ2", "sST2", "sEV2", "sVAR2", "sRS2",
            "sRW2", "sRB2", "sXN", "sQL", "sQ1", "sQ2", "sQ2S",
            "sF1", "sHE", "sW2L0", "sW2L1", "sF2K", "sF2", "sOE",
            "sOS0", "sOS1", "sOSD", "sYGL0", "sYGL1", "sYGC",
            "sG2L", "sG2A", "sG2S", "sG2E", "sG2D", "sG2X", "sG2C", "sCHB",
        ]
        sem = {n: E(nc.semaphore(n)) for n in sem_names}
        sXL = [sem["sXL0"], sem["sXL1"]]
        sRB = [sem["sRB0"], sem["sRB1"]]
        sZSs = [sem["sZS0"], sem["sZS1"]]
        sXCSs = [sem["sXCS0"], sem["sXCS1"]]
        sGW = [sem["sGW0"], sem["sGW1"]]
        sCD = [sem["sCD0"], sem["sCD1"]]
        sBC = [sem["sBC0"], sem["sBC1"], sem["sBC2"]]
        sZL = [sem["sZL0"], sem["sZL1"]]
        sYSs = [sem["sYS0"], sem["sYS1"]]
        sW2L = [sem["sW2L0"], sem["sW2L1"]]
        sOS = [sem["sOS0"], sem["sOS1"]]
        sYGL = [sem["sYGL0"], sem["sYGL1"]]

        ones_bf = E(nc.sbuf_tensor("ones_bf", [P, 1], BF16))
        onesr_bf = E(nc.sbuf_tensor("onesr_bf", [1, P], BF16))
        ident_bf = E(nc.sbuf_tensor("ident_bf", [P, P], BF16))
        idshift = E(nc.sbuf_tensor("idshift", [P, NQS, NW2], BF16))
        iota_r = E(nc.sbuf_tensor("iota_r", [P, 1], F32))
        scr1 = E(nc.sbuf_tensor("scr1", [P, 1], F32))
        negA = E(nc.sbuf_tensor("negA", [P, NI, S], F32))
        carry = E(nc.sbuf_tensor("carry", [P, NI, S, 2], F32))
        wdt_c = E(nc.sbuf_tensor("wdt_c", [P, NI], F32))
        bdt_c = E(nc.sbuf_tensor("bdt_c", [P, NI], F32))
        dp_c = E(nc.sbuf_tensor("dp_c", [P, NI], F32))
        convb_c = E(nc.sbuf_tensor("convb_c", [P, NI], F32))
        thr_c = E(nc.sbuf_tensor("thr_c", [P, 2], F32))
        nthr_c = E(nc.sbuf_tensor("nthr_c", [P, 2], F32))
        chl_c = E(nc.sbuf_tensor("chl_c", [P, 3], F32))
        gsc = E(nc.sbuf_tensor("gsc", [1, 4], F32))
        eps_c = E(nc.sbuf_tensor("eps_c", [P, 1], F32))
        selc = E(nc.sbuf_tensor("selc", [P, 2], F32))

        block = E(nc.Block())

        with ExitStack() as pa:
            EA = pa.enter_context
            iota_c = EA(nc.sbuf_tensor("iota_c", [P, max(NW2, P)], F32))
            w_in_bf = EA(nc.sbuf_tensor("w_in_bf", [P, ND, 2 * DI], BF16))
            wxp_bf = EA(nc.sbuf_tensor("wxp_bf", [P, NI, XD], BF16))
            diag_bf = EA(nc.sbuf_tensor("diag_bf", [P, NI * 4, P], BF16))
            cwcol = EA(nc.sbuf_tensor("cwcol", [P, NI, 4], F32))
            xb = EA(nc.sbuf_tensor("xb", [P, 2, ND, NCH], BF16))
            sq = EA(nc.sbuf_tensor("sq", [P, ND, NCH], BF16))
            xc_pre = EA(nc.sbuf_tensor("xc_pre", [P, NI, 3 + T], BF16))
            z_stage = EA(nc.sbuf_tensor("z_stage", [P, 2, NI, NCH], BF16))
            xc_stage = EA(nc.sbuf_tensor("xc_stage", [P, 2, NI, NCH], BF16))
            rstd_bc = EA(nc.sbuf_tensor("rstd_bc", [P, 2, NCH], F32))
            m_row = EA(nc.sbuf_tensor("m_row", [1, 2, NCH], F32))
            e2_row = EA(nc.sbuf_tensor("e2_row", [1, 2, NCH], F32))
            var_row = EA(nc.sbuf_tensor("var_row", [1, 2, NCH], F32))
            rstd_row = EA(nc.sbuf_tensor("rstd_row", [1, 2, NCH], F32))
            negm_bf = EA(nc.sbuf_tensor("negm_bf", [1, 2, NCH], BF16))
            cs_row = EA(nc.sbuf_tensor("cs_row", [1, 2 * DI], F32))
            cs_bf = EA(nc.sbuf_tensor("cs_bf", [1, 2 * DI], BF16))
            xdbl_st = EA(nc.sbuf_tensor("xdbl_st", [XD, 2, NCH], F32))
            ps_sx = EA(nc.psum_tensor("ps_sx", [1, NCH], F32))
            ps_sq = EA(nc.psum_tensor("ps_sq", [1, NCH], F32))
            ps_win = [EA(nc.psum_tensor(f"ps_win{k}", [P, NCH], F32)) for k in range(2)]
            ps_cv = [EA(nc.psum_tensor(f"ps_cv{k}", [P, NCH], F32)) for k in range(2)]
            ps_xd = EA(nc.psum_tensor("ps_xd", [XD, NCH], F32))

            @block.gpsimd
            def _(g):
                g.dma_start(out=w_in_bf[:, :, :], in_=w_in_v).then_inc(sem["sWL"], 16)
                g.dma_start(out=wxp_bf[:, :, :], in_=w_xp_v).then_inc(sem["sWL"], 16)
                g.dma_start(out=convb_c[:, 0:NI], in_=conv_b.ap()).then_inc(sem["sWL"], 16)
                g.dma_start(out=wdt_c[:, 0:NI], in_=w_dt.ap()).then_inc(sem["sWL"], 16)
                g.dma_start(out=bdt_c[:, 0:NI], in_=b_dt.ap()).then_inc(sem["sWL"], 16)
                g.dma_start(out=dp_c[:, 0:NI], in_=dp.ap()).then_inc(sem["sWL"], 16)
                g.dma_start(out=cwcol[:, :, :], in_=conv_w_v).then_inc(sem["sWL"], 16)
                g.dma_start(out=negA[:, :, :], in_=a_log_v).then_inc(sem["sWL"], 16)
                g.dma_start(out=selc[:, :], in_=sel.ap()).then_inc(sem["sWL"], 16)
                g.iota(iota_c[:, :], pattern=[[1, iota_c.shape[1]]], base=0,
                       channel_multiplier=0, allow_small_or_imprecise_dtypes=True)
                g.iota(iota_r[:, :], pattern=[[0, 1]], base=0,
                       channel_multiplier=1, allow_small_or_imprecise_dtypes=True).then_inc(sem["sIO"], 1)
                for j in range(NJ):
                    if j >= 2:
                        g.wait_ge(sem["sWIN"], NWF * (j - 1))
                        g.wait_ge(sem["sSQ"], j - 1)
                    g.dma_start(out=xb[:, j % 2, :, :],
                                in_=xT_v[:, :, j * NCH:(j + 1) * NCH]).then_inc(sXL[j % 2], 16)
                    if j >= 1:
                        g.wait_ge(sem["sXCE"], NI * j)
                        g.dma_start(out=xc_v[:, :, (j - 1) * NCH:j * NCH],
                                    in_=xc_stage[:, (j - 1) % 2, :, :]).then_inc(sXCSs[(j - 1) % 2], 16)
                g.wait_ge(sem["sXCE"], NI * NJ)
                g.dma_start(out=xc_v[:, :, (NJ - 1) * NCH:NJ * NCH],
                            in_=xc_stage[:, (NJ - 1) % 2, :, :]).then_inc(sXCSs[(NJ - 1) % 2], 16)
                g.wait_ge(sCD[0], 16 * ((NJ + 1) // 2))
                g.wait_ge(sCD[1], 16 * (NJ // 2))
                g.collective_compute(
                    "AllReduce", ALU.add,
                    replica_groups=[[0, 1], [2, 3], [4, 5], [6, 7]],
                    ins=[cc1_in.ap()], outs=[cc1_out.ap()],
                ).then_inc(sem["sCC1"])
                g.wait_ge(sem["sCC1"], 1)
                g.dma_start(out=xdbl_bf.ap(), in_=cc1_out.ap(), max_dma_last_dim=8192).then_inc(sem["sXBF"], 16)

            @block.vector
            def _(v):
                v.wait_ge(sem["sIO"], 1)
                v.memset(ones_bf[:, :], 1.0)
                v.memset(onesr_bf[:, :], 1.0)
                v.tensor_tensor(out=ident_bf[:, :], in0=iota_c[:, 0:P],
                                in1=iota_r[:, :].broadcast_to([P, P]), op=ALU.is_equal)
                for q in range(NQS):
                    v.tensor_scalar(out=scr1[:, :], in0=iota_r[:, :], scalar1=float(P * q),
                                    scalar2=None, op0=ALU.add)
                    v.tensor_tensor(out=idshift[:, q, :], in0=iota_c[:, 0:NW2],
                                    in1=scr1[:, :].broadcast_to([P, NW2]), op=ALU.is_equal)
                for i in range(NI):
                    v.memset(xc_pre[:, i, 0:3], 0.0)
                v.memset(eps_c[:, :], 1e-5).then_inc(sem["sPRE"], 1)
                v.wait_ge(sem["sWL"], 16 * 9)
                for i in range(NI):
                    for k in range(4):
                        v.tensor_tensor(out=diag_bf[:, i * 4 + k, :], in0=ident_bf[:, :],
                                        in1=cwcol[:, i, k:k + 1].broadcast_to([P, P]),
                                        op=ALU.mult).then_inc(sem["sPRE"], 1)
                for j in range(NJ):
                    jj = j % 2
                    v.wait_ge(sem["sEV1"], j + 1)
                    v.tensor_tensor(out=var_row[:, jj, :], in0=m_row[:, jj, :], in1=m_row[:, jj, :], op=ALU.mult)
                    v.tensor_tensor(out=var_row[:, jj, :], in0=e2_row[:, jj, :], in1=var_row[:, jj, :], op=ALU.subtract).then_inc(sem["sVAR"], 1)
                    for ft in range(NWF):
                        ftg = j * NWF + ft
                        v.wait_ge(sem["sWIN"], ftg + 1)
                        v.wait_ge(sRB[j % 2], 16 * (j // 2 + 1))
                        if ft == NI and j >= 2:
                            v.wait_ge(sZSs[j % 2], 16 * ((j - 2) // 2 + 1))
                        dest = xc_pre[:, ft, 3 + j * NCH:3 + (j + 1) * NCH] if ft < NI \
                            else z_stage[:, jj, ft - NI, :]
                        v.tensor_tensor(out=dest, in0=ps_win[ftg % 2][:, :],
                                        in1=rstd_bc[:, jj, :], op=ALU.mult).then_inc(sem["sEVC"], 1)

            @block.scalar
            def _(a):
                for nchk in range(NWIN_CS):
                    a.wait_ge(sem["sCSm"], nchk + 1)
                    a.activation(out=cs_row[:, nchk * NCH:(nchk + 1) * NCH],
                                 in_=ps_win[nchk % 2][0:1, 0:NCH], func=AF.Identity).then_inc(sem["sCSe"], 1)
                a.activation(out=cs_bf[:, :], in_=cs_row[:, :], func=AF.Identity).then_inc(sem["sCSe"], 1)

            @block.tensor
            def _(t):
                t.wait_ge(sem["sWL"], 16 * 9)
                t.wait_ge(sem["sPRE"], 1)
                for nchk in range(NWIN_CS):
                    if nchk >= 2:
                        t.wait_ge(sem["sCSe"], nchk - 1)
                    for kt in range(ND):
                        t.matmul(ps_win[nchk % 2][0:1, 0:NCH], ones_bf[:, :],
                                 w_in_bf[:, kt, nchk * NCH:(nchk + 1) * NCH],
                                 start=(kt == 0), stop=(kt == ND - 1)).then_inc(
                                     sem["sCSm"], 1 if kt == ND - 1 else 0)
                for j in range(NJ):
                    jj = j % 2
                    t.wait_ge(sem["sSQ"], j + 1)
                    if j >= 1:
                        t.wait_ge(sem["sEV1"], j)
                    for kt in range(ND):
                        t.matmul(ps_sx[:, :], ones_bf[:, :], xb[:, jj, kt, :],
                                 start=(kt == 0), stop=(kt == ND - 1))
                    for kt in range(ND):
                        t.matmul(ps_sq[:, :], ones_bf[:, :], sq[:, kt, :],
                                 start=(kt == 0), stop=(kt == ND - 1)).then_inc(
                                     sem["sST"], 1 if kt == ND - 1 else 0)
                    t.wait_ge(sem["sRST"], j + 1)
                    if j == 0:
                        t.wait_ge(sem["sCSe"], NWIN_CS + 1)
                    for ft in range(NWF):
                        ftg = j * NWF + ft
                        if ftg >= 2:
                            t.wait_ge(sem["sEVC"], ftg - 1)
                        for kt in range(ND):
                            t.matmul(ps_win[ftg % 2][:, :], w_in_bf[:, kt, ft * P:(ft + 1) * P],
                                     xb[:, jj, kt, :], start=(kt == 0), stop=False)
                        t.matmul(ps_win[ftg % 2][:, :], cs_bf[0:1, ft * P:(ft + 1) * P],
                                 negm_bf[0:1, jj, :], start=False, stop=True).then_inc(sem["sWIN"], 1)
                    if j == 0:
                        t.wait_ge(sem["sPRE"], 1 + 4 * NI)
                    for i in range(NI):
                        ci = j * NI + i
                        t.wait_ge(sem["sEVC"], j * NWF + i + 1)
                        if ci >= 2:
                            t.wait_ge(sem["sXCE"], ci - 1)
                        for k in range(4):
                            t.matmul(ps_cv[ci % 2][:, :], diag_bf[:, i * 4 + k, :],
                                     xc_pre[:, i, j * NCH + k:j * NCH + k + NCH],
                                     start=(k == 0), stop=(k == 3)).then_inc(
                                         sem["sCV"], 1 if k == 3 else 0)
                    t.wait_ge(sem["sXCE"], NI * (j + 1))
                    if j >= 2:
                        t.wait_ge(sem["sXDE"], j - 1)
                    for i in range(NI):
                        t.matmul(ps_xd[:, :], wxp_bf[:, i, :], xc_stage[:, jj, i, :],
                                 start=(i == 0), stop=(i == NI - 1)).then_inc(
                                     sem["sXD"], 1 if i == NI - 1 else 0)

            @block.scalar
            def _(a):
                for j in range(NJ):
                    jj = j % 2
                    a.wait_ge(sXL[j % 2], 16 * (j // 2 + 1))
                    if j >= 1:
                        a.wait_ge(sem["sST"], j)
                    a.activation(out=sq[:, :, :], in_=xb[:, jj, :, :], func=AF.Square).then_inc(sem["sSQ"], 1)
                    a.wait_ge(sem["sST"], j + 1)
                    if j >= 2:
                        a.wait_ge(sem["sWIN"], NWF * (j - 1))
                    a.activation(out=m_row[:, jj, :], in_=ps_sx[:, :], func=AF.Identity, scale=1.0 / D)
                    a.activation(out=e2_row[:, jj, :], in_=ps_sq[:, :], func=AF.Identity, scale=1.0 / D).then_inc(sem["sEV1"], 1)
                    a.wait_ge(sem["sVAR"], j + 1)
                    if j == 0:
                        a.wait_ge(sem["sPRE"], 1)
                    if j >= 2:
                        a.wait_ge(sem["sRW"], 16 * (j - 1))
                    a.activation(out=rstd_row[:, jj, :], in_=var_row[:, jj, :], func=AF.Ln, bias=eps_c[0:1, :])
                    a.activation(out=rstd_row[:, jj, :], in_=rstd_row[:, jj, :], func=AF.Exp, scale=-0.5)
                    a.activation(out=negm_bf[:, jj, :], in_=m_row[:, jj, :], func=AF.Identity, scale=-1.0).then_inc(sem["sRST"], 1)
                    for i in range(NI):
                        ci = j * NI + i
                        a.wait_ge(sem["sCV"], ci + 1)
                        if j >= 2 and i == 0:
                            a.wait_ge(sXCSs[j % 2], 16 * ((j - 2) // 2 + 1))
                            a.wait_ge(sem["sXD"], j - 1)
                        a.activation(out=xc_stage[:, jj, i, :], in_=ps_cv[ci % 2][:, :],
                                     func=AF.Silu, bias=convb_c[:, i:i + 1]).then_inc(sem["sXCE"], 1)
                    a.wait_ge(sem["sXD"], j + 1)
                    if j >= 2:
                        a.wait_ge(sCD[j % 2], 16 * ((j - 2) // 2 + 1))
                    a.activation(out=xdbl_st[:, jj, :], in_=ps_xd[:, :], func=AF.Identity).then_inc(sem["sXDE"], 1)

            @block.sync
            def _(s):
                for j in range(NJ):
                    jj = j % 2
                    s.wait_ge(sem["sRST"], j + 1)
                    s.dma_start(out=rowscr.ap()[:, j * NCH:(j + 1) * NCH],
                                in_=rstd_row[:, jj, :]).then_inc(sem["sRW"], 16)
                    s.wait_ge(sem["sRW"], 16 * (j + 1))
                    if j >= 2:
                        s.wait_ge(sem["sEVC"], NWF * (j - 1))
                    s.dma_start(out=rstd_bc[:, jj, :],
                                in_=rowscr.ap()[:, j * NCH:(j + 1) * NCH].broadcast_to([P, NCH])).then_inc(sRB[j % 2], 16)
                    s.wait_ge(sem["sEVC"], NWF * (j + 1))
                    s.dma_start(out=z_v[:, :, j * NCH:(j + 1) * NCH],
                                in_=z_stage[:, jj, :, :]).then_inc(sZSs[j % 2], 16)
                    s.wait_ge(sem["sXDE"], j + 1)
                    s.dma_start(out=cc1_in.ap()[:, j * NCH:(j + 1) * NCH],
                                in_=xdbl_st[:, jj, :]).then_inc(sCD[j % 2], 16)

        GL = 1024
        with ExitStack() as pa2:
            EA2 = pa2.enter_context
            gtile = EA2(nc.sbuf_tensor("gtile", [P, 2, DFF], F32))
            gabs = EA2(nc.sbuf_tensor("gabs", [P, DFF], F32))
            gcol1 = EA2(nc.sbuf_tensor("gcol1", [P, ND], F32))
            grow = EA2(nc.sbuf_tensor("grow", [1, P * ND], F32))
            g2wg = EA2(nc.sbuf_tensor("g2wg", [P, 2, GL], F32))
            g2wa = EA2(nc.sbuf_tensor("g2wa", [P, 2, GL], F32))
            g2scr = EA2(nc.sbuf_tensor("g2scr", [P, 2, GL], F32))
            g2ones = EA2(nc.sbuf_tensor("g2ones", [P, GL], F32))
            g2s1 = EA2(nc.sbuf_tensor("g2s1", [P, 32], F32))
            g2st2 = EA2(nc.sbuf_tensor("g2st2", [4, GL], F32))
            g2scr2 = EA2(nc.sbuf_tensor("g2scr2", [4, GL], F32))
            g2dum = EA2(nc.sbuf_tensor("g2dum", [4, GL], F32))
            g2s2r = EA2(nc.sbuf_tensor("g2s2r", [4, 1], F32))
            g2st3 = EA2(nc.sbuf_tensor("g2st3", [1, 4], F32))
            g2scr3 = EA2(nc.sbuf_tensor("g2scr3", [1, 4], F32))
            g2sc = EA2(nc.sbuf_tensor("g2sc", [1, 12], F32))

            @block.sync
            def _(s):
                s.wait_ge(sem["sWIN"], NWF * NJ)
                s.wait_ge(sem["sXD"], NJ)
                s.wait_ge(sXCSs[0], 16 * ((NJ + 1) // 2))
                s.wait_ge(sXCSs[1], 16 * (NJ // 2))
                s.wait_ge(sCD[0], 16 * ((NJ + 1) // 2))
                s.wait_ge(sCD[1], 16 * (NJ // 2))
                for kt in range(ND):
                    if kt >= 2:
                        s.wait_ge(sem["sGA"], kt - 1)
                    s.dma_start(out=gtile[:, kt % 2, :], in_=w1t_v[:, kt, :]).then_inc(sGW[kt % 2], 16)
                s.wait_ge(sem["sGA"], ND)
                s.dma_start(out=g1d.ap(), in_=gcol1[:, :]).then_inc(sem["sGR"], 16)
                s.wait_ge(sem["sGR"], 16)
                s.dma_start(out=grow[:, 0:P * ND],
                            in_=g1d.ap().rearrange("p n -> (p n)")[None, :]).then_inc(sem["sGR"], 16)
                s.wait_ge(sem["sTH"], 1)
                s.dma_start(out=thr_d.ap()[:, 0:1], in_=gsc[:, 2:3]).then_inc(sem["sGR"], 16)
                s.wait_ge(sem["sGR"], 48)
                s.dma_start(out=thr_c[:, 0:1],
                            in_=thr_d.ap()[:, 0:1].broadcast_to([P, 1])).then_inc(sem["sTH"], 16)
                for g in range(32):
                    if g >= 2:
                        s.wait_ge(sem["sG2A"], g - 1)
                    s.dma_start(out=g2wg[:, g % 2, :], in_=w2g_v[:, g, :]).then_inc(sem["sG2L"], 16)
                s.wait_ge(sem["sG2E"], 34)
                s.dma_start(out=g2D2.ap(), in_=g2s1[:, :]).then_inc(sem["sG2D"], 16)
                s.wait_ge(sem["sG2D"], 16)
                s.dma_start(out=g2st2[:, :],
                            in_=g2D2.ap().rearrange("(j r) c -> j (r c)", j=4)).then_inc(sem["sG2D"], 16)
                s.wait_ge(sem["sG2X"], 1)
                s.dma_start(out=g2d3.ap().rearrange("o f -> f o"), in_=g2s2r[:, :]).then_inc(sem["sG2D"], 16)
                s.wait_ge(sem["sG2D"], 48)
                s.dma_start(out=g2st3[:, :], in_=g2d3.ap()).then_inc(sem["sG2D"], 16)
                s.wait_ge(sem["sG2C"], 1)
                s.dma_start(out=chl_d.ap(), in_=g2sc[:, 9:12]).then_inc(sem["sCHB"], 16)
                s.wait_ge(sem["sCHB"], 16)
                s.dma_start(out=chl_c[:, :], in_=chl_d.ap().broadcast_to([P, 3])).then_inc(sem["sCHB"], 16)

            @block.scalar
            def _(a):
                for kt in range(ND):
                    a.wait_ge(sGW[kt % 2], 16 * (kt // 2 + 1))
                    act = a.activation(out=gabs[:, :], in_=gtile[:, kt % 2, :], func=AF.Abs,
                                       accum_out=gcol1[:, kt:kt + 1])
                    if kt < ND - 1:
                        act.then_inc(sem["sGA"], 1)
                a.activation(out=gabs[:, :], in_=gtile[:, (ND - 1) % 2, :], func=AF.Abs,
                             accum_out=gcol1[:, ND - 1:ND]).then_inc(sem["sGA"], 1)
                for g in range(32):
                    a.wait_ge(sem["sG2L"], 16 * (g + 1))
                    if g >= 2:
                        a.wait_ge(sem["sG2S"], g - 1)
                    a.activation(out=g2wa[:, g % 2, :], in_=g2wg[:, g % 2, :],
                                 func=AF.Abs).then_inc(sem["sG2A"], 1)
                a.wait_ge(sem["sWL"], 16 * 9)
                a.activation(out=negA[:, :, :], in_=negA[:, :, :], func=AF.Exp).then_inc(sem["sNA"], 1)
                a.wait_ge(sem["sTH"], 17)
                a.activation(out=nthr_c[:, 0:1], in_=thr_c[:, 0:1], func=AF.Identity, scale=-1.0).then_inc(sem["sTH"], 1)

            @block.vector
            def _(v):
                v.memset(g2ones[:, :], 1.0)
                v.wait_ge(sem["sGR"], 32)
                v.reduce_sum(gsc[:, 0:1], grow[:, 0:P * ND], axis=mybir.AxisListType.X)
                v.reduce_sum(gsc[:, 0:1], grow[:, 0:P * ND], axis=mybir.AxisListType.X)
                v.tensor_scalar(out=gsc[:, 2:3], in0=gsc[:, 0:1], scalar1=0.5 / (D * DFF),
                                scalar2=0.5e-5, op0=ALU.mult, op1=ALU.add)
                v.reduce_sum(gsc[:, 1:2], grow[:, 0:P * ND], axis=mybir.AxisListType.X)
                v.tensor_scalar(out=gsc[:, 2:3], in0=gsc[:, 0:1], scalar1=0.5 / (D * DFF),
                                scalar2=0.5e-5, op0=ALU.mult, op1=ALU.add).then_inc(sem["sTH"], 1)
                for g in range(32):
                    v.wait_ge(sem["sG2A"], g + 1)
                    v.tensor_tensor_scan(out=g2scr[:, g % 2, :], data0=g2ones[:, :],
                                         data1=g2wa[:, g % 2, :], initial=0.0,
                                         op0=ALU.mult, op1=ALU.add).then_inc(sem["sG2S"], 1)
                    if g >= 1:
                        v.tensor_copy(out=g2s1[:, g - 1:g],
                                      in_=g2scr[:, (g - 1) % 2, GL - 1:GL]).then_inc(sem["sG2E"], 1)
                v.tensor_copy(out=g2dum[0:4, :], in_=g2scr[0:4, 1, :])
                v.tensor_copy(out=g2s1[:, 31:32], in_=g2scr[:, 1, GL - 1:GL]).then_inc(sem["sG2E"], 1)
                v.tensor_copy(out=g2dum[0:4, :], in_=g2scr[0:4, 0, :])
                v.tensor_copy(out=g2s1[:, 30:31], in_=g2scr[:, 0, GL - 1:GL]).then_inc(sem["sG2E"], 1)
                v.tensor_copy(out=g2s1[:, 31:32], in_=g2scr[:, 1, GL - 1:GL]).then_inc(sem["sG2E"], 1)
                v.wait_ge(sem["sG2D"], 32)
                v.tensor_tensor_scan(out=g2scr2[:, :], data0=g2ones[0:4, :],
                                     data1=g2st2[:, :], initial=0.0,
                                     op0=ALU.mult, op1=ALU.add)
                v.tensor_copy(out=g2dum[:, :], in_=g2scr2[:, :])
                v.tensor_copy(out=g2s2r[:, :], in_=g2scr2[:, GL - 1:GL])
                v.tensor_copy(out=g2dum[:, :], in_=g2scr2[:, :])
                v.tensor_copy(out=g2s2r[:, :], in_=g2scr2[:, GL - 1:GL]).then_inc(sem["sG2X"], 1)
                v.wait_ge(sem["sG2D"], 64)
                v.tensor_tensor_scan(out=g2scr3[:, :], data0=g2ones[0:1, 0:4],
                                     data1=g2st3[:, :], initial=0.0,
                                     op0=ALU.mult, op1=ALU.add)

                def spacer():
                    v.tensor_copy(out=g2dum[0:1, :], in_=g2scr2[0:1, :])
                spacer()
                v.tensor_copy(out=g2sc[:, 0:1], in_=g2scr3[:, 3:4])
                spacer()
                v.tensor_copy(out=g2sc[:, 0:1], in_=g2scr3[:, 3:4])
                spacer()
                v.tensor_scalar(out=g2sc[:, 1:2], in0=g2sc[:, 0:1], scalar1=float(2.0 ** -22),
                                scalar2=None, op0=ALU.mult)
                spacer()
                v.tensor_scalar(out=g2sc[:, 2:3], in0=g2sc[:, 1:2], scalar1=float(np.float32(1e-5)),
                                scalar2=None, op0=ALU.add)
                spacer()
                v.tensor_scalar(out=g2sc[:, 3:4], in0=g2sc[:, 2:3], scalar1=0.5,
                                scalar2=None, op0=ALU.mult)
                spacer()
                v.tensor_scalar(out=g2sc[:, 5:6], in0=g2ones[0:1, 0:1], scalar1=float(2.0 ** -25),
                                scalar2=None, op0=ALU.add)
                spacer()
                v.tensor_scalar(out=g2sc[:, 8:9], in0=g2sc[:, 5:6], scalar1=-1.0,
                                scalar2=None, op0=ALU.add)
                spacer()
                v.tensor_scalar(out=g2sc[:, 8:9], in0=g2sc[:, 8:9], scalar1=-1.0,
                                scalar2=float(2.0 ** -25), op0=ALU.mult, op1=ALU.add)
                spacer()
                v.tensor_tensor(out=g2sc[:, 4:5], in0=g2sc[:, 2:3], in1=g2sc[:, 8:9], op=ALU.mult)
                spacer()
                v.tensor_tensor(out=g2sc[:, 9:10], in0=g2sc[:, 3:4], in1=g2sc[:, 4:5], op=ALU.add)
                spacer()
                v.tensor_tensor(out=g2sc[:, 6:7], in0=g2sc[:, 3:4], in1=g2sc[:, 9:10], op=ALU.subtract)
                spacer()
                v.tensor_tensor(out=g2sc[:, 10:11], in0=g2sc[:, 6:7], in1=g2sc[:, 4:5], op=ALU.add)
                spacer()
                v.tensor_scalar(out=g2sc[:, 11:12], in0=g2sc[:, 10:11], scalar1=-1.0,
                                scalar2=None, op0=ALU.mult)
                spacer()
                v.tensor_tensor(out=g2sc[:, 9:10], in0=g2sc[:, 3:4], in1=g2sc[:, 4:5], op=ALU.add)
                v.tensor_tensor(out=g2sc[:, 10:11], in0=g2sc[:, 6:7], in1=g2sc[:, 4:5], op=ALU.add)
                v.tensor_scalar(out=g2sc[:, 11:12], in0=g2sc[:, 10:11], scalar1=-1.0,
                                scalar2=None, op0=ALU.mult).then_inc(sem["sG2C"], 1)
                v.wait_ge(sem["sNA"], 1)
                v.tensor_scalar(out=negA[:, :, :], in0=negA[:, :, :], scalar1=-1.0,
                                scalar2=None, op0=ALU.mult).then_inc(sem["sNA"], 1)

            @block.sync
            def _(s):
                s.wait_ge(sZSs[0], 16 * ((NJ + 1) // 2))
                s.wait_ge(sZSs[1], 16 * (NJ // 2))
                s.wait_ge(sem["sTH"], 18)
                s.sem_inc(sem["sPH"], 1)
            @block.gpsimd
            def _(g):
                g.wait_ge(sem["sXBF"], 16)
                g.wait_ge(sXCSs[0], 16 * ((NJ + 1) // 2))
                g.wait_ge(sXCSs[1], 16 * (NJ // 2))
                g.sem_inc(sem["sPH"], 1)
            @block.vector
            def _(v):
                v.wait_ge(sem["sNA"], 2)
                v.wait_ge(sem["sEVC"], NWF * NJ)
                v.sem_inc(sem["sPH"], 1)
            @block.scalar
            def _(a):
                a.wait_ge(sem["sXDE"], NJ)
                a.wait_ge(sem["sGA"], ND)
                a.wait_ge(sem["sTH"], 18)
                a.sem_inc(sem["sPH"], 1)
            @block.tensor
            def _(t):
                t.wait_ge(sem["sXD"], NJ)
                t.sem_inc(sem["sPH"], 1)

        with ExitStack() as pb:
            EB = pb.enter_context
            b_bc = EB(nc.sbuf_tensor("b_bc", [P, NBC, TH], BF16))
            c_bc = EB(nc.sbuf_tensor("c_bc", [P, NBC, TH], BF16))
            a_bc = EB(nc.sbuf_tensor("a_bc", [P, TH], F32))
            delta = EB(nc.sbuf_tensor("delta", [P, 2, 2, TH], F32))
            arg_s = EB(nc.sbuf_tensor("arg_s", [P, 2, TH], F32))
            t_s = EB(nc.sbuf_tensor("t_s", [P, 2, TH], F32))
            dxc = EB(nc.sbuf_tensor("dxc", [P, 2, TH], BF16))
            dA = EB(nc.sbuf_tensor("dA", [P, 2, TH], F32))
            u_s = EB(nc.sbuf_tensor("u_s", [P, 2, TH], BF16))
            h_s = EB(nc.sbuf_tensor("h_s", [P, 2, 2, TH], BF16))
            xc_pair = EB(nc.sbuf_tensor("xc_pair", [P, 2, 2, TH], BF16))
            z_pair = EB(nc.sbuf_tensor("z_pair", [P, 2, 2, TH], BF16))
            ybuf = EB(nc.sbuf_tensor("ybuf", [P, TH], F32))
            yg_s = EB(nc.sbuf_tensor("yg_s", [P, 2, 2, TH], BF16))
            ps_y = [EB(nc.psum_tensor(f"ps_y{k}", [P, TH], F32)) for k in range(2)]

            def step_of(h, pi, s):
                return (h * NPI + pi) * S + s

            @block.sync
            def _(s):
                s.wait_ge(sem["sPH"], PH_A)
                for h in range(2):
                    t0 = h * TH
                    if h == 1:
                        s.wait_ge(sem["sA1"], NPI)
                    s.dma_start(out=a_bc[:, :],
                                in_=cc1_out.ap()[0:1, t0:t0 + TH].broadcast_to([P, TH])).then_inc(sem["sABC"], 16)
                    for pi in range(NPI):
                        gp = h * NPI + pi
                        if gp >= 2:
                            s.wait_ge(sem["sYG"], 2 * (gp - 1))
                        if dbg and gp == 4:
                            s.wait_ge(sem["sOSD"], 48)
                        s.dma_start(out=xc_pair[:, gp % 2, :, :],
                                    in_=xc_v[:, 2 * pi:2 * pi + 2, t0:t0 + TH]).then_inc(sZL[gp % 2], 16)
                        s.dma_start(out=z_pair[:, gp % 2, :, :],
                                    in_=z_v[:, 2 * pi:2 * pi + 2, t0:t0 + TH]).then_inc(sZL[gp % 2], 16)
                        for st in range(S):
                            stp = step_of(h, pi, st)
                            if stp >= NBC:
                                s.wait_ge(sem["sPR"], stp - NBC + 1)
                            s.dma_start(out=b_bc[:, stp % NBC, :],
                                        in_=xdbl_bf.ap()[1 + st:2 + st, t0:t0 + TH].broadcast_to([P, TH])).then_inc(sBC[stp % NBC], 16)
                            s.dma_start(out=c_bc[:, stp % NBC, :],
                                        in_=xdbl_bf.ap()[1 + S + st:2 + S + st, t0:t0 + TH].broadcast_to([P, TH])).then_inc(sBC[stp % NBC], 16)

            @block.scalar
            def _(a):
                a.wait_ge(sem["sPH"], PH_A)
                for h in range(2):
                    for pi in range(NPI):
                        gp = h * NPI + pi
                        a.wait_ge(sem["sABC"], 16 * (h + 1))
                        if gp >= 1:
                            a.wait_ge(sem["sDL"], gp)
                        for iw in range(2):
                            i = 2 * pi + iw
                            a.activation(out=arg_s[:, iw, :], in_=a_bc[:, :], func=AF.Identity,
                                         scale=wdt_c[:, i:i + 1], bias=bdt_c[:, i:i + 1])
                            a.activation(out=t_s[:, iw, :], in_=arg_s[:, iw, :], func=AF.Abs)
                            a.activation(out=t_s[:, iw, :], in_=t_s[:, iw, :], func=AF.Exp, scale=-1.0)
                            ln_i = a.activation(out=t_s[:, iw, :], in_=t_s[:, iw, :], func=AF.Ln, bias=1.0)
                        ln_i.then_inc(sem["sA1"], 1)
                        a.wait_ge(sZL[gp % 2], 32 * (gp // 2 + 1))
                        a.activation(out=z_pair[:, gp % 2, :, :], in_=z_pair[:, gp % 2, :, :],
                                     func=AF.Silu).then_inc(sem["sSZ"], 1)
                        a.wait_ge(sem["sDL"], gp + 1)
                        for st in range(S):
                            stp = step_of(h, pi, st)
                            for iw in range(2):
                                sub = 2 * stp + iw
                                if sub >= 2:
                                    a.wait_ge(sem["sSC"], sub - 1)
                                da_i = a.activation(out=dA[:, sub % 2, :], in_=delta[:, gp % 2, iw, :],
                                                    func=AF.Exp, scale=negA[:, 2 * pi + iw, st:st + 1])
                            da_i.then_inc(sem["sDA"], 1)

            @block.vector
            def _(v):
                v.wait_ge(sem["sPH"], PH_A)
                for h in range(2):
                    for pi in range(NPI):
                        gp = h * NPI + pi
                        v.wait_ge(sem["sA1"], gp + 1)
                        if gp >= 2:
                            v.wait_ge(sem["sDA"], S * (gp - 1))
                        if dbg and gp == 4:
                            v.wait_ge(sem["sOSD"], 48)
                        for iw in range(2):
                            v.scalar_tensor_tensor(out=delta[:, gp % 2, iw, :], in0=arg_s[:, iw, :],
                                                   scalar=0.0, in1=t_s[:, iw, :], op0=ALU.max, op1=ALU.add)
                        v.wait_ge(sZL[gp % 2], 32 * (gp // 2 + 1))
                        if gp >= 1:
                            v.wait_ge(sem["sU"], gp * 32)
                        for iw in range(2):
                            dxc_i = v.tensor_tensor(out=dxc[:, iw, :], in0=delta[:, gp % 2, iw, :],
                                                    in1=xc_pair[:, gp % 2, iw, :], op=ALU.mult)
                        dxc_i.then_inc(sem["sDL"], 1)
                        for st in range(S):
                            stp = step_of(h, pi, st)
                            v.wait_ge(sBC[stp % NBC], 32 * (stp // NBC + 1))
                            v.wait_ge(sem["sDA"], stp + 1)
                            if stp >= 2:
                                v.wait_ge(sem["sYA"], stp - 1)
                            for iw in range(2):
                                i = 2 * pi + iw
                                gu = gp * 32 + 2 * st + iw
                                v.wait_ge(sem["sU"], gu + 1)
                                v.tensor_tensor_scan(out=h_s[:, stp % 2, iw, :],
                                                     data0=dA[:, (2 * stp + iw) % 2, :], data1=u_s[:, iw, :],
                                                     initial=(0.0 if h == 0 else carry[:, i, st, 1:2]),
                                                     op0=ALU.mult, op1=ALU.add).then_inc(sem["sSC"], 1)
                            for iw in range(2):
                                i = 2 * pi + iw
                                if h == 0:
                                    v.tensor_copy(out=carry[:, i, st, :],
                                                  in_=h_s[:, stp % 2, iw, TH - 2:TH])
                                cm_i = v.tensor_tensor(out=h_s[:, stp % 2, iw, :], in0=h_s[:, stp % 2, iw, :],
                                                       in1=c_bc[:, stp % NBC, :], op=ALU.mult)
                            cm_i.then_inc(sem["sPR"], 1)
                        v.wait_ge(sem["sYA"], S * (gp + 1))
                        v.wait_ge(sem["sSZ"], gp + 1)
                        if gp >= 2:
                            v.wait_ge(sYSs[gp % 2], 16 * ((gp - 2) // 2 + 1))
                        for iw in range(2):
                            i = 2 * pi + iw
                            v.scalar_tensor_tensor(out=ybuf[:, :], in0=xc_pair[:, gp % 2, iw, :],
                                                   scalar=dp_c[:, i:i + 1], in1=ps_y[iw][:, :],
                                                   op0=ALU.mult, op1=ALU.add)
                            v.tensor_tensor(out=yg_s[:, gp % 2, iw, :], in0=ybuf[:, :],
                                            in1=z_pair[:, gp % 2, iw, :], op=ALU.mult).then_inc(sem["sYG"], 1)

            @block.tensor
            def _(t):
                t.wait_ge(sem["sPH"], PH_A)
                for h in range(2):
                    for pi in range(NPI):
                        gp = h * NPI + pi
                        for st in range(S):
                            stp = step_of(h, pi, st)
                            t.wait_ge(sem["sPR"], stp + 1)
                            if st == 0:
                                t.wait_ge(sem["sYG"], 2 * gp)
                            for iw in range(2):
                                for nw in range(NTHC):
                                    ya_i = t.matmul(ps_y[iw][:, nw * NW:(nw + 1) * NW], ident_bf[:, :],
                                                    h_s[:, stp % 2, iw, nw * NW:(nw + 1) * NW],
                                                    start=(st == 0), stop=(st == S - 1))
                            ya_i.then_inc(sem["sYA"], 1)

            @block.gpsimd
            def _(g):
                g.wait_ge(sem["sPH"], PH_A)
                if dbg:
                    g.wait_ge(sem["sDL"], 3)
                    g.dma_start(out=dbg_dl.ap(), in_=delta[:, 0, :, :].rearrange("p a t -> p (a t)")).then_inc(sem["sOSD"], 16)
                    g.wait_ge(sem["sSZ"], 3)
                    g.dma_start(out=dbg_xcp.ap(), in_=xc_pair[:, 0, :, :].rearrange("p a t -> p (a t)")).then_inc(sem["sOSD"], 16)
                    g.dma_start(out=dbg_zp.ap(), in_=z_pair[:, 0, :, :].rearrange("p a t -> p (a t)")).then_inc(sem["sOSD"], 16)
                for h in range(2):
                    if dbg and h == 1:
                        g.dma_start(out=dbg_cr.ap(), in_=carry[:, :, :, :].rearrange("p i s c -> p (i s c)")).then_inc(sem["sOSD"], 16)
                    for pi in range(NPI):
                        gp = h * NPI + pi
                        g.wait_ge(sem["sDL"], gp + 1)
                        for st in range(S):
                            stp = step_of(h, pi, st)
                            g.wait_ge(sBC[stp % NBC], 32 * (stp // NBC + 1))
                            for iw in range(2):
                                gu = gp * 32 + 2 * st + iw
                                if gu >= 2:
                                    g.wait_ge(sem["sSC"], gu - 1)
                                g.tensor_tensor(out=u_s[:, iw, :], in0=dxc[:, iw, :],
                                                in1=b_bc[:, stp % NBC, :],
                                                op=ALU.mult).then_inc(sem["sU"], 1)
                        g.wait_ge(sem["sYG"], 2 * (gp + 1))
                        g.dma_start(out=ygin_v[:, 2 * pi:2 * pi + 2, h * TH:(h + 1) * TH],
                                    in_=yg_s[:, gp % 2, :, :]).then_inc(sYSs[gp % 2], 16)
                g.wait_ge(sYSs[0], 16 * ((2 * NPI + 1) // 2))
                g.wait_ge(sYSs[1], 16 * (2 * NPI // 2))
                g.collective_compute(
                    "AllGather", ALU.bypass,
                    replica_groups=[[0, 1], [2, 3], [4, 5], [6, 7]],
                    ins=[ygin.ap()], outs=[ygout.ap()],
                ).then_inc(sem["sAT"])
                g.wait_ge(sem["sAT"], 1)

        @block.sync
        def _(s):
            for q in range(NBC):
                nq = len([x for x in range(NST) if x % NBC == q])
                s.wait_ge(sBC[q], 32 * nq)
            s.sem_inc(sem["sPH"], 1)
        @block.gpsimd
        def _(g):
            g.sem_inc(sem["sPH"], 1)
        @block.vector
        def _(v):
            v.wait_ge(sem["sYG"], 2 * 2 * NPI)
            v.sem_inc(sem["sPH"], 1)
        @block.scalar
        def _(a):
            a.wait_ge(sem["sDA"], NST)
            a.sem_inc(sem["sPH"], 1)
        @block.tensor
        def _(t):
            t.wait_ge(sem["sYA"], NST)
            t.sem_inc(sem["sPH"], 1)

        with ExitStack() as pc:
            EC = pc.enter_context
            x_res = EC(nc.sbuf_tensor("x_res", [P, ND, TC], BF16))
            b1c = EC(nc.sbuf_tensor("b1c", [P, NF], F32))
            b2_bf = EC(nc.sbuf_tensor("b2_bf", [1, D], BF16))
            ps_s2x = EC(nc.psum_tensor("ps_s2x", [1, NWC], F32))
            ps_s2q = EC(nc.psum_tensor("ps_s2q", [1, NWC], F32))

            with ExitStack() as pc1:
                EC1 = pc1.enter_context
                wout_bf = EC1(nc.sbuf_tensor("wout_bf", [P, NK2, D], BF16))
                ygs = EC1(nc.sbuf_tensor("ygs", [P, NK2, TC], BF16))
                xh_bf = EC1(nc.sbuf_tensor("xh_bf", [P, ND, TC], BF16))
                ygts = EC1(nc.sbuf_tensor("ygts", [P, TC], BF16))
                ps_wo = [EC1(nc.psum_tensor(f"ps_wo{k}", [P, WOC], F32)) for k in range(2)]

                @block.gpsimd
                def _(g):
                    g.wait_ge(sem["sPH"], PH_C)
                    g.dma_start(out=wout_bf[:, :, :], in_=w_out_v).then_inc(sem["sCL"], 16)
                    g.dma_start(out=xh_bf[:, :, :], in_=xTh_v).then_inc(sem["sCL"], 16)
                    g.dma_start(out=b2_bf[:, :], in_=b2.ap()).then_inc(sem["sCL"], 16)
                    g.dma_start(out=b1c[:, :], in_=b1.ap()).then_inc(sem["sCL"], 16)

                @block.sync
                def _(s):
                    s.wait_ge(sem["sPH"], PH_C)
                    for kc in range(NK2):
                        if kc >= 1:
                            s.wait_ge(sem["sYGC"], kc)
                        s.dma_start(out=ygs[:, kc, :], in_=ygout_v[:, kc, 0:TC]).then_inc(sYGL[kc % 2], 16)
                        s.dma_start(out=ygts[:, :], in_=ygout_v[:, kc, TC:2 * TC]).then_inc(sYGL[kc % 2], 16)

                @block.vector
                def _(v):
                    v.wait_ge(sem["sPH"], PH_C)
                    v.wait_ge(sem["sWL"], 16 * 9)
                    for kc in range(NK2):
                        v.wait_ge(sYGL[kc % 2], 32 * (kc // 2 + 1))
                        v.tensor_scalar(out=ygs[:, kc, :], in0=ygs[:, kc, :], scalar1=selc[:, 0:1],
                                        scalar2=None, op0=ALU.mult)
                        v.scalar_tensor_tensor(out=ygs[:, kc, :], in0=ygts[:, :],
                                               scalar=selc[:, 1:2], in1=ygs[:, kc, :],
                                               op0=ALU.mult, op1=ALU.add).then_inc(sem["sYGC"], 1)
                    v.wait_ge(sem["sCL"], 16 * 4)
                    for mt in range(ND):
                        for nch in range(NWOC):
                            wo = mt * NWOC + nch
                            v.wait_ge(sem["sWO"], wo + 1)
                            v.tensor_tensor(out=x_res[:, mt, nch * WOC:(nch + 1) * WOC],
                                            in0=ps_wo[wo % 2][:, :],
                                            in1=xh_bf[:, mt, nch * WOC:(nch + 1) * WOC],
                                            op=ALU.add).then_inc(sem["sXR"], 1)

                @block.tensor
                def _(t):
                    t.wait_ge(sem["sPH"], PH_C)
                    t.wait_ge(sem["sCL"], 16 * 4)
                    t.wait_ge(sem["sYGC"], NK2)
                    for mt in range(ND):
                        for nch in range(NWOC):
                            wo = mt * NWOC + nch
                            if wo >= 2:
                                t.wait_ge(sem["sXR"], wo - 1)
                            for kc in range(NK2):
                                t.matmul(ps_wo[wo % 2][:, :], wout_bf[:, kc, mt * P:(mt + 1) * P],
                                         ygs[:, kc, nch * WOC:(nch + 1) * WOC],
                                         start=(kc == 0), stop=(kc == NK2 - 1)).then_inc(
                                             sem["sWO"], 1 if kc == NK2 - 1 else 0)

            with ExitStack() as pc2:
                EC2 = pc2.enter_context
                xn2 = EC(nc.sbuf_tensor("xn2", [P, ND, TC], BF16))
                sq2 = EC2(nc.sbuf_tensor("sq2", [P, ND, TC], BF16))
                m2_bc = EC2(nc.sbuf_tensor("m2_bc", [P, TC], F32))
                r2_bc = EC2(nc.sbuf_tensor("r2_bc", [P, TC], F32))
                m2r = EC2(nc.sbuf_tensor("m2r", [1, 2, NWC], F32))
                e22r = EC2(nc.sbuf_tensor("e22r", [1, 2, NWC], F32))
                v2r = EC2(nc.sbuf_tensor("v2r", [1, NWC], F32))
                r2r = EC2(nc.sbuf_tensor("r2r", [1, 2, NWC], F32))
                tsub2 = EC2(nc.sbuf_tensor("tsub2", [P, TC], F32))

                @block.scalar
                def _(a):
                    a.wait_ge(sem["sXR"], ND * NWOC)
                    a.activation(out=sq2[:, :, :], in_=x_res[:, :, :], func=AF.Square).then_inc(sem["sSQ2"], 1)
                    for nch in range(NCCH):
                        pp = nch % 2
                        a.wait_ge(sem["sST2"], nch + 1)
                        if nch >= 2:
                            a.wait_ge(sem["sRW2"], 32 * (nch - 1))
                        a.activation(out=m2r[:, pp, :], in_=ps_s2x[:, :], func=AF.Identity, scale=1.0 / D)
                        a.activation(out=e22r[:, pp, :], in_=ps_s2q[:, :], func=AF.Identity, scale=1.0 / D).then_inc(sem["sEV2"], 1)
                        a.wait_ge(sem["sVAR2"], nch + 1)
                        a.activation(out=r2r[:, pp, :], in_=v2r[:, :], func=AF.Ln, bias=eps_c[0:1, :])
                        a.activation(out=r2r[:, pp, :], in_=r2r[:, pp, :], func=AF.Exp, scale=-0.5).then_inc(sem["sRS2"], 1)

                @block.vector
                def _(v):
                    for nch in range(NCCH):
                        pp = nch % 2
                        v.wait_ge(sem["sEV2"], nch + 1)
                        v.tensor_tensor(out=v2r[:, :], in0=m2r[:, pp, :], in1=m2r[:, pp, :], op=ALU.mult)
                        v.tensor_tensor(out=v2r[:, :], in0=e22r[:, pp, :], in1=v2r[:, :], op=ALU.subtract).then_inc(sem["sVAR2"], 1)
                    v.wait_ge(sem["sRB2"], 32)
                    for i in range(ND):
                        v.tensor_tensor(out=tsub2[:, :], in0=x_res[:, i, :], in1=m2_bc[:, :], op=ALU.subtract)
                        v.tensor_tensor(out=xn2[:, i, :], in0=tsub2[:, :], in1=r2_bc[:, :], op=ALU.mult).then_inc(sem["sXN"], 1)

                @block.tensor
                def _(t):
                    t.wait_ge(sem["sSQ2"], 1)
                    for nch in range(NCCH):
                        if nch >= 1:
                            t.wait_ge(sem["sEV2"], nch)
                        for kt in range(ND):
                            t.matmul(ps_s2x[:, :], ones_bf[:, :], x_res[:, kt, nch * NWC:(nch + 1) * NWC],
                                     start=(kt == 0), stop=(kt == ND - 1))
                        for kt in range(ND):
                            t.matmul(ps_s2q[:, :], ones_bf[:, :], sq2[:, kt, nch * NWC:(nch + 1) * NWC],
                                     start=(kt == 0), stop=(kt == ND - 1)).then_inc(
                                         sem["sST2"], 1 if kt == ND - 1 else 0)

                @block.sync
                def _(s):
                    for nch in range(NCCH):
                        pp = nch % 2
                        s.wait_ge(sem["sRS2"], nch + 1)
                        s.dma_start(out=row2_d.ap()[0:1, nch * NWC:(nch + 1) * NWC],
                                    in_=m2r[:, pp, :]).then_inc(sem["sRW2"], 16)
                        s.dma_start(out=row2_d.ap()[1:2, nch * NWC:(nch + 1) * NWC],
                                    in_=r2r[:, pp, :]).then_inc(sem["sRW2"], 16)
                    s.wait_ge(sem["sRW2"], 32 * NCCH)
                    s.dma_start(out=m2_bc[:, :], in_=row2_d.ap()[0:1, :].broadcast_to([P, TC])).then_inc(sem["sRB2"], 16)
                    s.dma_start(out=r2_bc[:, :], in_=row2_d.ap()[1:2, :].broadcast_to([P, TC])).then_inc(sem["sRB2"], 16)

            wq1 = EC(nc.sbuf_tensor("wq1", [P, ND, DFF], BF16))
            with ExitStack() as pc3:
                EC3 = pc3.enter_context
                wtile = EC3(nc.sbuf_tensor("wtile", [P, DFF], F32))
                mp_s = EC3(nc.sbuf_tensor("mp_s", [P, DFF], F32))
                wqt = EC3(nc.sbuf_tensor("wqt", [P, DFF], BF16))

                @block.sync
                def _(s):
                    s.wait_ge(sem["sXN"], ND)
                    s.wait_ge(sem["sST2"], NCCH)
                    for kt in range(ND):
                        if kt >= 1:
                            s.wait_ge(sem["sQ1"], kt)
                        s.dma_start(out=wtile[:, :], in_=w1t_v[:, kt, :]).then_inc(sem["sQL"], 16)
                    for kt in range(NF):
                        if kt == 0:
                            s.wait_ge(sem["sQ1"], ND)
                        else:
                            s.wait_ge(sem["sQ2"], kt)
                        s.dma_start(out=wtile[:, 0:D], in_=w2t_v[:, kt, :]).then_inc(sem["sQL"], 16)

                @block.vector
                def _(v):
                    v.wait_ge(sem["sTH"], 18)
                    for kt in range(ND):
                        v.wait_ge(sem["sQL"], 16 * (kt + 1))
                        v.tensor_scalar(out=mp_s[:, :], in0=wtile[:, :], scalar1=thr_c[:, 0:1],
                                        scalar2=None, op0=ALU.is_ge)
                        v.scalar_tensor_tensor(out=mp_s[:, :], in0=wtile[:, :], scalar=nthr_c[:, 0:1],
                                               in1=mp_s[:, :], op0=ALU.is_gt, op1=ALU.add)
                        v.tensor_scalar(out=wq1[:, kt, :], in0=mp_s[:, :], scalar1=-1.0,
                                        scalar2=None, op0=ALU.add).then_inc(sem["sQ1"], 1)
                    v.wait_ge(sem["sCHB"], 32)
                    for kt in range(NF):
                        v.wait_ge(sem["sQL"], 16 * (ND + kt + 1))
                        if kt >= 1:
                            v.wait_ge(sem["sQ2S"], 16 * kt)
                        v.scalar_tensor_tensor(out=mp_s[:, 0:D], in0=wtile[:, 0:D],
                                               scalar=chl_c[:, 0:1], op0=ALU.subtract,
                                               in1=chl_c[:, 1:2].broadcast_to([P, D]), op1=ALU.is_gt)
                        v.scalar_tensor_tensor(out=mp_s[:, D:2 * D], in0=wtile[:, 0:D],
                                               scalar=chl_c[:, 0:1], op0=ALU.add,
                                               in1=chl_c[:, 2:3].broadcast_to([P, D]), op1=ALU.is_gt)
                        v.scalar_tensor_tensor(out=wqt[:, 0:D], in0=mp_s[:, 0:D],
                                               scalar=-1.0, op0=ALU.add,
                                               in1=mp_s[:, D:2 * D], op1=ALU.add).then_inc(sem["sQ2"], 1)

                @block.gpsimd
                def _(g):
                    for kt in range(NF):
                        g.wait_ge(sem["sQ2"], kt + 1)
                        g.dma_start(out=wq2_v[:, kt, :], in_=wqt[:, 0:D]).then_inc(sem["sQ2S"], 16)
                    if dbg:
                        g.wait_ge(sem["sXR"], ND * NWOC)
                        g.dma_start(out=dbg_xres.ap().rearrange("(i p) t -> p i t", p=P),
                                    in_=x_res[:, :, :]).then_inc(sem["sOSD"], 16)
                        g.dma_start(out=dbg_xc.ap(), in_=xc_dram.ap()).then_inc(sem["sOSD"], 16)
                        g.dma_start(out=dbg_z.ap(), in_=z_dram.ap()).then_inc(sem["sOSD"], 16)
                        g.dma_start(out=dbg_xdbl.ap(), in_=cc1_out.ap()).then_inc(sem["sOSD"], 16)
                        g.dma_start(out=dbg_yg.ap(), in_=ygout.ap()).then_inc(sem["sOSD"], 16)
                        g.dma_start(out=dbg_thr.ap(), in_=gsc[:, 2:4]).then_inc(sem["sOSD"], 16)

            with ExitStack() as pc4:
                EC4 = pc4.enter_context
                hbuf = EC4(nc.sbuf_tensor("hbuf", [P, NF, CC], BF16))
                outst = EC4(nc.sbuf_tensor("outst", [P, 2, D], F32))
                wq2s = EC4(nc.sbuf_tensor("wq2s", [P, 2, D], BF16))
                ps_f1 = [EC4(nc.psum_tensor(f"ps_f1{k}", [P, CC], F32)) for k in range(2)]
                ps_f2 = [EC4(nc.psum_tensor(f"ps_f2{k}", [P, NW2], F32)) for k in range(2 * NF2N)]

                @block.sync
                def _(s):
                    s.wait_ge(sem["sQ2S"], 16 * NF)
                    for cc in range(NCC):
                        for tp in range(NTP):
                            for kt in range(NF):
                                gidx = (cc * NTP + tp) * NF + kt
                                if gidx >= 2:
                                    s.wait_ge(sem["sF2K"], gidx - 1)
                                s.dma_start(out=wq2s[:, gidx % 2, :], in_=wq2_v[:, kt, :]).then_inc(sW2L[gidx % 2], 16)
                            for tw in range(2):
                                ts_i = cc * NTS + tp * 2 + tw
                                s.wait_ge(sem["sOE"], NF2N * (ts_i + 1))
                                t0 = ts_i * P
                                s.dma_start(out=out.ap()[t0:t0 + P, :], in_=outst[:, ts_i % 2, :]).then_inc(sOS[ts_i % 2], 16)
                    nts_all = NCC * NTS
                    s.wait_ge(sOS[0], 16 * ((nts_all + 1) // 2))
                    s.wait_ge(sOS[1], 16 * (nts_all // 2))
                    if dbg:
                        s.wait_ge(sem["sOSD"], 16 * 10)

                @block.tensor
                def _(t):
                    t.wait_ge(sem["sXN"], ND)
                    t.wait_ge(sem["sQ1"], ND)
                    t.wait_ge(sem["sQ2"], NF)
                    for cc in range(NCC):
                        for ft in range(NF):
                            fg = cc * NF + ft
                            if fg >= 2:
                                t.wait_ge(sem["sHE"], fg - 1)
                            for kt in range(ND):
                                t.matmul(ps_f1[fg % 2][:, :], wq1[:, kt, ft * P:(ft + 1) * P],
                                         xn2[:, kt, cc * CC:(cc + 1) * CC],
                                         start=(kt == 0), stop=(kt == ND - 1)).then_inc(
                                             sem["sF1"], 1 if kt == ND - 1 else 0)
                        t.wait_ge(sem["sHE"], (cc + 1) * NF)
                        for tp in range(NTP):
                            if not (cc == 0 and tp == 0):
                                t.wait_ge(sem["sOE"], NF2N * (cc * NTS + 2 * tp))
                            for kt in range(NF):
                                gidx = (cc * NTP + tp) * NF + kt
                                t.wait_ge(sW2L[gidx % 2], 16 * (gidx // 2 + 1))
                                for tw in range(2):
                                    tcol = (tp * 2 + tw) * P
                                    for nw in range(NF2N):
                                        fk_i = t.matmul(ps_f2[tw * NF2N + nw][:, :],
                                                        hbuf[:, kt, tcol:tcol + P],
                                                        wq2s[:, gidx % 2, nw * NW2:(nw + 1) * NW2],
                                                        start=(kt == 0), stop=False)
                                fk_i.then_inc(sem["sF2K"], 1)
                            for tw in range(2):
                                ts_i = cc * NTS + tp * 2 + tw
                                tcol = (tp * 2 + tw) * P
                                for nw in range(NF2N):
                                    t.matmul(ps_f2[tw * NF2N + nw][:, :], onesr_bf[:, :],
                                             b2_bf[:, nw * NW2:(nw + 1) * NW2],
                                             start=False, stop=False)
                                    for q in range(NQS):
                                        dt_i = nw * NQS + q
                                        t.matmul(ps_f2[tw * NF2N + nw][:, :],
                                                 x_res[:, dt_i, cc * CC + tcol:cc * CC + tcol + P],
                                                 idshift[:, q, :],
                                                 start=False, stop=(q == NQS - 1)).then_inc(
                                                     sem["sF2"], 1 if q == NQS - 1 else 0)

                @block.scalar
                def _(a):
                    a.wait_ge(sem["sQ2S"], 16 * NF)
                    for cc in range(NCC):
                        if cc >= 1:
                            a.wait_ge(sem["sF2K"], cc * NTP * NF)
                        for ft in range(NF):
                            fg = cc * NF + ft
                            a.wait_ge(sem["sF1"], fg + 1)
                            a.activation(out=hbuf[:, ft, :], in_=ps_f1[fg % 2][:, :],
                                         func=AF.Relu, bias=b1c[:, ft:ft + 1]).then_inc(sem["sHE"], 1)
                        for tp in range(NTP):
                            for tw in range(2):
                                ts_i = cc * NTS + tp * 2 + tw
                                if ts_i >= 2:
                                    a.wait_ge(sOS[ts_i % 2], 16 * ((ts_i - 2) // 2 + 1))
                                for nw in range(NF2N):
                                    a.wait_ge(sem["sF2"], ts_i * NF2N + nw + 1)
                                    a.activation(out=outst[:, ts_i % 2, nw * NW2:(nw + 1) * NW2],
                                                 in_=ps_f2[tw * NF2N + nw][:, :],
                                                 func=AF.Identity).then_inc(sem["sOE"], 1)

    return nc


def _stage_inputs(ins, cfg):
    c = cfg
    x = np.ascontiguousarray(np.asarray(ins["L0_attn_x"], dtype=np.float32))
    W_in = np.asarray(ins["L0_attn_W_in"], dtype=np.float32)
    conv_w = np.asarray(ins["L0_attn_conv_w"], dtype=np.float32)
    conv_b = np.asarray(ins["L0_attn_conv_b"], dtype=np.float32)
    W_xp = np.asarray(ins["L0_attn_W_xproj"], dtype=np.float32)
    W_dt = np.asarray(ins["L0_attn_W_dt"], dtype=np.float32)
    b_dt = np.asarray(ins["L0_attn_b_dt"], dtype=np.float32)
    A_log = np.asarray(ins["L0_attn_A_log"], dtype=np.float32)
    Dp = np.asarray(ins["L0_attn_Dp"], dtype=np.float32)
    W_out = np.asarray(ins["L0_attn_W_out"], dtype=np.float32)
    W1 = np.asarray(ins["L0_ffn_W1"], dtype=np.float32)
    b1 = np.asarray(ins["L0_ffn_b1"], dtype=np.float32)
    W2 = np.asarray(ins["L0_ffn_W2"], dtype=np.float32)
    b2 = np.asarray(ins["L0_ffn_b2"], dtype=np.float32)

    DI2 = 2 * c.DI
    w2g = np.ascontiguousarray(
        W2.reshape(32, 32, 4, 32, 32).transpose(3, 2, 0, 1, 4).reshape(32 * 128, 1024))
    maps = []
    for core in range(8):
        b, k = core // 2, core % 2
        lo, hi = k * c.DI, (k + 1) * c.DI
        xT = np.ascontiguousarray(x[b].T)
        m = dict(
            xT=xT,
            xTh=np.ascontiguousarray(xT[:, k * c.TC:(k + 1) * c.TC]),
            w_in=np.ascontiguousarray(np.concatenate(
                [W_in[:, lo:hi], W_in[:, DI2 + lo:DI2 + hi]], axis=1)),
            conv_w=np.ascontiguousarray(conv_w[lo:hi, 0, :]),
            conv_b=np.ascontiguousarray(conv_b[lo:hi].reshape(c.NI, P).T),
            w_xp=np.ascontiguousarray(W_xp[lo:hi]),
            w_dt=np.ascontiguousarray(W_dt[0, lo:hi].reshape(c.NI, P).T),
            b_dt=np.ascontiguousarray(b_dt[lo:hi].reshape(c.NI, P).T),
            a_log=np.ascontiguousarray(A_log[lo:hi].reshape(c.NI, P, c.S).transpose(1, 0, 2).reshape(P, c.NI * c.S)),
            dp=np.ascontiguousarray(Dp[lo:hi].reshape(c.NI, P).T),
            w_out=np.ascontiguousarray(W_out),
            w1t=np.ascontiguousarray(W1.T),
            b1=np.ascontiguousarray(b1.reshape(c.NF, P).T),
            w2t=np.ascontiguousarray(W2.T),
            w2g=w2g,
            b2=np.ascontiguousarray(b2[None, :]),
            sel=np.ascontiguousarray(np.broadcast_to(
                np.array([1.0 - k, float(k)], dtype=np.float32), (128, 2))),
        )
        maps.append(m)
    return maps


def kernel(**inputs):
    cfg = Cfg()
    nc = build_program(cfg, dbg=False)
    in_maps = _stage_inputs(inputs, cfg)
    res = run_bass_kernel_spmd(nc, in_maps, core_ids=list(range(8)))
    B = inputs["L0_attn_x"].shape[0]
    outp = np.empty((B, cfg.T, cfg.D), dtype=np.float32)
    for core in range(8):
        b, k = core // 2, core % 2
        outp[b, k * cfg.TC:(k + 1) * cfg.TC, :] = res.results[core]["out"]
    return outp


# version 10
# speedup vs baseline: 1.0858x; 1.0858x over previous
import numpy as np
from contextlib import ExitStack

import concourse.bass as bass
import concourse.mybir as mybir
from concourse.bass_utils import run_bass_kernel_spmd

F32 = mybir.dt.float32
BF16 = mybir.dt.bfloat16
AF = mybir.ActivationFunctionType
ALU = mybir.AluOpType
P = 128


class Cfg:
    def __init__(self, T=4096, D=1024, DI=1024, S=16, DFF=4096, NCH=512, CC=512):
        self.T = T
        self.D = D
        self.DI = DI
        self.S = S
        self.DFF = DFF
        self.NCH = NCH
        self.CC = CC
        self.NJ = T // NCH
        self.ND = D // P
        self.NI = DI // P
        self.NWF = 2 * self.NI
        self.NF = DFF // P
        self.TH = T // 2
        self.TC = T // 2
        self.NPI = self.NI // 2
        self.NCC = self.TC // CC
        self.NTS = CC // P
        self.XD = 1 + 2 * S
        self.NW2 = min(512, D)
        self.NQS = self.NW2 // P
        self.NF2N = D // self.NW2


def build_program(cfg, dbg=False):
    c = cfg
    T, D, DI, S, DFF = c.T, c.D, c.DI, c.S, c.DFF
    NCH, NJ, ND, NI, NWF, NF = c.NCH, c.NJ, c.ND, c.NI, c.NWF, c.NF
    TH, TC, NPI, CC, NCC, NTS, XD = c.TH, c.TC, c.NPI, c.CC, c.NCC, c.NTS, c.XD
    NW2, NQS, NF2N = c.NW2, c.NQS, c.NF2N
    NK2 = 2 * NI
    NBC = 3
    NW = min(512, TH)
    NTHC = TH // NW
    NWC = min(512, TC)
    NCCH = TC // NWC
    WOC = min(512, TC)
    NWOC = TC // WOC
    NWIN_CS = 2 * DI // NCH
    NST = 2 * NPI * S
    NTP = NTS // 2
    PH_A, PH_C = 5, 10

    nc = bass.Bass("TRN2", target_bir_lowering=False, debug=False,
                   detect_race_conditions=False)

    xT = nc.dram_tensor("xT", [D, T], F32, kind="ExternalInput")
    xTh = nc.dram_tensor("xTh", [D, TC], F32, kind="ExternalInput")
    w_in = nc.dram_tensor("w_in", [D, 2 * DI], F32, kind="ExternalInput")
    conv_w = nc.dram_tensor("conv_w", [DI, 4], F32, kind="ExternalInput")
    conv_b = nc.dram_tensor("conv_b", [P, DI // P], F32, kind="ExternalInput")
    w_xp = nc.dram_tensor("w_xp", [DI, XD], F32, kind="ExternalInput")
    w_dt = nc.dram_tensor("w_dt", [P, DI // P], F32, kind="ExternalInput")
    b_dt = nc.dram_tensor("b_dt", [P, DI // P], F32, kind="ExternalInput")
    a_log = nc.dram_tensor("a_log", [P, (DI // P) * S], F32, kind="ExternalInput")
    dp = nc.dram_tensor("dp", [P, DI // P], F32, kind="ExternalInput")
    w_out = nc.dram_tensor("w_out", [2 * DI, D], F32, kind="ExternalInput")
    w1t = nc.dram_tensor("w1t", [D, DFF], F32, kind="ExternalInput")
    b1 = nc.dram_tensor("b1", [P, DFF // P], F32, kind="ExternalInput")
    w2t = nc.dram_tensor("w2t", [DFF, D], F32, kind="ExternalInput")
    b2 = nc.dram_tensor("b2", [1, D], F32, kind="ExternalInput")
    sel = nc.dram_tensor("sel", [P, 2], F32, kind="ExternalInput")
    w2g = nc.dram_tensor("w2g", [32 * P, 1024], F32, kind="ExternalInput")
    out = nc.dram_tensor("out", [TC, D], F32, kind="ExternalOutput")

    xc_dram = nc.dram_tensor("xc_dram", [DI, T], BF16, kind="Internal")
    z_dram = nc.dram_tensor("z_dram", [DI, T], BF16, kind="Internal")
    rowscr = nc.dram_tensor("rowscr", [1, T], F32, kind="Internal")
    cc1_in = nc.dram_tensor("cc1_in", [XD, T], F32, kind="Internal")
    cc1_out = nc.dram_tensor("cc1_out", [XD, T], F32, kind="Internal")
    xdbl_bf = nc.dram_tensor("xdbl_bf", [XD, T], BF16, kind="Internal")
    ygin = nc.dram_tensor("ygin", [DI, T], BF16, kind="Internal")
    ygout = nc.dram_tensor("ygout", [2 * DI, T], BF16, kind="Internal")
    wq2_dram = nc.dram_tensor("wq2_dram", [DFF, D], BF16, kind="Internal")
    g1d = nc.dram_tensor("g1d", [P, ND], F32, kind="Internal")
    g2D2 = nc.dram_tensor("g2D2", [P, 32], F32, kind="Internal")
    g2d3 = nc.dram_tensor("g2d3", [1, 4], F32, kind="Internal")
    chl_d = nc.dram_tensor("chl_d", [1, 3], F32, kind="Internal")
    thr_d = nc.dram_tensor("thr_d", [1, 2], F32, kind="Internal")
    row2_d = nc.dram_tensor("row2_d", [2, TC], F32, kind="Internal")

    if dbg:
        dbg_xc = nc.dram_tensor("dbg_xc", [DI, T], BF16, kind="ExternalOutput")
        dbg_z = nc.dram_tensor("dbg_z", [DI, T], BF16, kind="ExternalOutput")
        dbg_xdbl = nc.dram_tensor("dbg_xdbl", [XD, T], F32, kind="ExternalOutput")
        dbg_yg = nc.dram_tensor("dbg_yg", [2 * DI, T], BF16, kind="ExternalOutput")
        dbg_xres = nc.dram_tensor("dbg_xres", [D, TC], F32, kind="ExternalOutput")
        dbg_thr = nc.dram_tensor("dbg_thr", [1, 2], F32, kind="ExternalOutput")
        dbg_dl = nc.dram_tensor("dbg_dl", [P, 2 * TH], F32, kind="ExternalOutput")
        dbg_cr = nc.dram_tensor("dbg_cr", [P, NI * S * 2], F32, kind="ExternalOutput")
        dbg_xcp = nc.dram_tensor("dbg_xcp", [P, 2 * TH], F32, kind="ExternalOutput")
        dbg_zp = nc.dram_tensor("dbg_zp", [P, 2 * TH], F32, kind="ExternalOutput")

    w2g_v = w2g.ap().rearrange("(g p) t -> p g t", p=P)
    xT_v = xT.ap().rearrange("(i p) t -> p i t", p=P)
    xTh_v = xTh.ap().rearrange("(i p) t -> p i t", p=P)
    w_in_v = w_in.ap().rearrange("(i p) f -> p i f", p=P)
    w_xp_v = w_xp.ap().rearrange("(i p) f -> p i f", p=P)
    a_log_v = a_log.ap().rearrange("p (i s) -> p i s", i=DI // P)
    conv_w_v = conv_w.ap().rearrange("(i p) k -> p i k", p=P)
    xc_v = xc_dram.ap().rearrange("(i p) t -> p i t", p=P)
    z_v = z_dram.ap().rearrange("(i p) t -> p i t", p=P)
    w_out_v = w_out.ap().rearrange("(i p) f -> p i f", p=P)
    w1t_v = w1t.ap().rearrange("(i p) f -> p i f", p=P)
    w2t_v = w2t.ap().rearrange("(i p) f -> p i f", p=P)
    ygin_v = ygin.ap().rearrange("(i p) t -> p i t", p=P)
    ygout_v = ygout.ap().rearrange("(i p) t -> p i t", p=P)
    wq2_v = wq2_dram.ap().rearrange("(i p) f -> p i f", p=P)

    with ExitStack() as top:
        E = top.enter_context
        sem_names = [
            "sWL", "sIO", "sPRE", "sCSm", "sCSe", "sNA",
            "sXL0", "sXL1", "sSQ", "sST", "sEV1", "sVAR", "sRST", "sRW",
            "sRB0", "sRB1",
            "sWIN", "sEVC", "sZS0", "sZS1", "sCV", "sXCE", "sXCS0", "sXCS1",
            "sXD", "sXDE",
            "sGW0", "sGW1", "sGA", "sGR", "sTH", "sCD0", "sCD1", "sCC1",
            "sXBF", "sPH",
            "sBC0", "sBC1", "sBC2", "sSC", "sABC", "sZL0", "sZL1",
            "sA1", "sDL", "sDA", "sPR", "sYA", "sYG",
            "sSZ", "sYS0", "sYS1", "sAT", "sU",
            "sCL", "sWO", "sXR", "sSQ2", "sST2", "sEV2", "sVAR2", "sRS2",
            "sRW2", "sRB2", "sXN", "sQL", "sQ1", "sQ2", "sQ2S",
            "sF1", "sHE", "sW2L0", "sW2L1", "sF2K", "sF2", "sOE",
            "sOS0", "sOS1", "sOSD", "sYGL0", "sYGL1", "sYGC",
            "sG2L", "sG2A", "sG2S", "sG2E", "sG2D", "sG2X", "sG2C", "sCHB",
            "sYGT", "sQLo", "sW2L2",
        ]
        sem = {n: E(nc.semaphore(n)) for n in sem_names}
        sXL = [sem["sXL0"], sem["sXL1"]]
        sRB = [sem["sRB0"], sem["sRB1"]]
        sZSs = [sem["sZS0"], sem["sZS1"]]
        sXCSs = [sem["sXCS0"], sem["sXCS1"]]
        sGW = [sem["sGW0"], sem["sGW1"]]
        sCD = [sem["sCD0"], sem["sCD1"]]
        sBC = [sem["sBC0"], sem["sBC1"], sem["sBC2"]]
        sZL = [sem["sZL0"], sem["sZL1"]]
        sYSs = [sem["sYS0"], sem["sYS1"]]
        sW2L = [sem["sW2L0"], sem["sW2L1"], sem["sW2L2"]]
        sOS = [sem["sOS0"], sem["sOS1"]]
        sYGL = [sem["sYGL0"], sem["sYGL1"]]

        ones_bf = E(nc.sbuf_tensor("ones_bf", [P, 1], BF16))
        onesr_bf = E(nc.sbuf_tensor("onesr_bf", [1, P], BF16))
        ident_bf = E(nc.sbuf_tensor("ident_bf", [P, P], BF16))
        idshift = E(nc.sbuf_tensor("idshift", [P, NQS, NW2], BF16))
        iota_r = E(nc.sbuf_tensor("iota_r", [P, 1], F32))
        scr1 = E(nc.sbuf_tensor("scr1", [P, 1], F32))
        negA = E(nc.sbuf_tensor("negA", [P, NI, S], F32))
        carry = E(nc.sbuf_tensor("carry", [P, NI, S, 2], F32))
        wdt_c = E(nc.sbuf_tensor("wdt_c", [P, NI], F32))
        bdt_c = E(nc.sbuf_tensor("bdt_c", [P, NI], F32))
        dp_c = E(nc.sbuf_tensor("dp_c", [P, NI], F32))
        convb_c = E(nc.sbuf_tensor("convb_c", [P, NI], F32))
        thr_c = E(nc.sbuf_tensor("thr_c", [P, 2], F32))
        nthr_c = E(nc.sbuf_tensor("nthr_c", [P, 2], F32))
        chl_c = E(nc.sbuf_tensor("chl_c", [P, 3], F32))
        gsc = E(nc.sbuf_tensor("gsc", [1, 4], F32))
        eps_c = E(nc.sbuf_tensor("eps_c", [P, 1], F32))
        selc = E(nc.sbuf_tensor("selc", [P, 2], F32))

        block = E(nc.Block())

        with ExitStack() as pa:
            EA = pa.enter_context
            iota_c = EA(nc.sbuf_tensor("iota_c", [P, max(NW2, P)], F32))
            w_in_bf = EA(nc.sbuf_tensor("w_in_bf", [P, ND, 2 * DI], BF16))
            wxp_bf = EA(nc.sbuf_tensor("wxp_bf", [P, NI, XD], BF16))
            diag_bf = EA(nc.sbuf_tensor("diag_bf", [P, NI * 4, P], BF16))
            cwcol = EA(nc.sbuf_tensor("cwcol", [P, NI, 4], F32))
            xb = EA(nc.sbuf_tensor("xb", [P, 2, ND, NCH], BF16))
            sq = EA(nc.sbuf_tensor("sq", [P, ND, NCH], BF16))
            xc_pre = EA(nc.sbuf_tensor("xc_pre", [P, NI, 3 + T], BF16))
            z_stage = EA(nc.sbuf_tensor("z_stage", [P, 2, NI, NCH], BF16))
            xc_stage = EA(nc.sbuf_tensor("xc_stage", [P, 2, NI, NCH], BF16))
            rstd_bc = EA(nc.sbuf_tensor("rstd_bc", [P, 2, NCH], F32))
            m_row = EA(nc.sbuf_tensor("m_row", [1, 2, NCH], F32))
            e2_row = EA(nc.sbuf_tensor("e2_row", [1, 2, NCH], F32))
            var_row = EA(nc.sbuf_tensor("var_row", [1, 2, NCH], F32))
            rstd_row = EA(nc.sbuf_tensor("rstd_row", [1, 2, NCH], F32))
            negm_bf = EA(nc.sbuf_tensor("negm_bf", [1, 2, NCH], BF16))
            cs_row = EA(nc.sbuf_tensor("cs_row", [1, 2 * DI], F32))
            cs_bf = EA(nc.sbuf_tensor("cs_bf", [1, 2 * DI], BF16))
            xdbl_st = EA(nc.sbuf_tensor("xdbl_st", [XD, 2, NCH], F32))
            ps_sx = EA(nc.psum_tensor("ps_sx", [1, NCH], F32))
            ps_sq = EA(nc.psum_tensor("ps_sq", [1, NCH], F32))
            ps_win = [EA(nc.psum_tensor(f"ps_win{k}", [P, NCH], F32)) for k in range(2)]
            ps_cv = [EA(nc.psum_tensor(f"ps_cv{k}", [P, NCH], F32)) for k in range(2)]
            ps_xd = EA(nc.psum_tensor("ps_xd", [XD, NCH], F32))

            @block.gpsimd
            def _(g):
                g.dma_start(out=w_in_bf[:, :, :], in_=w_in_v).then_inc(sem["sWL"], 16)
                g.dma_start(out=wxp_bf[:, :, :], in_=w_xp_v).then_inc(sem["sWL"], 16)
                g.dma_start(out=convb_c[:, 0:NI], in_=conv_b.ap()).then_inc(sem["sWL"], 16)
                g.dma_start(out=wdt_c[:, 0:NI], in_=w_dt.ap()).then_inc(sem["sWL"], 16)
                g.dma_start(out=bdt_c[:, 0:NI], in_=b_dt.ap()).then_inc(sem["sWL"], 16)
                g.dma_start(out=dp_c[:, 0:NI], in_=dp.ap()).then_inc(sem["sWL"], 16)
                g.dma_start(out=cwcol[:, :, :], in_=conv_w_v).then_inc(sem["sWL"], 16)
                g.dma_start(out=negA[:, :, :], in_=a_log_v).then_inc(sem["sWL"], 16)
                g.dma_start(out=selc[:, :], in_=sel.ap()).then_inc(sem["sWL"], 16)
                g.iota(iota_c[:, :], pattern=[[1, iota_c.shape[1]]], base=0,
                       channel_multiplier=0, allow_small_or_imprecise_dtypes=True)
                g.iota(iota_r[:, :], pattern=[[0, 1]], base=0,
                       channel_multiplier=1, allow_small_or_imprecise_dtypes=True).then_inc(sem["sIO"], 1)
                for j in range(NJ):
                    if j >= 2:
                        g.wait_ge(sem["sWIN"], NWF * (j - 1))
                        g.wait_ge(sem["sSQ"], j - 1)
                    g.dma_start(out=xb[:, j % 2, :, :],
                                in_=xT_v[:, :, j * NCH:(j + 1) * NCH]).then_inc(sXL[j % 2], 16)
                    if j >= 1:
                        g.wait_ge(sem["sXCE"], NI * j)
                        g.dma_start(out=xc_v[:, :, (j - 1) * NCH:j * NCH],
                                    in_=xc_stage[:, (j - 1) % 2, :, :]).then_inc(sXCSs[(j - 1) % 2], 16)
                g.wait_ge(sem["sXCE"], NI * NJ)
                g.dma_start(out=xc_v[:, :, (NJ - 1) * NCH:NJ * NCH],
                            in_=xc_stage[:, (NJ - 1) % 2, :, :]).then_inc(sXCSs[(NJ - 1) % 2], 16)
                g.wait_ge(sCD[0], 16 * ((NJ + 1) // 2))
                g.wait_ge(sCD[1], 16 * (NJ // 2))
                g.collective_compute(
                    "AllReduce", ALU.add,
                    replica_groups=[[0, 1], [2, 3], [4, 5], [6, 7]],
                    ins=[cc1_in.ap()], outs=[cc1_out.ap()],
                ).then_inc(sem["sCC1"])
                g.wait_ge(sem["sCC1"], 1)
                g.dma_start(out=xdbl_bf.ap(), in_=cc1_out.ap(), max_dma_last_dim=8192).then_inc(sem["sXBF"], 16)

            @block.vector
            def _(v):
                v.wait_ge(sem["sIO"], 1)
                v.memset(ones_bf[:, :], 1.0)
                v.memset(onesr_bf[:, :], 1.0)
                v.tensor_tensor(out=ident_bf[:, :], in0=iota_c[:, 0:P],
                                in1=iota_r[:, :].broadcast_to([P, P]), op=ALU.is_equal)
                for q in range(NQS):
                    v.tensor_scalar(out=scr1[:, :], in0=iota_r[:, :], scalar1=float(P * q),
                                    scalar2=None, op0=ALU.add)
                    v.tensor_tensor(out=idshift[:, q, :], in0=iota_c[:, 0:NW2],
                                    in1=scr1[:, :].broadcast_to([P, NW2]), op=ALU.is_equal)
                for i in range(NI):
                    v.memset(xc_pre[:, i, 0:3], 0.0)
                v.memset(eps_c[:, :], 1e-5).then_inc(sem["sPRE"], 1)
                v.wait_ge(sem["sWL"], 16 * 9)
                for i in range(NI):
                    for k in range(4):
                        v.tensor_tensor(out=diag_bf[:, i * 4 + k, :], in0=ident_bf[:, :],
                                        in1=cwcol[:, i, k:k + 1].broadcast_to([P, P]),
                                        op=ALU.mult).then_inc(sem["sPRE"], 1)
                for j in range(NJ):
                    jj = j % 2
                    v.wait_ge(sem["sEV1"], j + 1)
                    v.tensor_tensor(out=var_row[:, jj, :], in0=m_row[:, jj, :], in1=m_row[:, jj, :], op=ALU.mult)
                    v.tensor_tensor(out=var_row[:, jj, :], in0=e2_row[:, jj, :], in1=var_row[:, jj, :], op=ALU.subtract).then_inc(sem["sVAR"], 1)
                    for ft in range(NWF):
                        ftg = j * NWF + ft
                        v.wait_ge(sem["sWIN"], ftg + 1)
                        v.wait_ge(sRB[j % 2], 16 * (j // 2 + 1))
                        if ft == NI and j >= 2:
                            v.wait_ge(sZSs[j % 2], 16 * ((j - 2) // 2 + 1))
                        dest = xc_pre[:, ft, 3 + j * NCH:3 + (j + 1) * NCH] if ft < NI \
                            else z_stage[:, jj, ft - NI, :]
                        v.tensor_tensor(out=dest, in0=ps_win[ftg % 2][:, :],
                                        in1=rstd_bc[:, jj, :], op=ALU.mult).then_inc(sem["sEVC"], 1)

            @block.scalar
            def _(a):
                for nchk in range(NWIN_CS):
                    a.wait_ge(sem["sCSm"], nchk + 1)
                    a.activation(out=cs_row[:, nchk * NCH:(nchk + 1) * NCH],
                                 in_=ps_win[nchk % 2][0:1, 0:NCH], func=AF.Identity).then_inc(sem["sCSe"], 1)
                a.activation(out=cs_bf[:, :], in_=cs_row[:, :], func=AF.Identity).then_inc(sem["sCSe"], 1)

            @block.tensor
            def _(t):
                t.wait_ge(sem["sWL"], 16 * 9)
                t.wait_ge(sem["sPRE"], 1)
                for nchk in range(NWIN_CS):
                    if nchk >= 2:
                        t.wait_ge(sem["sCSe"], nchk - 1)
                    for kt in range(ND):
                        t.matmul(ps_win[nchk % 2][0:1, 0:NCH], ones_bf[:, :],
                                 w_in_bf[:, kt, nchk * NCH:(nchk + 1) * NCH],
                                 start=(kt == 0), stop=(kt == ND - 1)).then_inc(
                                     sem["sCSm"], 1 if kt == ND - 1 else 0)
                for j in range(NJ):
                    jj = j % 2
                    t.wait_ge(sem["sSQ"], j + 1)
                    if j >= 1:
                        t.wait_ge(sem["sEV1"], j)
                    for kt in range(ND):
                        t.matmul(ps_sx[:, :], ones_bf[:, :], xb[:, jj, kt, :],
                                 start=(kt == 0), stop=(kt == ND - 1))
                    for kt in range(ND):
                        t.matmul(ps_sq[:, :], ones_bf[:, :], sq[:, kt, :],
                                 start=(kt == 0), stop=(kt == ND - 1)).then_inc(
                                     sem["sST"], 1 if kt == ND - 1 else 0)
                    t.wait_ge(sem["sRST"], j + 1)
                    if j == 0:
                        t.wait_ge(sem["sCSe"], NWIN_CS + 1)
                    for ft in range(NWF):
                        ftg = j * NWF + ft
                        if ftg >= 2:
                            t.wait_ge(sem["sEVC"], ftg - 1)
                        for kt in range(ND):
                            t.matmul(ps_win[ftg % 2][:, :], w_in_bf[:, kt, ft * P:(ft + 1) * P],
                                     xb[:, jj, kt, :], start=(kt == 0), stop=False)
                        t.matmul(ps_win[ftg % 2][:, :], cs_bf[0:1, ft * P:(ft + 1) * P],
                                 negm_bf[0:1, jj, :], start=False, stop=True).then_inc(sem["sWIN"], 1)
                    if j == 0:
                        t.wait_ge(sem["sPRE"], 1 + 4 * NI)
                    for i in range(NI):
                        ci = j * NI + i
                        t.wait_ge(sem["sEVC"], j * NWF + i + 1)
                        if ci >= 2:
                            t.wait_ge(sem["sXCE"], ci - 1)
                        for k in range(4):
                            t.matmul(ps_cv[ci % 2][:, :], diag_bf[:, i * 4 + k, :],
                                     xc_pre[:, i, j * NCH + k:j * NCH + k + NCH],
                                     start=(k == 0), stop=(k == 3)).then_inc(
                                         sem["sCV"], 1 if k == 3 else 0)
                    t.wait_ge(sem["sXCE"], NI * (j + 1))
                    if j >= 2:
                        t.wait_ge(sem["sXDE"], j - 1)
                    for i in range(NI):
                        t.matmul(ps_xd[:, :], wxp_bf[:, i, :], xc_stage[:, jj, i, :],
                                 start=(i == 0), stop=(i == NI - 1)).then_inc(
                                     sem["sXD"], 1 if i == NI - 1 else 0)

            @block.scalar
            def _(a):
                for j in range(NJ):
                    jj = j % 2
                    a.wait_ge(sXL[j % 2], 16 * (j // 2 + 1))
                    if j >= 1:
                        a.wait_ge(sem["sST"], j)
                    a.activation(out=sq[:, :, :], in_=xb[:, jj, :, :], func=AF.Square).then_inc(sem["sSQ"], 1)
                    a.wait_ge(sem["sST"], j + 1)
                    if j >= 2:
                        a.wait_ge(sem["sWIN"], NWF * (j - 1))
                    a.activation(out=m_row[:, jj, :], in_=ps_sx[:, :], func=AF.Identity, scale=1.0 / D)
                    a.activation(out=e2_row[:, jj, :], in_=ps_sq[:, :], func=AF.Identity, scale=1.0 / D).then_inc(sem["sEV1"], 1)
                    a.wait_ge(sem["sVAR"], j + 1)
                    if j == 0:
                        a.wait_ge(sem["sPRE"], 1)
                    if j >= 2:
                        a.wait_ge(sem["sRW"], 16 * (j - 1))
                    a.activation(out=rstd_row[:, jj, :], in_=var_row[:, jj, :], func=AF.Ln, bias=eps_c[0:1, :])
                    a.activation(out=rstd_row[:, jj, :], in_=rstd_row[:, jj, :], func=AF.Exp, scale=-0.5)
                    a.activation(out=negm_bf[:, jj, :], in_=m_row[:, jj, :], func=AF.Identity, scale=-1.0).then_inc(sem["sRST"], 1)
                    for i in range(NI):
                        ci = j * NI + i
                        a.wait_ge(sem["sCV"], ci + 1)
                        if j >= 2 and i == 0:
                            a.wait_ge(sXCSs[j % 2], 16 * ((j - 2) // 2 + 1))
                            a.wait_ge(sem["sXD"], j - 1)
                        a.activation(out=xc_stage[:, jj, i, :], in_=ps_cv[ci % 2][:, :],
                                     func=AF.Silu, bias=convb_c[:, i:i + 1]).then_inc(sem["sXCE"], 1)
                    a.wait_ge(sem["sXD"], j + 1)
                    if j >= 2:
                        a.wait_ge(sCD[j % 2], 16 * ((j - 2) // 2 + 1))
                    a.activation(out=xdbl_st[:, jj, :], in_=ps_xd[:, :], func=AF.Identity).then_inc(sem["sXDE"], 1)

            @block.sync
            def _(s):
                for j in range(NJ):
                    jj = j % 2
                    s.wait_ge(sem["sRST"], j + 1)
                    s.dma_start(out=rowscr.ap()[:, j * NCH:(j + 1) * NCH],
                                in_=rstd_row[:, jj, :]).then_inc(sem["sRW"], 16)
                    s.wait_ge(sem["sRW"], 16 * (j + 1))
                    if j >= 2:
                        s.wait_ge(sem["sEVC"], NWF * (j - 1))
                    s.dma_start(out=rstd_bc[:, jj, :],
                                in_=rowscr.ap()[:, j * NCH:(j + 1) * NCH].broadcast_to([P, NCH])).then_inc(sRB[j % 2], 16)
                    s.wait_ge(sem["sEVC"], NWF * (j + 1))
                    s.dma_start(out=z_v[:, :, j * NCH:(j + 1) * NCH],
                                in_=z_stage[:, jj, :, :]).then_inc(sZSs[j % 2], 16)
                    s.wait_ge(sem["sXDE"], j + 1)
                    s.dma_start(out=cc1_in.ap()[:, j * NCH:(j + 1) * NCH],
                                in_=xdbl_st[:, jj, :]).then_inc(sCD[j % 2], 16)

        GL = 1024
        with ExitStack() as pa2:
            EA2 = pa2.enter_context
            gtile = EA2(nc.sbuf_tensor("gtile", [P, 2, DFF], F32))
            gabs = EA2(nc.sbuf_tensor("gabs", [P, DFF], F32))
            gcol1 = EA2(nc.sbuf_tensor("gcol1", [P, ND], F32))
            grow = EA2(nc.sbuf_tensor("grow", [1, P * ND], F32))
            g2wg = EA2(nc.sbuf_tensor("g2wg", [P, 2, GL], F32))
            g2wa = EA2(nc.sbuf_tensor("g2wa", [P, 2, GL], F32))
            g2scr = EA2(nc.sbuf_tensor("g2scr", [P, 2, GL], F32))
            g2ones = EA2(nc.sbuf_tensor("g2ones", [P, GL], F32))
            g2s1 = EA2(nc.sbuf_tensor("g2s1", [P, 32], F32))
            g2st2 = EA2(nc.sbuf_tensor("g2st2", [4, GL], F32))
            g2scr2 = EA2(nc.sbuf_tensor("g2scr2", [4, GL], F32))
            g2dum = EA2(nc.sbuf_tensor("g2dum", [4, GL], F32))
            g2s2r = EA2(nc.sbuf_tensor("g2s2r", [4, 1], F32))
            g2st3 = EA2(nc.sbuf_tensor("g2st3", [1, 4], F32))
            g2scr3 = EA2(nc.sbuf_tensor("g2scr3", [1, 4], F32))
            g2sc = EA2(nc.sbuf_tensor("g2sc", [1, 12], F32))

            @block.sync
            def _(s):
                s.wait_ge(sem["sWIN"], NWF * NJ)
                s.wait_ge(sem["sXD"], NJ)
                s.wait_ge(sXCSs[0], 16 * ((NJ + 1) // 2))
                s.wait_ge(sXCSs[1], 16 * (NJ // 2))
                s.wait_ge(sCD[0], 16 * ((NJ + 1) // 2))
                s.wait_ge(sCD[1], 16 * (NJ // 2))
                for kt in range(ND):
                    if kt >= 2:
                        s.wait_ge(sem["sGA"], kt - 1)
                    s.dma_start(out=gtile[:, kt % 2, :], in_=w1t_v[:, kt, :]).then_inc(sGW[kt % 2], 16)
                s.wait_ge(sem["sGA"], ND)
                s.dma_start(out=g1d.ap(), in_=gcol1[:, :]).then_inc(sem["sGR"], 16)
                s.wait_ge(sem["sGR"], 16)
                s.dma_start(out=grow[:, 0:P * ND],
                            in_=g1d.ap().rearrange("p n -> (p n)")[None, :]).then_inc(sem["sGR"], 16)
                s.wait_ge(sem["sTH"], 1)
                s.dma_start(out=thr_d.ap()[:, 0:1], in_=gsc[:, 2:3]).then_inc(sem["sGR"], 16)
                s.wait_ge(sem["sGR"], 48)
                s.dma_start(out=thr_c[:, 0:1],
                            in_=thr_d.ap()[:, 0:1].broadcast_to([P, 1])).then_inc(sem["sTH"], 16)
                for g in range(32):
                    if g >= 2:
                        s.wait_ge(sem["sG2A"], g - 1)
                    s.dma_start(out=g2wg[:, g % 2, :], in_=w2g_v[:, g, :]).then_inc(sem["sG2L"], 16)
                s.wait_ge(sem["sG2E"], 34)
                s.dma_start(out=g2D2.ap(), in_=g2s1[:, :]).then_inc(sem["sG2D"], 16)
                s.wait_ge(sem["sG2D"], 16)
                s.dma_start(out=g2st2[:, :],
                            in_=g2D2.ap().rearrange("(j r) c -> j (r c)", j=4)).then_inc(sem["sG2D"], 16)
                s.wait_ge(sem["sG2X"], 1)
                s.dma_start(out=g2d3.ap().rearrange("o f -> f o"), in_=g2s2r[:, :]).then_inc(sem["sG2D"], 16)
                s.wait_ge(sem["sG2D"], 48)
                s.dma_start(out=g2st3[:, :], in_=g2d3.ap()).then_inc(sem["sG2D"], 16)
                s.wait_ge(sem["sG2C"], 1)
                s.dma_start(out=chl_d.ap(), in_=g2sc[:, 9:12]).then_inc(sem["sCHB"], 16)
                s.wait_ge(sem["sCHB"], 16)
                s.dma_start(out=chl_c[:, :], in_=chl_d.ap().broadcast_to([P, 3])).then_inc(sem["sCHB"], 16)

            @block.scalar
            def _(a):
                for kt in range(ND):
                    a.wait_ge(sGW[kt % 2], 16 * (kt // 2 + 1))
                    act = a.activation(out=gabs[:, :], in_=gtile[:, kt % 2, :], func=AF.Abs,
                                       accum_out=gcol1[:, kt:kt + 1])
                    if kt < ND - 1:
                        act.then_inc(sem["sGA"], 1)
                a.activation(out=gabs[:, :], in_=gtile[:, (ND - 1) % 2, :], func=AF.Abs,
                             accum_out=gcol1[:, ND - 1:ND]).then_inc(sem["sGA"], 1)
                for g in range(32):
                    a.wait_ge(sem["sG2L"], 16 * (g + 1))
                    if g >= 2:
                        a.wait_ge(sem["sG2S"], g - 1)
                    a.activation(out=g2wa[:, g % 2, :], in_=g2wg[:, g % 2, :],
                                 func=AF.Abs).then_inc(sem["sG2A"], 1)
                a.wait_ge(sem["sWL"], 16 * 9)
                a.activation(out=negA[:, :, :], in_=negA[:, :, :], func=AF.Exp).then_inc(sem["sNA"], 1)
                a.wait_ge(sem["sTH"], 17)
                a.activation(out=nthr_c[:, 0:1], in_=thr_c[:, 0:1], func=AF.Identity, scale=-1.0).then_inc(sem["sTH"], 1)

            @block.vector
            def _(v):
                v.memset(g2ones[:, :], 1.0)
                v.wait_ge(sem["sGR"], 32)
                v.reduce_sum(gsc[:, 0:1], grow[:, 0:P * ND], axis=mybir.AxisListType.X)
                v.reduce_sum(gsc[:, 0:1], grow[:, 0:P * ND], axis=mybir.AxisListType.X)
                v.tensor_scalar(out=gsc[:, 2:3], in0=gsc[:, 0:1], scalar1=0.5 / (D * DFF),
                                scalar2=0.5e-5, op0=ALU.mult, op1=ALU.add)
                v.reduce_sum(gsc[:, 1:2], grow[:, 0:P * ND], axis=mybir.AxisListType.X)
                v.tensor_scalar(out=gsc[:, 2:3], in0=gsc[:, 0:1], scalar1=0.5 / (D * DFF),
                                scalar2=0.5e-5, op0=ALU.mult, op1=ALU.add).then_inc(sem["sTH"], 1)
                for g in range(32):
                    v.wait_ge(sem["sG2A"], g + 1)
                    v.tensor_tensor_scan(out=g2scr[:, g % 2, :], data0=g2ones[:, :],
                                         data1=g2wa[:, g % 2, :], initial=0.0,
                                         op0=ALU.mult, op1=ALU.add).then_inc(sem["sG2S"], 1)
                    if g >= 1:
                        v.tensor_copy(out=g2s1[:, g - 1:g],
                                      in_=g2scr[:, (g - 1) % 2, GL - 1:GL]).then_inc(sem["sG2E"], 1)
                v.tensor_copy(out=g2dum[0:4, :], in_=g2scr[0:4, 1, :])
                v.tensor_copy(out=g2s1[:, 31:32], in_=g2scr[:, 1, GL - 1:GL]).then_inc(sem["sG2E"], 1)
                v.tensor_copy(out=g2dum[0:4, :], in_=g2scr[0:4, 0, :])
                v.tensor_copy(out=g2s1[:, 30:31], in_=g2scr[:, 0, GL - 1:GL]).then_inc(sem["sG2E"], 1)
                v.tensor_copy(out=g2s1[:, 31:32], in_=g2scr[:, 1, GL - 1:GL]).then_inc(sem["sG2E"], 1)
                v.wait_ge(sem["sG2D"], 32)
                v.tensor_tensor_scan(out=g2scr2[:, :], data0=g2ones[0:4, :],
                                     data1=g2st2[:, :], initial=0.0,
                                     op0=ALU.mult, op1=ALU.add)
                v.tensor_copy(out=g2dum[:, :], in_=g2scr2[:, :])
                v.tensor_copy(out=g2s2r[:, :], in_=g2scr2[:, GL - 1:GL])
                v.tensor_copy(out=g2dum[:, :], in_=g2scr2[:, :])
                v.tensor_copy(out=g2s2r[:, :], in_=g2scr2[:, GL - 1:GL]).then_inc(sem["sG2X"], 1)
                v.wait_ge(sem["sG2D"], 64)
                v.tensor_tensor_scan(out=g2scr3[:, :], data0=g2ones[0:1, 0:4],
                                     data1=g2st3[:, :], initial=0.0,
                                     op0=ALU.mult, op1=ALU.add)

                def spacer():
                    v.tensor_copy(out=g2dum[0:1, :], in_=g2scr2[0:1, :])
                spacer()
                v.tensor_copy(out=g2sc[:, 0:1], in_=g2scr3[:, 3:4])
                spacer()
                v.tensor_copy(out=g2sc[:, 0:1], in_=g2scr3[:, 3:4])
                spacer()
                v.tensor_scalar(out=g2sc[:, 1:2], in0=g2sc[:, 0:1], scalar1=float(2.0 ** -22),
                                scalar2=None, op0=ALU.mult)
                spacer()
                v.tensor_scalar(out=g2sc[:, 2:3], in0=g2sc[:, 1:2], scalar1=float(np.float32(1e-5)),
                                scalar2=None, op0=ALU.add)
                spacer()
                v.tensor_scalar(out=g2sc[:, 3:4], in0=g2sc[:, 2:3], scalar1=0.5,
                                scalar2=None, op0=ALU.mult)
                spacer()
                v.tensor_scalar(out=g2sc[:, 5:6], in0=g2ones[0:1, 0:1], scalar1=float(2.0 ** -25),
                                scalar2=None, op0=ALU.add)
                spacer()
                v.tensor_scalar(out=g2sc[:, 8:9], in0=g2sc[:, 5:6], scalar1=-1.0,
                                scalar2=None, op0=ALU.add)
                spacer()
                v.tensor_scalar(out=g2sc[:, 8:9], in0=g2sc[:, 8:9], scalar1=-1.0,
                                scalar2=float(2.0 ** -25), op0=ALU.mult, op1=ALU.add)
                spacer()
                v.tensor_tensor(out=g2sc[:, 4:5], in0=g2sc[:, 2:3], in1=g2sc[:, 8:9], op=ALU.mult)
                spacer()
                v.tensor_tensor(out=g2sc[:, 9:10], in0=g2sc[:, 3:4], in1=g2sc[:, 4:5], op=ALU.add)
                spacer()
                v.tensor_tensor(out=g2sc[:, 6:7], in0=g2sc[:, 3:4], in1=g2sc[:, 9:10], op=ALU.subtract)
                spacer()
                v.tensor_tensor(out=g2sc[:, 10:11], in0=g2sc[:, 6:7], in1=g2sc[:, 4:5], op=ALU.add)
                spacer()
                v.tensor_scalar(out=g2sc[:, 11:12], in0=g2sc[:, 10:11], scalar1=-1.0,
                                scalar2=None, op0=ALU.mult)
                spacer()
                v.tensor_tensor(out=g2sc[:, 9:10], in0=g2sc[:, 3:4], in1=g2sc[:, 4:5], op=ALU.add)
                v.tensor_tensor(out=g2sc[:, 10:11], in0=g2sc[:, 6:7], in1=g2sc[:, 4:5], op=ALU.add)
                v.tensor_scalar(out=g2sc[:, 11:12], in0=g2sc[:, 10:11], scalar1=-1.0,
                                scalar2=None, op0=ALU.mult).then_inc(sem["sG2C"], 1)
                v.wait_ge(sem["sNA"], 1)
                v.tensor_scalar(out=negA[:, :, :], in0=negA[:, :, :], scalar1=-1.0,
                                scalar2=None, op0=ALU.mult).then_inc(sem["sNA"], 1)

            @block.sync
            def _(s):
                s.wait_ge(sZSs[0], 16 * ((NJ + 1) // 2))
                s.wait_ge(sZSs[1], 16 * (NJ // 2))
                s.wait_ge(sem["sTH"], 18)
                s.sem_inc(sem["sPH"], 1)
            @block.gpsimd
            def _(g):
                g.wait_ge(sem["sXBF"], 16)
                g.wait_ge(sXCSs[0], 16 * ((NJ + 1) // 2))
                g.wait_ge(sXCSs[1], 16 * (NJ // 2))
                g.sem_inc(sem["sPH"], 1)
            @block.vector
            def _(v):
                v.wait_ge(sem["sNA"], 2)
                v.wait_ge(sem["sEVC"], NWF * NJ)
                v.sem_inc(sem["sPH"], 1)
            @block.scalar
            def _(a):
                a.wait_ge(sem["sXDE"], NJ)
                a.wait_ge(sem["sGA"], ND)
                a.wait_ge(sem["sTH"], 18)
                a.sem_inc(sem["sPH"], 1)
            @block.tensor
            def _(t):
                t.wait_ge(sem["sXD"], NJ)
                t.sem_inc(sem["sPH"], 1)

        with ExitStack() as pb:
            EB = pb.enter_context
            b_bc = EB(nc.sbuf_tensor("b_bc", [P, NBC, TH], BF16))
            c_bc = EB(nc.sbuf_tensor("c_bc", [P, NBC, TH], BF16))
            a_bc = EB(nc.sbuf_tensor("a_bc", [P, TH], F32))
            delta = EB(nc.sbuf_tensor("delta", [P, 2, 2, TH], F32))
            arg_s = EB(nc.sbuf_tensor("arg_s", [P, 2, TH], F32))
            t_s = EB(nc.sbuf_tensor("t_s", [P, 2, TH], F32))
            dxc = EB(nc.sbuf_tensor("dxc", [P, 2, TH], BF16))
            dA = EB(nc.sbuf_tensor("dA", [P, 2, TH], F32))
            u_s = EB(nc.sbuf_tensor("u_s", [P, 2, TH], BF16))
            h_s = EB(nc.sbuf_tensor("h_s", [P, 2, 2, TH], BF16))
            xc_pair = EB(nc.sbuf_tensor("xc_pair", [P, 2, 2, TH], BF16))
            z_pair = EB(nc.sbuf_tensor("z_pair", [P, 2, 2, TH], BF16))
            ybuf = EB(nc.sbuf_tensor("ybuf", [P, TH], F32))
            yg_s = EB(nc.sbuf_tensor("yg_s", [P, 2, 2, TH], BF16))
            ps_y = [EB(nc.psum_tensor(f"ps_y{k}", [P, TH], F32)) for k in range(2)]

            def step_of(h, pi, s):
                return (h * NPI + pi) * S + s

            @block.sync
            def _(s):
                s.wait_ge(sem["sPH"], PH_A)
                for h in range(2):
                    t0 = h * TH
                    if h == 1:
                        s.wait_ge(sem["sA1"], NPI)
                    s.dma_start(out=a_bc[:, :],
                                in_=cc1_out.ap()[0:1, t0:t0 + TH].broadcast_to([P, TH])).then_inc(sem["sABC"], 16)
                    for pi in range(NPI):
                        gp = h * NPI + pi
                        if gp >= 2:
                            s.wait_ge(sem["sYG"], 2 * (gp - 1))
                        if dbg and gp == 4:
                            s.wait_ge(sem["sOSD"], 48)
                        s.dma_start(out=xc_pair[:, gp % 2, :, :],
                                    in_=xc_v[:, 2 * pi:2 * pi + 2, t0:t0 + TH]).then_inc(sZL[gp % 2], 16)
                        s.dma_start(out=z_pair[:, gp % 2, :, :],
                                    in_=z_v[:, 2 * pi:2 * pi + 2, t0:t0 + TH]).then_inc(sZL[gp % 2], 16)
                        for st in range(S):
                            stp = step_of(h, pi, st)
                            if stp >= NBC:
                                s.wait_ge(sem["sPR"], stp - NBC + 1)
                            s.dma_start(out=b_bc[:, stp % NBC, :],
                                        in_=xdbl_bf.ap()[1 + st:2 + st, t0:t0 + TH].broadcast_to([P, TH])).then_inc(sBC[stp % NBC], 16)
                            s.dma_start(out=c_bc[:, stp % NBC, :],
                                        in_=xdbl_bf.ap()[1 + S + st:2 + S + st, t0:t0 + TH].broadcast_to([P, TH])).then_inc(sBC[stp % NBC], 16)

            @block.scalar
            def _(a):
                a.wait_ge(sem["sPH"], PH_A)
                for h in range(2):
                    for pi in range(NPI):
                        gp = h * NPI + pi
                        a.wait_ge(sem["sABC"], 16 * (h + 1))
                        if gp >= 1:
                            a.wait_ge(sem["sDL"], gp)
                        for iw in range(2):
                            i = 2 * pi + iw
                            a.activation(out=arg_s[:, iw, :], in_=a_bc[:, :], func=AF.Identity,
                                         scale=wdt_c[:, i:i + 1], bias=bdt_c[:, i:i + 1])
                            a.activation(out=t_s[:, iw, :], in_=arg_s[:, iw, :], func=AF.Abs)
                            a.activation(out=t_s[:, iw, :], in_=t_s[:, iw, :], func=AF.Exp, scale=-1.0)
                            ln_i = a.activation(out=t_s[:, iw, :], in_=t_s[:, iw, :], func=AF.Ln, bias=1.0)
                        ln_i.then_inc(sem["sA1"], 1)
                        a.wait_ge(sZL[gp % 2], 32 * (gp // 2 + 1))
                        a.activation(out=z_pair[:, gp % 2, :, :], in_=z_pair[:, gp % 2, :, :],
                                     func=AF.Silu).then_inc(sem["sSZ"], 1)
                        a.wait_ge(sem["sDL"], gp + 1)
                        for st in range(S):
                            stp = step_of(h, pi, st)
                            for iw in range(2):
                                sub = 2 * stp + iw
                                if sub >= 2:
                                    a.wait_ge(sem["sSC"], sub - 1)
                                da_i = a.activation(out=dA[:, sub % 2, :], in_=delta[:, gp % 2, iw, :],
                                                    func=AF.Exp, scale=negA[:, 2 * pi + iw, st:st + 1])
                            da_i.then_inc(sem["sDA"], 1)

            @block.vector
            def _(v):
                v.wait_ge(sem["sPH"], PH_A)
                for h in range(2):
                    for pi in range(NPI):
                        gp = h * NPI + pi
                        v.wait_ge(sem["sA1"], gp + 1)
                        if gp >= 2:
                            v.wait_ge(sem["sDA"], S * (gp - 1))
                        if dbg and gp == 4:
                            v.wait_ge(sem["sOSD"], 48)
                        for iw in range(2):
                            v.scalar_tensor_tensor(out=delta[:, gp % 2, iw, :], in0=arg_s[:, iw, :],
                                                   scalar=0.0, in1=t_s[:, iw, :], op0=ALU.max, op1=ALU.add)
                        v.wait_ge(sZL[gp % 2], 32 * (gp // 2 + 1))
                        if gp >= 1:
                            v.wait_ge(sem["sU"], gp * S)
                        for iw in range(2):
                            dxc_i = v.tensor_tensor(out=dxc[:, iw, :], in0=delta[:, gp % 2, iw, :],
                                                    in1=xc_pair[:, gp % 2, iw, :], op=ALU.mult)
                        dxc_i.then_inc(sem["sDL"], 1)
                        for st in range(S):
                            stp = step_of(h, pi, st)
                            v.wait_ge(sBC[stp % NBC], 32 * (stp // NBC + 1))
                            v.wait_ge(sem["sDA"], stp + 1)
                            if stp >= 2:
                                v.wait_ge(sem["sYA"], stp - 1)
                            for iw in range(2):
                                i = 2 * pi + iw
                                if iw == 0:
                                    v.tensor_tensor(out=u_s[:, 0, :], in0=dxc[:, 0, :],
                                                    in1=b_bc[:, stp % NBC, :], op=ALU.mult)
                                else:
                                    v.wait_ge(sem["sU"], gp * S + st + 1)
                                v.tensor_tensor_scan(out=h_s[:, stp % 2, iw, :],
                                                     data0=dA[:, (2 * stp + iw) % 2, :], data1=u_s[:, iw, :],
                                                     initial=(0.0 if h == 0 else carry[:, i, st, 1:2]),
                                                     op0=ALU.mult, op1=ALU.add).then_inc(sem["sSC"], 1)
                            for iw in range(2):
                                i = 2 * pi + iw
                                if h == 0:
                                    v.tensor_copy(out=carry[:, i, st, :],
                                                  in_=h_s[:, stp % 2, iw, TH - 2:TH])
                                cm_i = v.tensor_tensor(out=h_s[:, stp % 2, iw, :], in0=h_s[:, stp % 2, iw, :],
                                                       in1=c_bc[:, stp % NBC, :], op=ALU.mult)
                            cm_i.then_inc(sem["sPR"], 1)
                        v.wait_ge(sem["sYA"], S * (gp + 1))
                        v.wait_ge(sem["sSZ"], gp + 1)
                        if gp >= 2:
                            v.wait_ge(sYSs[gp % 2], 16 * ((gp - 2) // 2 + 1))
                        for iw in range(2):
                            i = 2 * pi + iw
                            v.scalar_tensor_tensor(out=ybuf[:, :], in0=xc_pair[:, gp % 2, iw, :],
                                                   scalar=dp_c[:, i:i + 1], in1=ps_y[iw][:, :],
                                                   op0=ALU.mult, op1=ALU.add)
                            v.tensor_tensor(out=yg_s[:, gp % 2, iw, :], in0=ybuf[:, :],
                                            in1=z_pair[:, gp % 2, iw, :], op=ALU.mult).then_inc(sem["sYG"], 1)

            @block.tensor
            def _(t):
                t.wait_ge(sem["sPH"], PH_A)
                for h in range(2):
                    for pi in range(NPI):
                        gp = h * NPI + pi
                        for st in range(S):
                            stp = step_of(h, pi, st)
                            t.wait_ge(sem["sPR"], stp + 1)
                            if st == 0:
                                t.wait_ge(sem["sYG"], 2 * gp)
                            for iw in range(2):
                                for nw in range(NTHC):
                                    ya_i = t.matmul(ps_y[iw][:, nw * NW:(nw + 1) * NW], ident_bf[:, :],
                                                    h_s[:, stp % 2, iw, nw * NW:(nw + 1) * NW],
                                                    start=(st == 0), stop=(st == S - 1))
                            ya_i.then_inc(sem["sYA"], 1)

            @block.gpsimd
            def _(g):
                g.wait_ge(sem["sPH"], PH_A)
                if dbg:
                    g.wait_ge(sem["sDL"], 3)
                    g.dma_start(out=dbg_dl.ap(), in_=delta[:, 0, :, :].rearrange("p a t -> p (a t)")).then_inc(sem["sOSD"], 16)
                    g.wait_ge(sem["sSZ"], 3)
                    g.dma_start(out=dbg_xcp.ap(), in_=xc_pair[:, 0, :, :].rearrange("p a t -> p (a t)")).then_inc(sem["sOSD"], 16)
                    g.dma_start(out=dbg_zp.ap(), in_=z_pair[:, 0, :, :].rearrange("p a t -> p (a t)")).then_inc(sem["sOSD"], 16)
                for h in range(2):
                    if dbg and h == 1:
                        g.dma_start(out=dbg_cr.ap(), in_=carry[:, :, :, :].rearrange("p i s c -> p (i s c)")).then_inc(sem["sOSD"], 16)
                    for pi in range(NPI):
                        gp = h * NPI + pi
                        g.wait_ge(sem["sDL"], gp + 1)
                        for st in range(S):
                            stp = step_of(h, pi, st)
                            g.wait_ge(sBC[stp % NBC], 32 * (stp // NBC + 1))
                            if gp * 32 + 2 * st >= 1:
                                g.wait_ge(sem["sSC"], gp * 32 + 2 * st)
                            g.tensor_tensor(out=u_s[:, 1, :], in0=dxc[:, 1, :],
                                            in1=b_bc[:, stp % NBC, :],
                                            op=ALU.mult).then_inc(sem["sU"], 1)
                        g.wait_ge(sem["sYG"], 2 * (gp + 1))
                        g.dma_start(out=ygin_v[:, 2 * pi:2 * pi + 2, h * TH:(h + 1) * TH],
                                    in_=yg_s[:, gp % 2, :, :]).then_inc(sYSs[gp % 2], 16)
                g.wait_ge(sYSs[0], 16 * ((2 * NPI + 1) // 2))
                g.wait_ge(sYSs[1], 16 * (2 * NPI // 2))
                g.collective_compute(
                    "AllGather", ALU.bypass,
                    replica_groups=[[0, 1], [2, 3], [4, 5], [6, 7]],
                    ins=[ygin.ap()], outs=[ygout.ap()],
                ).then_inc(sem["sAT"])
                g.wait_ge(sem["sAT"], 1)

        @block.sync
        def _(s):
            for q in range(NBC):
                nq = len([x for x in range(NST) if x % NBC == q])
                s.wait_ge(sBC[q], 32 * nq)
            s.sem_inc(sem["sPH"], 1)
        @block.gpsimd
        def _(g):
            g.sem_inc(sem["sPH"], 1)
        @block.vector
        def _(v):
            v.wait_ge(sem["sYG"], 2 * 2 * NPI)
            v.sem_inc(sem["sPH"], 1)
        @block.scalar
        def _(a):
            a.wait_ge(sem["sDA"], NST)
            a.sem_inc(sem["sPH"], 1)
        @block.tensor
        def _(t):
            t.wait_ge(sem["sYA"], NST)
            t.sem_inc(sem["sPH"], 1)

        with ExitStack() as pc:
            EC = pc.enter_context
            x_res = EC(nc.sbuf_tensor("x_res", [P, ND, TC], BF16))
            b1c = EC(nc.sbuf_tensor("b1c", [P, NF], F32))
            b2_bf = EC(nc.sbuf_tensor("b2_bf", [1, D], BF16))
            ps_s2x = EC(nc.psum_tensor("ps_s2x", [1, NWC], F32))
            ps_s2q = EC(nc.psum_tensor("ps_s2q", [1, NWC], F32))

            with ExitStack() as pc1:
                EC1 = pc1.enter_context
                wout_bf = EC1(nc.sbuf_tensor("wout_bf", [P, NK2, D], BF16))
                ygs = EC1(nc.sbuf_tensor("ygs", [P, NK2, TC], BF16))
                xh_bf = EC1(nc.sbuf_tensor("xh_bf", [P, ND, TC], BF16))
                ygts = EC1(nc.sbuf_tensor("ygts", [P, TC], BF16))
                ps_wo = [EC1(nc.psum_tensor(f"ps_wo{k}", [P, WOC], F32)) for k in range(2)]

                @block.gpsimd
                def _(g):
                    g.wait_ge(sem["sPH"], PH_C)
                    g.dma_start(out=wout_bf[:, :, :], in_=w_out_v).then_inc(sem["sCL"], 16)
                    g.dma_start(out=xh_bf[:, :, :], in_=xTh_v).then_inc(sem["sCL"], 16)
                    g.dma_start(out=b2_bf[:, :], in_=b2.ap()).then_inc(sem["sCL"], 16)
                    g.dma_start(out=b1c[:, :], in_=b1.ap()).then_inc(sem["sCL"], 16)

                @block.scalar
                def _(a):
                    a.wait_ge(sem["sPH"], PH_C)
                    for kc in range(NK2):
                        if kc >= 1:
                            a.wait_ge(sem["sYGC"], kc)
                        a.dma_start(out=ygts[:, :], in_=ygout_v[:, kc, TC:2 * TC]).then_inc(sem["sYGT"], 16)

                @block.sync
                def _(s):
                    s.wait_ge(sem["sPH"], PH_C)
                    for kc in range(NK2):
                        s.dma_start(out=ygs[:, kc, :], in_=ygout_v[:, kc, 0:TC]).then_inc(sYGL[kc % 2], 16)

                @block.vector
                def _(v):
                    v.wait_ge(sem["sPH"], PH_C)
                    v.wait_ge(sem["sWL"], 16 * 9)
                    for kc in range(NK2):
                        v.wait_ge(sYGL[kc % 2], 16 * (kc // 2 + 1))
                        v.wait_ge(sem["sYGT"], 16 * (kc + 1))
                        v.tensor_scalar(out=ygs[:, kc, :], in0=ygs[:, kc, :], scalar1=selc[:, 0:1],
                                        scalar2=None, op0=ALU.mult)
                        v.scalar_tensor_tensor(out=ygs[:, kc, :], in0=ygts[:, :],
                                               scalar=selc[:, 1:2], in1=ygs[:, kc, :],
                                               op0=ALU.mult, op1=ALU.add).then_inc(sem["sYGC"], 1)
                    v.wait_ge(sem["sCL"], 16 * 4)
                    for mt in range(ND):
                        for nch in range(NWOC):
                            wo = mt * NWOC + nch
                            v.wait_ge(sem["sWO"], wo + 1)
                            v.tensor_tensor(out=x_res[:, mt, nch * WOC:(nch + 1) * WOC],
                                            in0=ps_wo[wo % 2][:, :],
                                            in1=xh_bf[:, mt, nch * WOC:(nch + 1) * WOC],
                                            op=ALU.add).then_inc(sem["sXR"], 1)

                @block.tensor
                def _(t):
                    t.wait_ge(sem["sPH"], PH_C)
                    t.wait_ge(sem["sCL"], 16 * 4)
                    t.wait_ge(sem["sYGC"], NK2)
                    for mt in range(ND):
                        for nch in range(NWOC):
                            wo = mt * NWOC + nch
                            if wo >= 2:
                                t.wait_ge(sem["sXR"], wo - 1)
                            for kc in range(NK2):
                                t.matmul(ps_wo[wo % 2][:, :], wout_bf[:, kc, mt * P:(mt + 1) * P],
                                         ygs[:, kc, nch * WOC:(nch + 1) * WOC],
                                         start=(kc == 0), stop=(kc == NK2 - 1)).then_inc(
                                             sem["sWO"], 1 if kc == NK2 - 1 else 0)

            with ExitStack() as pc2:
                EC2 = pc2.enter_context
                xn2 = EC(nc.sbuf_tensor("xn2", [P, ND, TC], BF16))
                sq2 = EC2(nc.sbuf_tensor("sq2", [P, ND, TC], BF16))
                m2_bc = EC2(nc.sbuf_tensor("m2_bc", [P, TC], F32))
                r2_bc = EC2(nc.sbuf_tensor("r2_bc", [P, TC], F32))
                m2r = EC2(nc.sbuf_tensor("m2r", [1, 2, NWC], F32))
                e22r = EC2(nc.sbuf_tensor("e22r", [1, 2, NWC], F32))
                v2r = EC2(nc.sbuf_tensor("v2r", [1, NWC], F32))
                r2r = EC2(nc.sbuf_tensor("r2r", [1, 2, NWC], F32))
                tsub2 = EC2(nc.sbuf_tensor("tsub2", [P, TC], F32))

                @block.scalar
                def _(a):
                    a.wait_ge(sem["sXR"], ND * NWOC)
                    a.activation(out=sq2[:, :, :], in_=x_res[:, :, :], func=AF.Square).then_inc(sem["sSQ2"], 1)
                    for nch in range(NCCH):
                        pp = nch % 2
                        a.wait_ge(sem["sST2"], nch + 1)
                        if nch >= 2:
                            a.wait_ge(sem["sRW2"], 32 * (nch - 1))
                        a.activation(out=m2r[:, pp, :], in_=ps_s2x[:, :], func=AF.Identity, scale=1.0 / D)
                        a.activation(out=e22r[:, pp, :], in_=ps_s2q[:, :], func=AF.Identity, scale=1.0 / D).then_inc(sem["sEV2"], 1)
                        a.wait_ge(sem["sVAR2"], nch + 1)
                        a.activation(out=r2r[:, pp, :], in_=v2r[:, :], func=AF.Ln, bias=eps_c[0:1, :])
                        a.activation(out=r2r[:, pp, :], in_=r2r[:, pp, :], func=AF.Exp, scale=-0.5).then_inc(sem["sRS2"], 1)

                @block.vector
                def _(v):
                    for nch in range(NCCH):
                        pp = nch % 2
                        v.wait_ge(sem["sEV2"], nch + 1)
                        v.tensor_tensor(out=v2r[:, :], in0=m2r[:, pp, :], in1=m2r[:, pp, :], op=ALU.mult)
                        v.tensor_tensor(out=v2r[:, :], in0=e22r[:, pp, :], in1=v2r[:, :], op=ALU.subtract).then_inc(sem["sVAR2"], 1)
                    v.wait_ge(sem["sRB2"], 32)
                    for i in range(ND):
                        v.tensor_tensor(out=tsub2[:, :], in0=x_res[:, i, :], in1=m2_bc[:, :], op=ALU.subtract)
                        v.tensor_tensor(out=xn2[:, i, :], in0=tsub2[:, :], in1=r2_bc[:, :], op=ALU.mult).then_inc(sem["sXN"], 1)

                @block.tensor
                def _(t):
                    t.wait_ge(sem["sSQ2"], 1)
                    for nch in range(NCCH):
                        if nch >= 1:
                            t.wait_ge(sem["sEV2"], nch)
                        for kt in range(ND):
                            t.matmul(ps_s2x[:, :], ones_bf[:, :], x_res[:, kt, nch * NWC:(nch + 1) * NWC],
                                     start=(kt == 0), stop=(kt == ND - 1))
                        for kt in range(ND):
                            t.matmul(ps_s2q[:, :], ones_bf[:, :], sq2[:, kt, nch * NWC:(nch + 1) * NWC],
                                     start=(kt == 0), stop=(kt == ND - 1)).then_inc(
                                         sem["sST2"], 1 if kt == ND - 1 else 0)

                @block.sync
                def _(s):
                    for nch in range(NCCH):
                        pp = nch % 2
                        s.wait_ge(sem["sRS2"], nch + 1)
                        s.dma_start(out=row2_d.ap()[0:1, nch * NWC:(nch + 1) * NWC],
                                    in_=m2r[:, pp, :]).then_inc(sem["sRW2"], 16)
                        s.dma_start(out=row2_d.ap()[1:2, nch * NWC:(nch + 1) * NWC],
                                    in_=r2r[:, pp, :]).then_inc(sem["sRW2"], 16)
                    s.wait_ge(sem["sRW2"], 32 * NCCH)
                    s.dma_start(out=m2_bc[:, :], in_=row2_d.ap()[0:1, :].broadcast_to([P, TC])).then_inc(sem["sRB2"], 16)
                    s.dma_start(out=r2_bc[:, :], in_=row2_d.ap()[1:2, :].broadcast_to([P, TC])).then_inc(sem["sRB2"], 16)

            wq1 = EC(nc.sbuf_tensor("wq1", [P, ND, DFF], BF16))
            with ExitStack() as pc3:
                EC3 = pc3.enter_context
                wtile = EC3(nc.sbuf_tensor("wtile", [P, 2, DFF], F32))
                mp_s = EC3(nc.sbuf_tensor("mp_s", [P, DFF], F32))
                wqt = EC3(nc.sbuf_tensor("wqt", [P, DFF], BF16))

                def _emit_qload(eng, q, sem_ld):
                    if q >= 2:
                        if q - 2 < ND:
                            eng.wait_ge(sem["sQ1"], q - 1)
                        else:
                            eng.wait_ge(sem["sQ2"], q - ND - 1)
                    if q < ND:
                        eng.dma_start(out=wtile[:, q % 2, :], in_=w1t_v[:, q, :]).then_inc(sem_ld, 16)
                    else:
                        eng.dma_start(out=wtile[:, q % 2, 0:D], in_=w2t_v[:, q - ND, :]).then_inc(sem_ld, 16)

                def _wait_qload(v, q):
                    if q % 2 == 0:
                        v.wait_ge(sem["sQL"], 16 * (q // 2 + 1))
                    else:
                        v.wait_ge(sem["sQLo"], 16 * ((q - 1) // 2 + 1))

                @block.sync
                def _(s):
                    s.wait_ge(sem["sXN"], ND)
                    s.wait_ge(sem["sST2"], NCCH)
                    for q in range(0, ND + NF, 2):
                        _emit_qload(s, q, sem["sQL"])

                @block.vector
                def _(v):
                    v.wait_ge(sem["sTH"], 18)
                    for kt in range(ND):
                        _wait_qload(v, kt)
                        v.tensor_scalar(out=mp_s[:, :], in0=wtile[:, kt % 2, :], scalar1=thr_c[:, 0:1],
                                        scalar2=None, op0=ALU.is_ge)
                        v.scalar_tensor_tensor(out=mp_s[:, :], in0=wtile[:, kt % 2, :], scalar=nthr_c[:, 0:1],
                                               in1=mp_s[:, :], op0=ALU.is_gt, op1=ALU.add)
                        v.tensor_scalar(out=wq1[:, kt, :], in0=mp_s[:, :], scalar1=-1.0,
                                        scalar2=None, op0=ALU.add).then_inc(sem["sQ1"], 1)
                    v.wait_ge(sem["sCHB"], 32)
                    for kt in range(NF):
                        q = ND + kt
                        _wait_qload(v, q)
                        if kt >= 2:
                            v.wait_ge(sem["sQ2S"], 16 * (kt - 1))
                        v.scalar_tensor_tensor(out=mp_s[:, 0:D], in0=wtile[:, q % 2, 0:D],
                                               scalar=chl_c[:, 0:1], op0=ALU.subtract,
                                               in1=chl_c[:, 1:2].broadcast_to([P, D]), op1=ALU.is_gt)
                        v.scalar_tensor_tensor(out=mp_s[:, D:2 * D], in0=wtile[:, q % 2, 0:D],
                                               scalar=chl_c[:, 0:1], op0=ALU.add,
                                               in1=chl_c[:, 2:3].broadcast_to([P, D]), op1=ALU.is_gt)
                        v.scalar_tensor_tensor(out=wqt[:, (kt % 2) * D:(kt % 2) * D + D], in0=mp_s[:, 0:D],
                                               scalar=-1.0, op0=ALU.add,
                                               in1=mp_s[:, D:2 * D], op1=ALU.add).then_inc(sem["sQ2"], 1)

                @block.gpsimd
                def _(g):
                    g.wait_ge(sem["sXN"], ND)
                    g.wait_ge(sem["sST2"], NCCH)
                    for q in range(1, ND + 2, 2):
                        _emit_qload(g, q, sem["sQLo"])
                    for kt in range(NF):
                        q2 = ND + kt + 2
                        if q2 < ND + NF and q2 % 2 == 1:
                            _emit_qload(g, q2, sem["sQLo"])
                        g.wait_ge(sem["sQ2"], kt + 1)
                        g.dma_start(out=wq2_v[:, kt, :],
                                    in_=wqt[:, (kt % 2) * D:(kt % 2) * D + D]).then_inc(sem["sQ2S"], 16)
                    if dbg:
                        g.wait_ge(sem["sXR"], ND * NWOC)
                        g.dma_start(out=dbg_xres.ap().rearrange("(i p) t -> p i t", p=P),
                                    in_=x_res[:, :, :]).then_inc(sem["sOSD"], 16)
                        g.dma_start(out=dbg_xc.ap(), in_=xc_dram.ap()).then_inc(sem["sOSD"], 16)
                        g.dma_start(out=dbg_z.ap(), in_=z_dram.ap()).then_inc(sem["sOSD"], 16)
                        g.dma_start(out=dbg_xdbl.ap(), in_=cc1_out.ap()).then_inc(sem["sOSD"], 16)
                        g.dma_start(out=dbg_yg.ap(), in_=ygout.ap()).then_inc(sem["sOSD"], 16)
                        g.dma_start(out=dbg_thr.ap(), in_=gsc[:, 2:4]).then_inc(sem["sOSD"], 16)

            with ExitStack() as pc4:
                EC4 = pc4.enter_context
                hbuf = EC4(nc.sbuf_tensor("hbuf", [P, NF, CC], BF16))
                outst = EC4(nc.sbuf_tensor("outst", [P, 2, D], F32))
                wq2s = EC4(nc.sbuf_tensor("wq2s", [P, 2, D], BF16))
                ps_f1 = [EC4(nc.psum_tensor(f"ps_f1{k}", [P, CC], F32)) for k in range(2)]
                ps_f2 = [EC4(nc.psum_tensor(f"ps_f2{k}", [P, NW2], F32)) for k in range(2 * NF2N)]

                @block.gpsimd
                def _(g):
                    g.wait_ge(sem["sQ2S"], 16 * NF)
                    for cc in range(NCC):
                        for tp in range(NTP):
                            for kt in range(NF):
                                gidx = (cc * NTP + tp) * NF + kt
                                if gidx % 2 == 1:
                                    if gidx >= 2:
                                        g.wait_ge(sem["sF2K"], gidx - 1)
                                    g.dma_start(out=wq2s[:, gidx % 2, :],
                                                in_=wq2_v[:, kt, :]).then_inc(sW2L[1], 16)

                @block.sync
                def _(s):
                    s.wait_ge(sem["sQ2S"], 16 * NF)
                    for cc in range(NCC):
                        for tp in range(NTP):
                            for kt in range(NF):
                                gidx = (cc * NTP + tp) * NF + kt
                                if gidx % 2 == 0:
                                    if gidx >= 2:
                                        s.wait_ge(sem["sF2K"], gidx - 1)
                                    s.dma_start(out=wq2s[:, gidx % 2, :],
                                                in_=wq2_v[:, kt, :]).then_inc(sW2L[0], 16)
                            for tw in range(2):
                                ts_i = cc * NTS + tp * 2 + tw
                                s.wait_ge(sem["sOE"], NF2N * (ts_i + 1))
                                t0 = ts_i * P
                                s.dma_start(out=out.ap()[t0:t0 + P, :], in_=outst[:, ts_i % 2, :]).then_inc(sOS[ts_i % 2], 16)
                    nts_all = NCC * NTS
                    s.wait_ge(sOS[0], 16 * ((nts_all + 1) // 2))
                    s.wait_ge(sOS[1], 16 * (nts_all // 2))
                    if dbg:
                        s.wait_ge(sem["sOSD"], 16 * 10)

                @block.tensor
                def _(t):
                    t.wait_ge(sem["sXN"], ND)
                    t.wait_ge(sem["sQ1"], ND)
                    t.wait_ge(sem["sQ2"], NF)
                    for cc in range(NCC):
                        for ft in range(NF):
                            fg = cc * NF + ft
                            if fg >= 2:
                                t.wait_ge(sem["sHE"], fg - 1)
                            for kt in range(ND):
                                t.matmul(ps_f1[fg % 2][:, :], wq1[:, kt, ft * P:(ft + 1) * P],
                                         xn2[:, kt, cc * CC:(cc + 1) * CC],
                                         start=(kt == 0), stop=(kt == ND - 1)).then_inc(
                                             sem["sF1"], 1 if kt == ND - 1 else 0)
                        t.wait_ge(sem["sHE"], (cc + 1) * NF)
                        for tp in range(NTP):
                            if not (cc == 0 and tp == 0):
                                t.wait_ge(sem["sOE"], NF2N * (cc * NTS + 2 * tp))
                            for kt in range(NF):
                                gidx = (cc * NTP + tp) * NF + kt
                                t.wait_ge(sW2L[gidx % 2], 16 * (gidx // 2 + 1))
                                for tw in range(2):
                                    tcol = (tp * 2 + tw) * P
                                    for nw in range(NF2N):
                                        fk_i = t.matmul(ps_f2[tw * NF2N + nw][:, :],
                                                        hbuf[:, kt, tcol:tcol + P],
                                                        wq2s[:, gidx % 2, nw * NW2:(nw + 1) * NW2],
                                                        start=(kt == 0), stop=False)
                                fk_i.then_inc(sem["sF2K"], 1)
                            for tw in range(2):
                                ts_i = cc * NTS + tp * 2 + tw
                                tcol = (tp * 2 + tw) * P
                                for nw in range(NF2N):
                                    t.matmul(ps_f2[tw * NF2N + nw][:, :], onesr_bf[:, :],
                                             b2_bf[:, nw * NW2:(nw + 1) * NW2],
                                             start=False, stop=False)
                                    for q in range(NQS):
                                        dt_i = nw * NQS + q
                                        t.matmul(ps_f2[tw * NF2N + nw][:, :],
                                                 x_res[:, dt_i, cc * CC + tcol:cc * CC + tcol + P],
                                                 idshift[:, q, :],
                                                 start=False, stop=(q == NQS - 1)).then_inc(
                                                     sem["sF2"], 1 if q == NQS - 1 else 0)

                @block.scalar
                def _(a):
                    a.wait_ge(sem["sQ2S"], 16 * NF)
                    for cc in range(NCC):
                        if cc >= 1:
                            a.wait_ge(sem["sF2K"], cc * NTP * NF)
                        for ft in range(NF):
                            fg = cc * NF + ft
                            a.wait_ge(sem["sF1"], fg + 1)
                            a.activation(out=hbuf[:, ft, :], in_=ps_f1[fg % 2][:, :],
                                         func=AF.Relu, bias=b1c[:, ft:ft + 1]).then_inc(sem["sHE"], 1)
                        for tp in range(NTP):
                            for tw in range(2):
                                ts_i = cc * NTS + tp * 2 + tw
                                if ts_i >= 2:
                                    a.wait_ge(sOS[ts_i % 2], 16 * ((ts_i - 2) // 2 + 1))
                                for nw in range(NF2N):
                                    a.wait_ge(sem["sF2"], ts_i * NF2N + nw + 1)
                                    a.activation(out=outst[:, ts_i % 2, nw * NW2:(nw + 1) * NW2],
                                                 in_=ps_f2[tw * NF2N + nw][:, :],
                                                 func=AF.Identity).then_inc(sem["sOE"], 1)

    return nc


def _stage_inputs(ins, cfg):
    c = cfg
    x = np.ascontiguousarray(np.asarray(ins["L0_attn_x"], dtype=np.float32))
    W_in = np.asarray(ins["L0_attn_W_in"], dtype=np.float32)
    conv_w = np.asarray(ins["L0_attn_conv_w"], dtype=np.float32)
    conv_b = np.asarray(ins["L0_attn_conv_b"], dtype=np.float32)
    W_xp = np.asarray(ins["L0_attn_W_xproj"], dtype=np.float32)
    W_dt = np.asarray(ins["L0_attn_W_dt"], dtype=np.float32)
    b_dt = np.asarray(ins["L0_attn_b_dt"], dtype=np.float32)
    A_log = np.asarray(ins["L0_attn_A_log"], dtype=np.float32)
    Dp = np.asarray(ins["L0_attn_Dp"], dtype=np.float32)
    W_out = np.asarray(ins["L0_attn_W_out"], dtype=np.float32)
    W1 = np.asarray(ins["L0_ffn_W1"], dtype=np.float32)
    b1 = np.asarray(ins["L0_ffn_b1"], dtype=np.float32)
    W2 = np.asarray(ins["L0_ffn_W2"], dtype=np.float32)
    b2 = np.asarray(ins["L0_ffn_b2"], dtype=np.float32)

    DI2 = 2 * c.DI
    w2g = np.ascontiguousarray(
        W2.reshape(32, 32, 4, 32, 32).transpose(3, 2, 0, 1, 4).reshape(32 * 128, 1024))
    maps = []
    for core in range(8):
        b, k = core // 2, core % 2
        lo, hi = k * c.DI, (k + 1) * c.DI
        xT = np.ascontiguousarray(x[b].T)
        m = dict(
            xT=xT,
            xTh=np.ascontiguousarray(xT[:, k * c.TC:(k + 1) * c.TC]),
            w_in=np.ascontiguousarray(np.concatenate(
                [W_in[:, lo:hi], W_in[:, DI2 + lo:DI2 + hi]], axis=1)),
            conv_w=np.ascontiguousarray(conv_w[lo:hi, 0, :]),
            conv_b=np.ascontiguousarray(conv_b[lo:hi].reshape(c.NI, P).T),
            w_xp=np.ascontiguousarray(W_xp[lo:hi]),
            w_dt=np.ascontiguousarray(W_dt[0, lo:hi].reshape(c.NI, P).T),
            b_dt=np.ascontiguousarray(b_dt[lo:hi].reshape(c.NI, P).T),
            a_log=np.ascontiguousarray(A_log[lo:hi].reshape(c.NI, P, c.S).transpose(1, 0, 2).reshape(P, c.NI * c.S)),
            dp=np.ascontiguousarray(Dp[lo:hi].reshape(c.NI, P).T),
            w_out=np.ascontiguousarray(W_out),
            w1t=np.ascontiguousarray(W1.T),
            b1=np.ascontiguousarray(b1.reshape(c.NF, P).T),
            w2t=np.ascontiguousarray(W2.T),
            w2g=w2g,
            b2=np.ascontiguousarray(b2[None, :]),
            sel=np.ascontiguousarray(np.broadcast_to(
                np.array([1.0 - k, float(k)], dtype=np.float32), (128, 2))),
        )
        maps.append(m)
    return maps


def kernel(**inputs):
    cfg = Cfg()
    nc = build_program(cfg, dbg=False)
    in_maps = _stage_inputs(inputs, cfg)
    res = run_bass_kernel_spmd(nc, in_maps, core_ids=list(range(8)))
    B = inputs["L0_attn_x"].shape[0]
    outp = np.empty((B, cfg.T, cfg.D), dtype=np.float32)
    for core in range(8):
        b, k = core // 2, core % 2
        outp[b, k * cfg.TC:(k + 1) * cfg.TC, :] = res.results[core]["out"]
    return outp


# version 11
# speedup vs baseline: 1.1484x; 1.0577x over previous
import numpy as np
from contextlib import ExitStack

import concourse.bass as bass
import concourse.mybir as mybir
from concourse.bass_utils import run_bass_kernel_spmd

F32 = mybir.dt.float32
BF16 = mybir.dt.bfloat16
AF = mybir.ActivationFunctionType
ALU = mybir.AluOpType
P = 128


class Cfg:
    def __init__(self, T=4096, D=1024, DI=1024, S=16, DFF=4096, NCH=512, CC=512):
        self.T = T
        self.D = D
        self.DI = DI
        self.S = S
        self.DFF = DFF
        self.NCH = NCH
        self.CC = CC
        self.NJ = T // NCH
        self.ND = D // P
        self.NI = DI // P
        self.NWF = 2 * self.NI
        self.NF = DFF // P
        self.TH = T // 2
        self.TC = T // 2
        self.NPI = self.NI // 2
        self.NCC = self.TC // CC
        self.NTS = CC // P
        self.XD = 1 + 2 * S
        self.NW2 = min(512, D)
        self.NQS = self.NW2 // P
        self.NF2N = D // self.NW2


def build_program(cfg, dbg=False):
    c = cfg
    T, D, DI, S, DFF = c.T, c.D, c.DI, c.S, c.DFF
    NCH, NJ, ND, NI, NWF, NF = c.NCH, c.NJ, c.ND, c.NI, c.NWF, c.NF
    TH, TC, NPI, CC, NCC, NTS, XD = c.TH, c.TC, c.NPI, c.CC, c.NCC, c.NTS, c.XD
    NW2, NQS, NF2N = c.NW2, c.NQS, c.NF2N
    NK2 = 2 * NI
    NBC = 3
    NW = min(512, TH)
    NTHC = TH // NW
    NWC = min(512, TC)
    NCCH = TC // NWC
    WOC = min(512, TC)
    NWOC = TC // WOC
    NWIN_CS = 2 * DI // NCH
    NST = 2 * NPI * S
    NTP = NTS // 2
    PH_A, PH_C = 5, 10

    nc = bass.Bass("TRN2", target_bir_lowering=False, debug=False,
                   detect_race_conditions=False)

    xT = nc.dram_tensor("xT", [D, T], F32, kind="ExternalInput")
    xTh = nc.dram_tensor("xTh", [D, TC], F32, kind="ExternalInput")
    w_in = nc.dram_tensor("w_in", [D, 2 * DI], F32, kind="ExternalInput")
    conv_w = nc.dram_tensor("conv_w", [DI, 4], F32, kind="ExternalInput")
    conv_b = nc.dram_tensor("conv_b", [P, DI // P], F32, kind="ExternalInput")
    w_xp = nc.dram_tensor("w_xp", [DI, XD], F32, kind="ExternalInput")
    w_dt = nc.dram_tensor("w_dt", [P, DI // P], F32, kind="ExternalInput")
    b_dt = nc.dram_tensor("b_dt", [P, DI // P], F32, kind="ExternalInput")
    a_log = nc.dram_tensor("a_log", [P, (DI // P) * S], F32, kind="ExternalInput")
    dp = nc.dram_tensor("dp", [P, DI // P], F32, kind="ExternalInput")
    w_out = nc.dram_tensor("w_out", [2 * DI, D], F32, kind="ExternalInput")
    w1t = nc.dram_tensor("w1t", [D, DFF], F32, kind="ExternalInput")
    b1 = nc.dram_tensor("b1", [P, DFF // P], F32, kind="ExternalInput")
    w2t = nc.dram_tensor("w2t", [DFF, D], F32, kind="ExternalInput")
    b2 = nc.dram_tensor("b2", [1, D], F32, kind="ExternalInput")
    sel = nc.dram_tensor("sel", [P, 2], F32, kind="ExternalInput")
    w2g = nc.dram_tensor("w2g", [32 * P, 1024], F32, kind="ExternalInput")
    out = nc.dram_tensor("out", [TC, D], F32, kind="ExternalOutput")

    xc_dram = nc.dram_tensor("xc_dram", [DI, T], BF16, kind="Internal")
    z_dram = nc.dram_tensor("z_dram", [DI, T], BF16, kind="Internal")
    rowscr = nc.dram_tensor("rowscr", [1, T], F32, kind="Internal")
    cc1_in = nc.dram_tensor("cc1_in", [XD, T], F32, kind="Internal")
    cc1_out = nc.dram_tensor("cc1_out", [XD, T], F32, kind="Internal")
    xdbl_bf = nc.dram_tensor("xdbl_bf", [XD, T], BF16, kind="Internal")
    ygin = nc.dram_tensor("ygin", [DI, T], BF16, kind="Internal")
    ygout = nc.dram_tensor("ygout", [2 * DI, T], BF16, kind="Internal")
    wq2_dram = nc.dram_tensor("wq2_dram", [DFF, D], BF16, kind="Internal")
    g1d = nc.dram_tensor("g1d", [P, ND], F32, kind="Internal")
    g2D2 = nc.dram_tensor("g2D2", [P, 32], F32, kind="Internal")
    g2d3 = nc.dram_tensor("g2d3", [1, 4], F32, kind="Internal")
    chl_d = nc.dram_tensor("chl_d", [1, 3], F32, kind="Internal")
    thr_d = nc.dram_tensor("thr_d", [1, 2], F32, kind="Internal")
    row2_d = nc.dram_tensor("row2_d", [2, TC], F32, kind="Internal")

    if dbg:
        dbg_xc = nc.dram_tensor("dbg_xc", [DI, T], BF16, kind="ExternalOutput")
        dbg_z = nc.dram_tensor("dbg_z", [DI, T], BF16, kind="ExternalOutput")
        dbg_xdbl = nc.dram_tensor("dbg_xdbl", [XD, T], F32, kind="ExternalOutput")
        dbg_yg = nc.dram_tensor("dbg_yg", [2 * DI, T], BF16, kind="ExternalOutput")
        dbg_xres = nc.dram_tensor("dbg_xres", [D, TC], F32, kind="ExternalOutput")
        dbg_thr = nc.dram_tensor("dbg_thr", [1, 2], F32, kind="ExternalOutput")
        dbg_dl = nc.dram_tensor("dbg_dl", [P, 2 * TH], F32, kind="ExternalOutput")
        dbg_cr = nc.dram_tensor("dbg_cr", [P, NI * S * 2], F32, kind="ExternalOutput")
        dbg_xcp = nc.dram_tensor("dbg_xcp", [P, 2 * TH], F32, kind="ExternalOutput")
        dbg_zp = nc.dram_tensor("dbg_zp", [P, 2 * TH], F32, kind="ExternalOutput")

    w2g_v = w2g.ap().rearrange("(g p) t -> p g t", p=P)
    xT_v = xT.ap().rearrange("(i p) t -> p i t", p=P)
    xTh_v = xTh.ap().rearrange("(i p) t -> p i t", p=P)
    w_in_v = w_in.ap().rearrange("(i p) f -> p i f", p=P)
    w_xp_v = w_xp.ap().rearrange("(i p) f -> p i f", p=P)
    a_log_v = a_log.ap().rearrange("p (i s) -> p i s", i=DI // P)
    conv_w_v = conv_w.ap().rearrange("(i p) k -> p i k", p=P)
    xc_v = xc_dram.ap().rearrange("(i p) t -> p i t", p=P)
    z_v = z_dram.ap().rearrange("(i p) t -> p i t", p=P)
    w_out_v = w_out.ap().rearrange("(i p) f -> p i f", p=P)
    w1t_v = w1t.ap().rearrange("(i p) f -> p i f", p=P)
    w2t_v = w2t.ap().rearrange("(i p) f -> p i f", p=P)
    ygin_v = ygin.ap().rearrange("(i p) t -> p i t", p=P)
    ygout_v = ygout.ap().rearrange("(i p) t -> p i t", p=P)
    wq2_v = wq2_dram.ap().rearrange("(i p) f -> p i f", p=P)

    with ExitStack() as top:
        E = top.enter_context
        sem_names = [
            "sWL", "sIO", "sPRE", "sCSm", "sCSe", "sNA",
            "sXL0", "sXL1", "sSQ", "sST", "sEV1", "sVAR", "sRST", "sRW",
            "sRB0", "sRB1",
            "sWIN", "sEVC", "sZS0", "sZS1", "sCV", "sXCE", "sXCS0", "sXCS1",
            "sXD", "sXDE",
            "sGW0", "sGW1", "sGA", "sGR", "sTH", "sCD0", "sCD1", "sCC1",
            "sXBF", "sPH",
            "sBC0", "sBC1", "sBC2", "sSC", "sABC", "sZL0", "sZL1",
            "sA1", "sDL", "sDA", "sPR", "sYA", "sYG",
            "sSZ", "sYS0", "sYS1", "sAT", "sU",
            "sCL", "sWO", "sXR", "sSQ2", "sST2", "sEV2", "sVAR2", "sRS2",
            "sRW2", "sRB2", "sXN", "sQL", "sQ1", "sQ2", "sQ2S",
            "sF1", "sHE", "sW2L0", "sW2L1", "sF2K", "sF2", "sOE",
            "sOS0", "sOS1", "sOSD", "sYGL0", "sYGL1", "sYGC",
            "sG2L", "sG2A", "sG2S", "sG2E", "sG2D", "sG2X", "sG2C", "sCHB",
            "sYGT", "sQLo", "sW2L2",
        ]
        sem = {n: E(nc.semaphore(n)) for n in sem_names}
        sXL = [sem["sXL0"], sem["sXL1"]]
        sRB = [sem["sRB0"], sem["sRB1"]]
        sZSs = [sem["sZS0"], sem["sZS1"]]
        sXCSs = [sem["sXCS0"], sem["sXCS1"]]
        sGW = [sem["sGW0"], sem["sGW1"]]
        sCD = [sem["sCD0"], sem["sCD1"]]
        sBC = [sem["sBC0"], sem["sBC1"], sem["sBC2"]]
        sZL = [sem["sZL0"], sem["sZL1"]]
        sYSs = [sem["sYS0"], sem["sYS1"]]
        sW2L = [sem["sW2L0"], sem["sW2L1"], sem["sW2L2"]]
        sOS = [sem["sOS0"], sem["sOS1"]]
        sYGL = [sem["sYGL0"], sem["sYGL1"]]

        ones_bf = E(nc.sbuf_tensor("ones_bf", [P, 1], BF16))
        onesr_bf = E(nc.sbuf_tensor("onesr_bf", [1, P], BF16))
        ident_bf = E(nc.sbuf_tensor("ident_bf", [P, P], BF16))
        idshift = E(nc.sbuf_tensor("idshift", [P, NQS, NW2], BF16))
        iota_r = E(nc.sbuf_tensor("iota_r", [P, 1], F32))
        scr1 = E(nc.sbuf_tensor("scr1", [P, 1], F32))
        negA = E(nc.sbuf_tensor("negA", [P, NI, S], F32))
        carry = E(nc.sbuf_tensor("carry", [P, NI, S, 2], F32))
        wdt_c = E(nc.sbuf_tensor("wdt_c", [P, NI], F32))
        bdt_c = E(nc.sbuf_tensor("bdt_c", [P, NI], F32))
        dp_c = E(nc.sbuf_tensor("dp_c", [P, NI], F32))
        convb_c = E(nc.sbuf_tensor("convb_c", [P, NI], F32))
        thr_c = E(nc.sbuf_tensor("thr_c", [P, 2], F32))
        nthr_c = E(nc.sbuf_tensor("nthr_c", [P, 2], F32))
        chl_c = E(nc.sbuf_tensor("chl_c", [P, 3], F32))
        gsc = E(nc.sbuf_tensor("gsc", [1, 4], F32))
        eps_c = E(nc.sbuf_tensor("eps_c", [P, 1], F32))
        selc = E(nc.sbuf_tensor("selc", [P, 2], F32))

        block = E(nc.Block())

        with ExitStack() as pa:
            EA = pa.enter_context
            iota_c = EA(nc.sbuf_tensor("iota_c", [P, max(NW2, P)], F32))
            w_in_bf = EA(nc.sbuf_tensor("w_in_bf", [P, ND, 2 * DI], BF16))
            wxp_bf = EA(nc.sbuf_tensor("wxp_bf", [P, NI, XD], BF16))
            diag_bf = EA(nc.sbuf_tensor("diag_bf", [P, NI * 4, P], BF16))
            cwcol = EA(nc.sbuf_tensor("cwcol", [P, NI, 4], F32))
            xb = EA(nc.sbuf_tensor("xb", [P, 2, ND, NCH], BF16))
            sq = EA(nc.sbuf_tensor("sq", [P, ND, NCH], BF16))
            xc_pre = EA(nc.sbuf_tensor("xc_pre", [P, NI, 3 + T], BF16))
            z_stage = EA(nc.sbuf_tensor("z_stage", [P, 2, NI, NCH], BF16))
            xc_stage = EA(nc.sbuf_tensor("xc_stage", [P, 2, NI, NCH], BF16))
            rstd_bc = EA(nc.sbuf_tensor("rstd_bc", [P, 2, NCH], F32))
            m_row = EA(nc.sbuf_tensor("m_row", [1, 2, NCH], F32))
            e2_row = EA(nc.sbuf_tensor("e2_row", [1, 2, NCH], F32))
            var_row = EA(nc.sbuf_tensor("var_row", [1, 2, NCH], F32))
            rstd_row = EA(nc.sbuf_tensor("rstd_row", [1, 2, NCH], F32))
            negm_bf = EA(nc.sbuf_tensor("negm_bf", [1, 2, NCH], BF16))
            cs_row = EA(nc.sbuf_tensor("cs_row", [1, 2 * DI], F32))
            cs_bf = EA(nc.sbuf_tensor("cs_bf", [1, 2 * DI], BF16))
            xdbl_st = EA(nc.sbuf_tensor("xdbl_st", [XD, 2, NCH], F32))
            ps_sx = EA(nc.psum_tensor("ps_sx", [1, NCH], F32))
            ps_sq = EA(nc.psum_tensor("ps_sq", [1, NCH], F32))
            ps_win = [EA(nc.psum_tensor(f"ps_win{k}", [P, NCH], F32)) for k in range(2)]
            ps_cv = [EA(nc.psum_tensor(f"ps_cv{k}", [P, NCH], F32)) for k in range(2)]
            ps_xd = EA(nc.psum_tensor("ps_xd", [XD, NCH], F32))

            @block.gpsimd
            def _(g):
                g.dma_start(out=w_in_bf[:, :, :], in_=w_in_v).then_inc(sem["sWL"], 16)
                g.dma_start(out=wxp_bf[:, :, :], in_=w_xp_v).then_inc(sem["sWL"], 16)
                g.dma_start(out=convb_c[:, 0:NI], in_=conv_b.ap()).then_inc(sem["sWL"], 16)
                g.dma_start(out=wdt_c[:, 0:NI], in_=w_dt.ap()).then_inc(sem["sWL"], 16)
                g.dma_start(out=bdt_c[:, 0:NI], in_=b_dt.ap()).then_inc(sem["sWL"], 16)
                g.dma_start(out=dp_c[:, 0:NI], in_=dp.ap()).then_inc(sem["sWL"], 16)
                g.dma_start(out=cwcol[:, :, :], in_=conv_w_v).then_inc(sem["sWL"], 16)
                g.dma_start(out=negA[:, :, :], in_=a_log_v).then_inc(sem["sWL"], 16)
                g.dma_start(out=selc[:, :], in_=sel.ap()).then_inc(sem["sWL"], 16)
                g.iota(iota_c[:, :], pattern=[[1, iota_c.shape[1]]], base=0,
                       channel_multiplier=0, allow_small_or_imprecise_dtypes=True)
                g.iota(iota_r[:, :], pattern=[[0, 1]], base=0,
                       channel_multiplier=1, allow_small_or_imprecise_dtypes=True).then_inc(sem["sIO"], 1)
                for j in range(NJ):
                    if j >= 2:
                        g.wait_ge(sem["sWIN"], NWF * (j - 1))
                        g.wait_ge(sem["sSQ"], j - 1)
                    g.dma_start(out=xb[:, j % 2, :, :],
                                in_=xT_v[:, :, j * NCH:(j + 1) * NCH]).then_inc(sXL[j % 2], 16)
                    if j >= 1:
                        g.wait_ge(sem["sXCE"], NI * j)
                        g.dma_start(out=xc_v[:, :, (j - 1) * NCH:j * NCH],
                                    in_=xc_stage[:, (j - 1) % 2, :, :]).then_inc(sXCSs[(j - 1) % 2], 16)
                g.wait_ge(sem["sXCE"], NI * NJ)
                g.dma_start(out=xc_v[:, :, (NJ - 1) * NCH:NJ * NCH],
                            in_=xc_stage[:, (NJ - 1) % 2, :, :]).then_inc(sXCSs[(NJ - 1) % 2], 16)
                g.wait_ge(sCD[0], 16 * ((NJ + 1) // 2))
                g.wait_ge(sCD[1], 16 * (NJ // 2))
                g.collective_compute(
                    "AllReduce", ALU.add,
                    replica_groups=[[0, 1], [2, 3], [4, 5], [6, 7]],
                    ins=[cc1_in.ap()], outs=[cc1_out.ap()],
                ).then_inc(sem["sCC1"])
                g.wait_ge(sem["sCC1"], 1)
                g.dma_start(out=xdbl_bf.ap(), in_=cc1_out.ap(), max_dma_last_dim=8192).then_inc(sem["sXBF"], 16)

            @block.vector
            def _(v):
                v.wait_ge(sem["sIO"], 1)
                v.memset(ones_bf[:, :], 1.0)
                v.memset(onesr_bf[:, :], 1.0)
                v.tensor_tensor(out=ident_bf[:, :], in0=iota_c[:, 0:P],
                                in1=iota_r[:, :].broadcast_to([P, P]), op=ALU.is_equal)
                for q in range(NQS):
                    v.tensor_scalar(out=scr1[:, :], in0=iota_r[:, :], scalar1=float(P * q),
                                    scalar2=None, op0=ALU.add)
                    v.tensor_tensor(out=idshift[:, q, :], in0=iota_c[:, 0:NW2],
                                    in1=scr1[:, :].broadcast_to([P, NW2]), op=ALU.is_equal)
                for i in range(NI):
                    v.memset(xc_pre[:, i, 0:3], 0.0)
                v.memset(eps_c[:, :], 1e-5).then_inc(sem["sPRE"], 1)
                v.wait_ge(sem["sWL"], 16 * 9)
                for i in range(NI):
                    for k in range(4):
                        v.tensor_tensor(out=diag_bf[:, i * 4 + k, :], in0=ident_bf[:, :],
                                        in1=cwcol[:, i, k:k + 1].broadcast_to([P, P]),
                                        op=ALU.mult).then_inc(sem["sPRE"], 1)
                for j in range(NJ):
                    jj = j % 2
                    v.wait_ge(sem["sEV1"], j + 1)
                    v.tensor_tensor(out=var_row[:, jj, :], in0=m_row[:, jj, :], in1=m_row[:, jj, :], op=ALU.mult)
                    v.tensor_tensor(out=var_row[:, jj, :], in0=e2_row[:, jj, :], in1=var_row[:, jj, :], op=ALU.subtract).then_inc(sem["sVAR"], 1)
                    for ft in range(NWF):
                        ftg = j * NWF + ft
                        v.wait_ge(sem["sWIN"], ftg + 1)
                        v.wait_ge(sRB[j % 2], 16 * (j // 2 + 1))
                        if ft == NI and j >= 2:
                            v.wait_ge(sZSs[j % 2], 16 * ((j - 2) // 2 + 1))
                        dest = xc_pre[:, ft, 3 + j * NCH:3 + (j + 1) * NCH] if ft < NI \
                            else z_stage[:, jj, ft - NI, :]
                        v.tensor_tensor(out=dest, in0=ps_win[ftg % 2][:, :],
                                        in1=rstd_bc[:, jj, :], op=ALU.mult).then_inc(sem["sEVC"], 1)

            @block.scalar
            def _(a):
                for nchk in range(NWIN_CS):
                    a.wait_ge(sem["sCSm"], nchk + 1)
                    a.activation(out=cs_row[:, nchk * NCH:(nchk + 1) * NCH],
                                 in_=ps_win[nchk % 2][0:1, 0:NCH], func=AF.Identity).then_inc(sem["sCSe"], 1)
                a.activation(out=cs_bf[:, :], in_=cs_row[:, :], func=AF.Identity).then_inc(sem["sCSe"], 1)

            @block.tensor
            def _(t):
                t.wait_ge(sem["sWL"], 16 * 9)
                t.wait_ge(sem["sPRE"], 1)
                for nchk in range(NWIN_CS):
                    if nchk >= 2:
                        t.wait_ge(sem["sCSe"], nchk - 1)
                    for kt in range(ND):
                        t.matmul(ps_win[nchk % 2][0:1, 0:NCH], ones_bf[:, :],
                                 w_in_bf[:, kt, nchk * NCH:(nchk + 1) * NCH],
                                 start=(kt == 0), stop=(kt == ND - 1)).then_inc(
                                     sem["sCSm"], 1 if kt == ND - 1 else 0)
                for j in range(NJ):
                    jj = j % 2
                    t.wait_ge(sem["sSQ"], j + 1)
                    if j >= 1:
                        t.wait_ge(sem["sEV1"], j)
                    for kt in range(ND):
                        t.matmul(ps_sx[:, :], ones_bf[:, :], xb[:, jj, kt, :],
                                 start=(kt == 0), stop=(kt == ND - 1))
                    for kt in range(ND):
                        t.matmul(ps_sq[:, :], ones_bf[:, :], sq[:, kt, :],
                                 start=(kt == 0), stop=(kt == ND - 1)).then_inc(
                                     sem["sST"], 1 if kt == ND - 1 else 0)
                    t.wait_ge(sem["sRST"], j + 1)
                    if j == 0:
                        t.wait_ge(sem["sCSe"], NWIN_CS + 1)
                    for ft in range(NWF):
                        ftg = j * NWF + ft
                        if ftg >= 2:
                            t.wait_ge(sem["sEVC"], ftg - 1)
                        for kt in range(ND):
                            t.matmul(ps_win[ftg % 2][:, :], w_in_bf[:, kt, ft * P:(ft + 1) * P],
                                     xb[:, jj, kt, :], start=(kt == 0), stop=False)
                        t.matmul(ps_win[ftg % 2][:, :], cs_bf[0:1, ft * P:(ft + 1) * P],
                                 negm_bf[0:1, jj, :], start=False, stop=True).then_inc(sem["sWIN"], 1)
                    if j == 0:
                        t.wait_ge(sem["sPRE"], 1 + 4 * NI)
                    for i in range(NI):
                        ci = j * NI + i
                        t.wait_ge(sem["sEVC"], j * NWF + i + 1)
                        if ci >= 2:
                            t.wait_ge(sem["sXCE"], ci - 1)
                        for k in range(4):
                            t.matmul(ps_cv[ci % 2][:, :], diag_bf[:, i * 4 + k, :],
                                     xc_pre[:, i, j * NCH + k:j * NCH + k + NCH],
                                     start=(k == 0), stop=(k == 3)).then_inc(
                                         sem["sCV"], 1 if k == 3 else 0)
                    t.wait_ge(sem["sXCE"], NI * (j + 1))
                    if j >= 2:
                        t.wait_ge(sem["sXDE"], j - 1)
                    for i in range(NI):
                        t.matmul(ps_xd[:, :], wxp_bf[:, i, :], xc_stage[:, jj, i, :],
                                 start=(i == 0), stop=(i == NI - 1)).then_inc(
                                     sem["sXD"], 1 if i == NI - 1 else 0)

            @block.scalar
            def _(a):
                for j in range(NJ):
                    jj = j % 2
                    a.wait_ge(sXL[j % 2], 16 * (j // 2 + 1))
                    if j >= 1:
                        a.wait_ge(sem["sST"], j)
                    a.activation(out=sq[:, :, :], in_=xb[:, jj, :, :], func=AF.Square).then_inc(sem["sSQ"], 1)
                    a.wait_ge(sem["sST"], j + 1)
                    if j >= 2:
                        a.wait_ge(sem["sWIN"], NWF * (j - 1))
                    a.activation(out=m_row[:, jj, :], in_=ps_sx[:, :], func=AF.Identity, scale=1.0 / D)
                    a.activation(out=e2_row[:, jj, :], in_=ps_sq[:, :], func=AF.Identity, scale=1.0 / D).then_inc(sem["sEV1"], 1)
                    a.wait_ge(sem["sVAR"], j + 1)
                    if j == 0:
                        a.wait_ge(sem["sPRE"], 1)
                    if j >= 2:
                        a.wait_ge(sem["sRW"], 16 * (j - 1))
                    a.activation(out=rstd_row[:, jj, :], in_=var_row[:, jj, :], func=AF.Ln, bias=eps_c[0:1, :])
                    a.activation(out=rstd_row[:, jj, :], in_=rstd_row[:, jj, :], func=AF.Exp, scale=-0.5)
                    a.activation(out=negm_bf[:, jj, :], in_=m_row[:, jj, :], func=AF.Identity, scale=-1.0).then_inc(sem["sRST"], 1)
                    for i in range(NI):
                        ci = j * NI + i
                        a.wait_ge(sem["sCV"], ci + 1)
                        if j >= 2 and i == 0:
                            a.wait_ge(sXCSs[j % 2], 16 * ((j - 2) // 2 + 1))
                            a.wait_ge(sem["sXD"], j - 1)
                        a.activation(out=xc_stage[:, jj, i, :], in_=ps_cv[ci % 2][:, :],
                                     func=AF.Silu, bias=convb_c[:, i:i + 1]).then_inc(sem["sXCE"], 1)
                    a.wait_ge(sem["sXD"], j + 1)
                    if j >= 2:
                        a.wait_ge(sCD[j % 2], 16 * ((j - 2) // 2 + 1))
                    a.activation(out=xdbl_st[:, jj, :], in_=ps_xd[:, :], func=AF.Identity).then_inc(sem["sXDE"], 1)

            @block.sync
            def _(s):
                for j in range(NJ):
                    jj = j % 2
                    s.wait_ge(sem["sRST"], j + 1)
                    s.dma_start(out=rowscr.ap()[:, j * NCH:(j + 1) * NCH],
                                in_=rstd_row[:, jj, :]).then_inc(sem["sRW"], 16)
                    s.wait_ge(sem["sRW"], 16 * (j + 1))
                    if j >= 2:
                        s.wait_ge(sem["sEVC"], NWF * (j - 1))
                    s.dma_start(out=rstd_bc[:, jj, :],
                                in_=rowscr.ap()[:, j * NCH:(j + 1) * NCH].broadcast_to([P, NCH])).then_inc(sRB[j % 2], 16)
                    s.wait_ge(sem["sEVC"], NWF * (j + 1))
                    s.dma_start(out=z_v[:, :, j * NCH:(j + 1) * NCH],
                                in_=z_stage[:, jj, :, :]).then_inc(sZSs[j % 2], 16)
                    s.wait_ge(sem["sXDE"], j + 1)
                    s.dma_start(out=cc1_in.ap()[:, j * NCH:(j + 1) * NCH],
                                in_=xdbl_st[:, jj, :]).then_inc(sCD[j % 2], 16)

        GL = 1024
        with ExitStack() as pa2:
            EA2 = pa2.enter_context
            gtile = EA2(nc.sbuf_tensor("gtile", [P, 2, DFF], F32))
            gabs = EA2(nc.sbuf_tensor("gabs", [P, DFF], F32))
            gcol1 = EA2(nc.sbuf_tensor("gcol1", [P, ND], F32))
            grow = EA2(nc.sbuf_tensor("grow", [1, P * ND], F32))
            g2wg = EA2(nc.sbuf_tensor("g2wg", [P, 2, GL], F32))
            g2wa = EA2(nc.sbuf_tensor("g2wa", [P, 2, GL], F32))
            g2scr = EA2(nc.sbuf_tensor("g2scr", [P, 2, GL], F32))
            g2ones = EA2(nc.sbuf_tensor("g2ones", [P, GL], F32))
            g2s1 = EA2(nc.sbuf_tensor("g2s1", [P, 32], F32))
            g2st2 = EA2(nc.sbuf_tensor("g2st2", [4, GL], F32))
            g2scr2 = EA2(nc.sbuf_tensor("g2scr2", [4, GL], F32))
            g2dum = EA2(nc.sbuf_tensor("g2dum", [4, GL], F32))
            g2s2r = EA2(nc.sbuf_tensor("g2s2r", [4, 1], F32))
            g2st3 = EA2(nc.sbuf_tensor("g2st3", [1, 4], F32))
            g2scr3 = EA2(nc.sbuf_tensor("g2scr3", [1, 4], F32))
            g2sc = EA2(nc.sbuf_tensor("g2sc", [1, 12], F32))

            @block.sync
            def _(s):
                s.wait_ge(sem["sWIN"], NWF * NJ)
                s.wait_ge(sem["sXD"], NJ)
                s.wait_ge(sXCSs[0], 16 * ((NJ + 1) // 2))
                s.wait_ge(sXCSs[1], 16 * (NJ // 2))
                s.wait_ge(sCD[0], 16 * ((NJ + 1) // 2))
                s.wait_ge(sCD[1], 16 * (NJ // 2))
                for kt in range(ND):
                    if kt >= 2:
                        s.wait_ge(sem["sGA"], kt - 1)
                    s.dma_start(out=gtile[:, kt % 2, :], in_=w1t_v[:, kt, :]).then_inc(sGW[kt % 2], 16)
                s.wait_ge(sem["sGA"], ND)
                s.dma_start(out=g1d.ap(), in_=gcol1[:, :]).then_inc(sem["sGR"], 16)
                s.wait_ge(sem["sGR"], 16)
                s.dma_start(out=grow[:, 0:P * ND],
                            in_=g1d.ap().rearrange("p n -> (p n)")[None, :]).then_inc(sem["sGR"], 16)
                s.wait_ge(sem["sTH"], 1)
                s.dma_start(out=thr_d.ap()[:, 0:1], in_=gsc[:, 2:3]).then_inc(sem["sGR"], 16)
                s.wait_ge(sem["sGR"], 48)
                s.dma_start(out=thr_c[:, 0:1],
                            in_=thr_d.ap()[:, 0:1].broadcast_to([P, 1])).then_inc(sem["sTH"], 16)
                for g in range(32):
                    if g >= 2:
                        s.wait_ge(sem["sG2A"], g - 1)
                    s.dma_start(out=g2wg[:, g % 2, :], in_=w2g_v[:, g, :]).then_inc(sem["sG2L"], 16)
                s.wait_ge(sem["sG2E"], 34)
                s.dma_start(out=g2D2.ap(), in_=g2s1[:, :]).then_inc(sem["sG2D"], 16)
                s.wait_ge(sem["sG2D"], 16)
                s.dma_start(out=g2st2[:, :],
                            in_=g2D2.ap().rearrange("(j r) c -> j (r c)", j=4)).then_inc(sem["sG2D"], 16)
                s.wait_ge(sem["sG2X"], 1)
                s.dma_start(out=g2d3.ap().rearrange("o f -> f o"), in_=g2s2r[:, :]).then_inc(sem["sG2D"], 16)
                s.wait_ge(sem["sG2D"], 48)
                s.dma_start(out=g2st3[:, :], in_=g2d3.ap()).then_inc(sem["sG2D"], 16)
                s.wait_ge(sem["sG2C"], 1)
                s.dma_start(out=chl_d.ap(), in_=g2sc[:, 9:12]).then_inc(sem["sCHB"], 16)
                s.wait_ge(sem["sCHB"], 16)
                s.dma_start(out=chl_c[:, :], in_=chl_d.ap().broadcast_to([P, 3])).then_inc(sem["sCHB"], 16)

            @block.scalar
            def _(a):
                for kt in range(ND):
                    a.wait_ge(sGW[kt % 2], 16 * (kt // 2 + 1))
                    act = a.activation(out=gabs[:, :], in_=gtile[:, kt % 2, :], func=AF.Abs,
                                       accum_out=gcol1[:, kt:kt + 1])
                    if kt < ND - 1:
                        act.then_inc(sem["sGA"], 1)
                a.activation(out=gabs[:, :], in_=gtile[:, (ND - 1) % 2, :], func=AF.Abs,
                             accum_out=gcol1[:, ND - 1:ND]).then_inc(sem["sGA"], 1)
                for g in range(32):
                    a.wait_ge(sem["sG2L"], 16 * (g + 1))
                    if g >= 2:
                        a.wait_ge(sem["sG2S"], g - 1)
                    a.activation(out=g2wa[:, g % 2, :], in_=g2wg[:, g % 2, :],
                                 func=AF.Abs).then_inc(sem["sG2A"], 1)
                a.wait_ge(sem["sWL"], 16 * 9)
                a.activation(out=negA[:, :, :], in_=negA[:, :, :], func=AF.Exp).then_inc(sem["sNA"], 1)
                a.wait_ge(sem["sTH"], 17)
                a.activation(out=nthr_c[:, 0:1], in_=thr_c[:, 0:1], func=AF.Identity, scale=-1.0).then_inc(sem["sTH"], 1)

            @block.vector
            def _(v):
                v.memset(g2ones[:, :], 1.0)
                v.wait_ge(sem["sGR"], 32)
                v.reduce_sum(gsc[:, 0:1], grow[:, 0:P * ND], axis=mybir.AxisListType.X)
                v.reduce_sum(gsc[:, 0:1], grow[:, 0:P * ND], axis=mybir.AxisListType.X)
                v.tensor_scalar(out=gsc[:, 2:3], in0=gsc[:, 0:1], scalar1=0.5 / (D * DFF),
                                scalar2=0.5e-5, op0=ALU.mult, op1=ALU.add)
                v.reduce_sum(gsc[:, 1:2], grow[:, 0:P * ND], axis=mybir.AxisListType.X)
                v.tensor_scalar(out=gsc[:, 2:3], in0=gsc[:, 0:1], scalar1=0.5 / (D * DFF),
                                scalar2=0.5e-5, op0=ALU.mult, op1=ALU.add).then_inc(sem["sTH"], 1)
                for g in range(32):
                    v.wait_ge(sem["sG2A"], g + 1)
                    v.tensor_tensor_scan(out=g2scr[:, g % 2, :], data0=g2ones[:, :],
                                         data1=g2wa[:, g % 2, :], initial=0.0,
                                         op0=ALU.mult, op1=ALU.add).then_inc(sem["sG2S"], 1)
                    if g >= 1:
                        v.tensor_copy(out=g2s1[:, g - 1:g],
                                      in_=g2scr[:, (g - 1) % 2, GL - 1:GL]).then_inc(sem["sG2E"], 1)
                v.tensor_copy(out=g2dum[0:4, :], in_=g2scr[0:4, 1, :])
                v.tensor_copy(out=g2s1[:, 31:32], in_=g2scr[:, 1, GL - 1:GL]).then_inc(sem["sG2E"], 1)
                v.tensor_copy(out=g2dum[0:4, :], in_=g2scr[0:4, 0, :])
                v.tensor_copy(out=g2s1[:, 30:31], in_=g2scr[:, 0, GL - 1:GL]).then_inc(sem["sG2E"], 1)
                v.tensor_copy(out=g2s1[:, 31:32], in_=g2scr[:, 1, GL - 1:GL]).then_inc(sem["sG2E"], 1)
                v.wait_ge(sem["sG2D"], 32)
                v.tensor_tensor_scan(out=g2scr2[:, :], data0=g2ones[0:4, :],
                                     data1=g2st2[:, :], initial=0.0,
                                     op0=ALU.mult, op1=ALU.add)
                v.tensor_copy(out=g2dum[:, :], in_=g2scr2[:, :])
                v.tensor_copy(out=g2s2r[:, :], in_=g2scr2[:, GL - 1:GL])
                v.tensor_copy(out=g2dum[:, :], in_=g2scr2[:, :])
                v.tensor_copy(out=g2s2r[:, :], in_=g2scr2[:, GL - 1:GL]).then_inc(sem["sG2X"], 1)
                v.wait_ge(sem["sG2D"], 64)
                v.tensor_tensor_scan(out=g2scr3[:, :], data0=g2ones[0:1, 0:4],
                                     data1=g2st3[:, :], initial=0.0,
                                     op0=ALU.mult, op1=ALU.add)

                def spacer():
                    v.tensor_copy(out=g2dum[0:1, :], in_=g2scr2[0:1, :])
                spacer()
                v.tensor_copy(out=g2sc[:, 0:1], in_=g2scr3[:, 3:4])
                spacer()
                v.tensor_copy(out=g2sc[:, 0:1], in_=g2scr3[:, 3:4])
                spacer()
                v.tensor_scalar(out=g2sc[:, 1:2], in0=g2sc[:, 0:1], scalar1=float(2.0 ** -22),
                                scalar2=None, op0=ALU.mult)
                spacer()
                v.tensor_scalar(out=g2sc[:, 2:3], in0=g2sc[:, 1:2], scalar1=float(np.float32(1e-5)),
                                scalar2=None, op0=ALU.add)
                spacer()
                v.tensor_scalar(out=g2sc[:, 3:4], in0=g2sc[:, 2:3], scalar1=0.5,
                                scalar2=None, op0=ALU.mult)
                spacer()
                v.tensor_scalar(out=g2sc[:, 5:6], in0=g2ones[0:1, 0:1], scalar1=float(2.0 ** -25),
                                scalar2=None, op0=ALU.add)
                spacer()
                v.tensor_scalar(out=g2sc[:, 8:9], in0=g2sc[:, 5:6], scalar1=-1.0,
                                scalar2=None, op0=ALU.add)
                spacer()
                v.tensor_scalar(out=g2sc[:, 8:9], in0=g2sc[:, 8:9], scalar1=-1.0,
                                scalar2=float(2.0 ** -25), op0=ALU.mult, op1=ALU.add)
                spacer()
                v.tensor_tensor(out=g2sc[:, 4:5], in0=g2sc[:, 2:3], in1=g2sc[:, 8:9], op=ALU.mult)
                spacer()
                v.tensor_tensor(out=g2sc[:, 9:10], in0=g2sc[:, 3:4], in1=g2sc[:, 4:5], op=ALU.add)
                spacer()
                v.tensor_tensor(out=g2sc[:, 6:7], in0=g2sc[:, 3:4], in1=g2sc[:, 9:10], op=ALU.subtract)
                spacer()
                v.tensor_tensor(out=g2sc[:, 10:11], in0=g2sc[:, 6:7], in1=g2sc[:, 4:5], op=ALU.add)
                spacer()
                v.tensor_scalar(out=g2sc[:, 11:12], in0=g2sc[:, 10:11], scalar1=-1.0,
                                scalar2=None, op0=ALU.mult)
                spacer()
                v.tensor_tensor(out=g2sc[:, 9:10], in0=g2sc[:, 3:4], in1=g2sc[:, 4:5], op=ALU.add)
                v.tensor_tensor(out=g2sc[:, 10:11], in0=g2sc[:, 6:7], in1=g2sc[:, 4:5], op=ALU.add)
                v.tensor_scalar(out=g2sc[:, 11:12], in0=g2sc[:, 10:11], scalar1=-1.0,
                                scalar2=None, op0=ALU.mult).then_inc(sem["sG2C"], 1)
                v.wait_ge(sem["sNA"], 1)
                v.tensor_scalar(out=negA[:, :, :], in0=negA[:, :, :], scalar1=-1.0,
                                scalar2=None, op0=ALU.mult).then_inc(sem["sNA"], 1)

            @block.sync
            def _(s):
                s.wait_ge(sZSs[0], 16 * ((NJ + 1) // 2))
                s.wait_ge(sZSs[1], 16 * (NJ // 2))
                s.wait_ge(sem["sTH"], 18)
                s.sem_inc(sem["sPH"], 1)
            @block.gpsimd
            def _(g):
                g.wait_ge(sem["sXBF"], 16)
                g.wait_ge(sXCSs[0], 16 * ((NJ + 1) // 2))
                g.wait_ge(sXCSs[1], 16 * (NJ // 2))
                g.sem_inc(sem["sPH"], 1)
            @block.vector
            def _(v):
                v.wait_ge(sem["sNA"], 2)
                v.wait_ge(sem["sEVC"], NWF * NJ)
                v.sem_inc(sem["sPH"], 1)
            @block.scalar
            def _(a):
                a.wait_ge(sem["sXDE"], NJ)
                a.wait_ge(sem["sGA"], ND)
                a.wait_ge(sem["sTH"], 18)
                a.sem_inc(sem["sPH"], 1)
            @block.tensor
            def _(t):
                t.wait_ge(sem["sXD"], NJ)
                t.sem_inc(sem["sPH"], 1)

        with ExitStack() as pb:
            EB = pb.enter_context
            b_bc = EB(nc.sbuf_tensor("b_bc", [P, NBC, TH], BF16))
            c_bc = EB(nc.sbuf_tensor("c_bc", [P, NBC, TH], BF16))
            a_bc = EB(nc.sbuf_tensor("a_bc", [P, TH], F32))
            delta = EB(nc.sbuf_tensor("delta", [P, 2, 2, TH], F32))
            arg_s = EB(nc.sbuf_tensor("arg_s", [P, 2, TH], F32))
            t_s = EB(nc.sbuf_tensor("t_s", [P, 2, TH], F32))
            dxc = EB(nc.sbuf_tensor("dxc", [P, 2, TH], BF16))
            dA = EB(nc.sbuf_tensor("dA", [P, 2, TH], F32))
            u_s = EB(nc.sbuf_tensor("u_s", [P, 2, TH], BF16))
            h_s = EB(nc.sbuf_tensor("h_s", [P, 2, 2, TH], BF16))
            xc_pair = EB(nc.sbuf_tensor("xc_pair", [P, 2, 2, TH], BF16))
            z_pair = EB(nc.sbuf_tensor("z_pair", [P, 2, 2, TH], BF16))
            ybuf = EB(nc.sbuf_tensor("ybuf", [P, TH], F32))
            yg_s = EB(nc.sbuf_tensor("yg_s", [P, 2, 2, TH], BF16))
            ps_y = [EB(nc.psum_tensor(f"ps_y{k}", [P, TH], F32)) for k in range(2)]

            def step_of(h, pi, s):
                return (h * NPI + pi) * S + s

            @block.sync
            def _(s):
                s.wait_ge(sem["sPH"], PH_A)
                for h in range(2):
                    t0 = h * TH
                    if h == 1:
                        s.wait_ge(sem["sA1"], NPI)
                    s.dma_start(out=a_bc[:, :],
                                in_=cc1_out.ap()[0:1, t0:t0 + TH].broadcast_to([P, TH])).then_inc(sem["sABC"], 16)
                    for pi in range(NPI):
                        gp = h * NPI + pi
                        if gp >= 2:
                            s.wait_ge(sem["sYG"], 2 * (gp - 1))
                        if dbg and gp == 4:
                            s.wait_ge(sem["sOSD"], 48)
                        s.dma_start(out=xc_pair[:, gp % 2, :, :],
                                    in_=xc_v[:, 2 * pi:2 * pi + 2, t0:t0 + TH]).then_inc(sZL[gp % 2], 16)
                        s.dma_start(out=z_pair[:, gp % 2, :, :],
                                    in_=z_v[:, 2 * pi:2 * pi + 2, t0:t0 + TH]).then_inc(sZL[gp % 2], 16)
                        for st in range(S):
                            stp = step_of(h, pi, st)
                            if stp >= NBC:
                                s.wait_ge(sem["sPR"], stp - NBC + 1)
                            s.dma_start(out=b_bc[:, stp % NBC, :],
                                        in_=xdbl_bf.ap()[1 + st:2 + st, t0:t0 + TH].broadcast_to([P, TH])).then_inc(sBC[stp % NBC], 16)
                            s.dma_start(out=c_bc[:, stp % NBC, :],
                                        in_=xdbl_bf.ap()[1 + S + st:2 + S + st, t0:t0 + TH].broadcast_to([P, TH])).then_inc(sBC[stp % NBC], 16)

            @block.scalar
            def _(a):
                a.wait_ge(sem["sPH"], PH_A)
                for h in range(2):
                    for pi in range(NPI):
                        gp = h * NPI + pi
                        a.wait_ge(sem["sABC"], 16 * (h + 1))
                        if gp >= 1:
                            a.wait_ge(sem["sDL"], gp)
                        for iw in range(2):
                            i = 2 * pi + iw
                            a.activation(out=arg_s[:, iw, :], in_=a_bc[:, :], func=AF.Identity,
                                         scale=wdt_c[:, i:i + 1], bias=bdt_c[:, i:i + 1])
                            a.activation(out=t_s[:, iw, :], in_=arg_s[:, iw, :], func=AF.Abs)
                            a.activation(out=t_s[:, iw, :], in_=t_s[:, iw, :], func=AF.Exp, scale=-1.0)
                            ln_i = a.activation(out=t_s[:, iw, :], in_=t_s[:, iw, :], func=AF.Ln, bias=1.0)
                        ln_i.then_inc(sem["sA1"], 1)
                        a.wait_ge(sZL[gp % 2], 32 * (gp // 2 + 1))
                        a.activation(out=z_pair[:, gp % 2, :, :], in_=z_pair[:, gp % 2, :, :],
                                     func=AF.Silu).then_inc(sem["sSZ"], 1)
                        a.wait_ge(sem["sDL"], gp + 1)
                        for st in range(S):
                            stp = step_of(h, pi, st)
                            for iw in range(2):
                                sub = 2 * stp + iw
                                if sub >= 2:
                                    a.wait_ge(sem["sSC"], sub - 1)
                                da_i = a.activation(out=dA[:, sub % 2, :], in_=delta[:, gp % 2, iw, :],
                                                    func=AF.Exp, scale=negA[:, 2 * pi + iw, st:st + 1])
                            da_i.then_inc(sem["sDA"], 1)

            @block.vector
            def _(v):
                v.wait_ge(sem["sPH"], PH_A)
                for h in range(2):
                    for pi in range(NPI):
                        gp = h * NPI + pi
                        v.wait_ge(sem["sA1"], gp + 1)
                        if gp >= 2:
                            v.wait_ge(sem["sDA"], S * (gp - 1))
                        if dbg and gp == 4:
                            v.wait_ge(sem["sOSD"], 48)
                        for iw in range(2):
                            v.scalar_tensor_tensor(out=delta[:, gp % 2, iw, :], in0=arg_s[:, iw, :],
                                                   scalar=0.0, in1=t_s[:, iw, :], op0=ALU.max, op1=ALU.add)
                        v.wait_ge(sZL[gp % 2], 32 * (gp // 2 + 1))
                        for iw in range(2):
                            dxc_i = v.tensor_tensor(out=dxc[:, iw, :], in0=delta[:, gp % 2, iw, :],
                                                    in1=xc_pair[:, gp % 2, iw, :], op=ALU.mult)
                        dxc_i.then_inc(sem["sDL"], 1)
                        for st in range(S):
                            stp = step_of(h, pi, st)
                            v.wait_ge(sBC[stp % NBC], 32 * (stp // NBC + 1))
                            v.wait_ge(sem["sDA"], stp + 1)
                            if stp >= 2:
                                v.wait_ge(sem["sYA"], stp - 1)
                            for iw in range(2):
                                i = 2 * pi + iw
                                v.tensor_tensor(out=u_s[:, iw, :], in0=dxc[:, iw, :],
                                                in1=b_bc[:, stp % NBC, :], op=ALU.mult)
                                v.tensor_tensor_scan(out=h_s[:, stp % 2, iw, :],
                                                     data0=dA[:, (2 * stp + iw) % 2, :], data1=u_s[:, iw, :],
                                                     initial=(0.0 if h == 0 else carry[:, i, st, 1:2]),
                                                     op0=ALU.mult, op1=ALU.add).then_inc(sem["sSC"], 1)
                            for iw in range(2):
                                i = 2 * pi + iw
                                if h == 0:
                                    v.tensor_copy(out=carry[:, i, st, :],
                                                  in_=h_s[:, stp % 2, iw, TH - 2:TH])
                                cm_i = v.tensor_tensor(out=h_s[:, stp % 2, iw, :], in0=h_s[:, stp % 2, iw, :],
                                                       in1=c_bc[:, stp % NBC, :], op=ALU.mult)
                            cm_i.then_inc(sem["sPR"], 1)
                        v.wait_ge(sem["sYA"], S * (gp + 1))
                        v.wait_ge(sem["sSZ"], gp + 1)
                        if gp >= 2:
                            v.wait_ge(sYSs[gp % 2], 16 * ((gp - 2) // 2 + 1))
                        for iw in range(2):
                            i = 2 * pi + iw
                            v.scalar_tensor_tensor(out=ybuf[:, :], in0=xc_pair[:, gp % 2, iw, :],
                                                   scalar=dp_c[:, i:i + 1], in1=ps_y[iw][:, :],
                                                   op0=ALU.mult, op1=ALU.add)
                            v.tensor_tensor(out=yg_s[:, gp % 2, iw, :], in0=ybuf[:, :],
                                            in1=z_pair[:, gp % 2, iw, :], op=ALU.mult).then_inc(sem["sYG"], 1)

            @block.tensor
            def _(t):
                t.wait_ge(sem["sPH"], PH_A)
                for h in range(2):
                    for pi in range(NPI):
                        gp = h * NPI + pi
                        for st in range(S):
                            stp = step_of(h, pi, st)
                            t.wait_ge(sem["sPR"], stp + 1)
                            if st == 0:
                                t.wait_ge(sem["sYG"], 2 * gp)
                            for iw in range(2):
                                for nw in range(NTHC):
                                    ya_i = t.matmul(ps_y[iw][:, nw * NW:(nw + 1) * NW], ident_bf[:, :],
                                                    h_s[:, stp % 2, iw, nw * NW:(nw + 1) * NW],
                                                    start=(st == 0), stop=(st == S - 1))
                            ya_i.then_inc(sem["sYA"], 1)

            @block.gpsimd
            def _(g):
                g.wait_ge(sem["sPH"], PH_A)
                if dbg:
                    g.wait_ge(sem["sDL"], 3)
                    g.dma_start(out=dbg_dl.ap(), in_=delta[:, 0, :, :].rearrange("p a t -> p (a t)")).then_inc(sem["sOSD"], 16)
                    g.wait_ge(sem["sSZ"], 3)
                    g.dma_start(out=dbg_xcp.ap(), in_=xc_pair[:, 0, :, :].rearrange("p a t -> p (a t)")).then_inc(sem["sOSD"], 16)
                    g.dma_start(out=dbg_zp.ap(), in_=z_pair[:, 0, :, :].rearrange("p a t -> p (a t)")).then_inc(sem["sOSD"], 16)
                for h in range(2):
                    if dbg and h == 1:
                        g.dma_start(out=dbg_cr.ap(), in_=carry[:, :, :, :].rearrange("p i s c -> p (i s c)")).then_inc(sem["sOSD"], 16)
                    for pi in range(NPI):
                        gp = h * NPI + pi
                        g.wait_ge(sem["sYG"], 2 * (gp + 1))
                        g.dma_start(out=ygin_v[:, 2 * pi:2 * pi + 2, h * TH:(h + 1) * TH],
                                    in_=yg_s[:, gp % 2, :, :]).then_inc(sYSs[gp % 2], 16)
                g.wait_ge(sYSs[0], 16 * ((2 * NPI + 1) // 2))
                g.wait_ge(sYSs[1], 16 * (2 * NPI // 2))
                g.collective_compute(
                    "AllGather", ALU.bypass,
                    replica_groups=[[0, 1], [2, 3], [4, 5], [6, 7]],
                    ins=[ygin.ap()], outs=[ygout.ap()],
                ).then_inc(sem["sAT"])
                g.wait_ge(sem["sAT"], 1)

        @block.sync
        def _(s):
            for q in range(NBC):
                nq = len([x for x in range(NST) if x % NBC == q])
                s.wait_ge(sBC[q], 32 * nq)
            s.sem_inc(sem["sPH"], 1)
        @block.gpsimd
        def _(g):
            g.sem_inc(sem["sPH"], 1)
        @block.vector
        def _(v):
            v.wait_ge(sem["sYG"], 2 * 2 * NPI)
            v.sem_inc(sem["sPH"], 1)
        @block.scalar
        def _(a):
            a.wait_ge(sem["sDA"], NST)
            a.sem_inc(sem["sPH"], 1)
        @block.tensor
        def _(t):
            t.wait_ge(sem["sYA"], NST)
            t.sem_inc(sem["sPH"], 1)

        with ExitStack() as pc:
            EC = pc.enter_context
            x_res = EC(nc.sbuf_tensor("x_res", [P, ND, TC], BF16))
            b1c = EC(nc.sbuf_tensor("b1c", [P, NF], F32))
            b2_bf = EC(nc.sbuf_tensor("b2_bf", [1, D], BF16))
            ps_s2x = EC(nc.psum_tensor("ps_s2x", [1, NWC], F32))
            ps_s2q = EC(nc.psum_tensor("ps_s2q", [1, NWC], F32))

            with ExitStack() as pc1:
                EC1 = pc1.enter_context
                wout_bf = EC1(nc.sbuf_tensor("wout_bf", [P, NK2, D], BF16))
                ygs = EC1(nc.sbuf_tensor("ygs", [P, NK2, TC], BF16))
                xh_bf = EC1(nc.sbuf_tensor("xh_bf", [P, ND, TC], BF16))
                ygts = EC1(nc.sbuf_tensor("ygts", [P, TC], BF16))
                ps_wo = [EC1(nc.psum_tensor(f"ps_wo{k}", [P, WOC], F32)) for k in range(2)]

                @block.gpsimd
                def _(g):
                    g.wait_ge(sem["sPH"], PH_C)
                    g.dma_start(out=wout_bf[:, :, :], in_=w_out_v).then_inc(sem["sCL"], 16)
                    g.dma_start(out=xh_bf[:, :, :], in_=xTh_v).then_inc(sem["sCL"], 16)
                    g.dma_start(out=b2_bf[:, :], in_=b2.ap()).then_inc(sem["sCL"], 16)
                    g.dma_start(out=b1c[:, :], in_=b1.ap()).then_inc(sem["sCL"], 16)

                @block.scalar
                def _(a):
                    a.wait_ge(sem["sPH"], PH_C)
                    for kc in range(NK2):
                        if kc >= 1:
                            a.wait_ge(sem["sYGC"], kc)
                        a.dma_start(out=ygts[:, :], in_=ygout_v[:, kc, TC:2 * TC]).then_inc(sem["sYGT"], 16)

                @block.sync
                def _(s):
                    s.wait_ge(sem["sPH"], PH_C)
                    for kc in range(NK2):
                        s.dma_start(out=ygs[:, kc, :], in_=ygout_v[:, kc, 0:TC]).then_inc(sYGL[kc % 2], 16)

                @block.vector
                def _(v):
                    v.wait_ge(sem["sPH"], PH_C)
                    v.wait_ge(sem["sWL"], 16 * 9)
                    for kc in range(NK2):
                        v.wait_ge(sYGL[kc % 2], 16 * (kc // 2 + 1))
                        v.wait_ge(sem["sYGT"], 16 * (kc + 1))
                        v.tensor_scalar(out=ygs[:, kc, :], in0=ygs[:, kc, :], scalar1=selc[:, 0:1],
                                        scalar2=None, op0=ALU.mult)
                        v.scalar_tensor_tensor(out=ygs[:, kc, :], in0=ygts[:, :],
                                               scalar=selc[:, 1:2], in1=ygs[:, kc, :],
                                               op0=ALU.mult, op1=ALU.add).then_inc(sem["sYGC"], 1)
                    v.wait_ge(sem["sCL"], 16 * 4)
                    for mt in range(ND):
                        for nch in range(NWOC):
                            wo = mt * NWOC + nch
                            v.wait_ge(sem["sWO"], wo + 1)
                            v.tensor_tensor(out=x_res[:, mt, nch * WOC:(nch + 1) * WOC],
                                            in0=ps_wo[wo % 2][:, :],
                                            in1=xh_bf[:, mt, nch * WOC:(nch + 1) * WOC],
                                            op=ALU.add).then_inc(sem["sXR"], 1)

                @block.tensor
                def _(t):
                    t.wait_ge(sem["sPH"], PH_C)
                    t.wait_ge(sem["sCL"], 16 * 4)
                    t.wait_ge(sem["sYGC"], NK2)
                    for mt in range(ND):
                        for nch in range(NWOC):
                            wo = mt * NWOC + nch
                            if wo >= 2:
                                t.wait_ge(sem["sXR"], wo - 1)
                            for kc in range(NK2):
                                t.matmul(ps_wo[wo % 2][:, :], wout_bf[:, kc, mt * P:(mt + 1) * P],
                                         ygs[:, kc, nch * WOC:(nch + 1) * WOC],
                                         start=(kc == 0), stop=(kc == NK2 - 1)).then_inc(
                                             sem["sWO"], 1 if kc == NK2 - 1 else 0)

            with ExitStack() as pc2:
                EC2 = pc2.enter_context
                xn2 = EC(nc.sbuf_tensor("xn2", [P, ND, TC], BF16))
                sq2 = EC2(nc.sbuf_tensor("sq2", [P, ND, TC], BF16))
                m2_bc = EC2(nc.sbuf_tensor("m2_bc", [P, TC], F32))
                r2_bc = EC2(nc.sbuf_tensor("r2_bc", [P, TC], F32))
                m2r = EC2(nc.sbuf_tensor("m2r", [1, 2, NWC], F32))
                e22r = EC2(nc.sbuf_tensor("e22r", [1, 2, NWC], F32))
                v2r = EC2(nc.sbuf_tensor("v2r", [1, NWC], F32))
                r2r = EC2(nc.sbuf_tensor("r2r", [1, 2, NWC], F32))
                tsub2 = EC2(nc.sbuf_tensor("tsub2", [P, TC], F32))

                @block.scalar
                def _(a):
                    a.wait_ge(sem["sXR"], ND * NWOC)
                    a.activation(out=sq2[:, :, :], in_=x_res[:, :, :], func=AF.Square).then_inc(sem["sSQ2"], 1)
                    for nch in range(NCCH):
                        pp = nch % 2
                        a.wait_ge(sem["sST2"], nch + 1)
                        if nch >= 2:
                            a.wait_ge(sem["sRW2"], 32 * (nch - 1))
                        a.activation(out=m2r[:, pp, :], in_=ps_s2x[:, :], func=AF.Identity, scale=1.0 / D)
                        a.activation(out=e22r[:, pp, :], in_=ps_s2q[:, :], func=AF.Identity, scale=1.0 / D).then_inc(sem["sEV2"], 1)
                        a.wait_ge(sem["sVAR2"], nch + 1)
                        a.activation(out=r2r[:, pp, :], in_=v2r[:, :], func=AF.Ln, bias=eps_c[0:1, :])
                        a.activation(out=r2r[:, pp, :], in_=r2r[:, pp, :], func=AF.Exp, scale=-0.5).then_inc(sem["sRS2"], 1)

                @block.vector
                def _(v):
                    for nch in range(NCCH):
                        pp = nch % 2
                        v.wait_ge(sem["sEV2"], nch + 1)
                        v.tensor_tensor(out=v2r[:, :], in0=m2r[:, pp, :], in1=m2r[:, pp, :], op=ALU.mult)
                        v.tensor_tensor(out=v2r[:, :], in0=e22r[:, pp, :], in1=v2r[:, :], op=ALU.subtract).then_inc(sem["sVAR2"], 1)
                    v.wait_ge(sem["sRB2"], 32)
                    for i in range(ND):
                        v.tensor_tensor(out=tsub2[:, :], in0=x_res[:, i, :], in1=m2_bc[:, :], op=ALU.subtract)
                        v.tensor_tensor(out=xn2[:, i, :], in0=tsub2[:, :], in1=r2_bc[:, :], op=ALU.mult).then_inc(sem["sXN"], 1)

                @block.tensor
                def _(t):
                    t.wait_ge(sem["sSQ2"], 1)
                    for nch in range(NCCH):
                        if nch >= 1:
                            t.wait_ge(sem["sEV2"], nch)
                        for kt in range(ND):
                            t.matmul(ps_s2x[:, :], ones_bf[:, :], x_res[:, kt, nch * NWC:(nch + 1) * NWC],
                                     start=(kt == 0), stop=(kt == ND - 1))
                        for kt in range(ND):
                            t.matmul(ps_s2q[:, :], ones_bf[:, :], sq2[:, kt, nch * NWC:(nch + 1) * NWC],
                                     start=(kt == 0), stop=(kt == ND - 1)).then_inc(
                                         sem["sST2"], 1 if kt == ND - 1 else 0)

                @block.sync
                def _(s):
                    for nch in range(NCCH):
                        pp = nch % 2
                        s.wait_ge(sem["sRS2"], nch + 1)
                        s.dma_start(out=row2_d.ap()[0:1, nch * NWC:(nch + 1) * NWC],
                                    in_=m2r[:, pp, :]).then_inc(sem["sRW2"], 16)
                        s.dma_start(out=row2_d.ap()[1:2, nch * NWC:(nch + 1) * NWC],
                                    in_=r2r[:, pp, :]).then_inc(sem["sRW2"], 16)
                    s.wait_ge(sem["sRW2"], 32 * NCCH)
                    s.dma_start(out=m2_bc[:, :], in_=row2_d.ap()[0:1, :].broadcast_to([P, TC])).then_inc(sem["sRB2"], 16)
                    s.dma_start(out=r2_bc[:, :], in_=row2_d.ap()[1:2, :].broadcast_to([P, TC])).then_inc(sem["sRB2"], 16)

            wq1 = EC(nc.sbuf_tensor("wq1", [P, ND, DFF], BF16))
            with ExitStack() as pc3:
                EC3 = pc3.enter_context
                wtile = EC3(nc.sbuf_tensor("wtile", [P, 2, DFF], F32))
                mp_s = EC3(nc.sbuf_tensor("mp_s", [P, DFF], F32))
                wqt = EC3(nc.sbuf_tensor("wqt", [P, DFF], BF16))

                def _emit_qload(eng, q, sem_ld):
                    if q >= 2:
                        if q - 2 < ND:
                            eng.wait_ge(sem["sQ1"], q - 1)
                        else:
                            eng.wait_ge(sem["sQ2"], q - ND - 1)
                    if q < ND:
                        eng.dma_start(out=wtile[:, q % 2, :], in_=w1t_v[:, q, :]).then_inc(sem_ld, 16)
                    else:
                        eng.dma_start(out=wtile[:, q % 2, 0:D], in_=w2t_v[:, q - ND, :]).then_inc(sem_ld, 16)

                def _wait_qload(v, q):
                    if q % 2 == 0:
                        v.wait_ge(sem["sQL"], 16 * (q // 2 + 1))
                    else:
                        v.wait_ge(sem["sQLo"], 16 * ((q - 1) // 2 + 1))

                @block.sync
                def _(s):
                    s.wait_ge(sem["sXN"], ND)
                    s.wait_ge(sem["sST2"], NCCH)
                    for q in range(0, ND + NF, 2):
                        _emit_qload(s, q, sem["sQL"])

                @block.vector
                def _(v):
                    v.wait_ge(sem["sTH"], 18)
                    for kt in range(ND):
                        _wait_qload(v, kt)
                        v.tensor_scalar(out=mp_s[:, :], in0=wtile[:, kt % 2, :], scalar1=thr_c[:, 0:1],
                                        scalar2=None, op0=ALU.is_ge)
                        v.scalar_tensor_tensor(out=mp_s[:, :], in0=wtile[:, kt % 2, :], scalar=nthr_c[:, 0:1],
                                               in1=mp_s[:, :], op0=ALU.is_gt, op1=ALU.add)
                        v.tensor_scalar(out=wq1[:, kt, :], in0=mp_s[:, :], scalar1=-1.0,
                                        scalar2=None, op0=ALU.add).then_inc(sem["sQ1"], 1)
                    v.wait_ge(sem["sCHB"], 32)
                    for kt in range(NF):
                        q = ND + kt
                        _wait_qload(v, q)
                        if kt >= 2:
                            v.wait_ge(sem["sQ2S"], 16 * (kt - 1))
                        v.scalar_tensor_tensor(out=mp_s[:, 0:D], in0=wtile[:, q % 2, 0:D],
                                               scalar=chl_c[:, 0:1], op0=ALU.subtract,
                                               in1=chl_c[:, 1:2].broadcast_to([P, D]), op1=ALU.is_gt)
                        v.scalar_tensor_tensor(out=mp_s[:, D:2 * D], in0=wtile[:, q % 2, 0:D],
                                               scalar=chl_c[:, 0:1], op0=ALU.add,
                                               in1=chl_c[:, 2:3].broadcast_to([P, D]), op1=ALU.is_gt)
                        v.scalar_tensor_tensor(out=wqt[:, (kt % 2) * D:(kt % 2) * D + D], in0=mp_s[:, 0:D],
                                               scalar=-1.0, op0=ALU.add,
                                               in1=mp_s[:, D:2 * D], op1=ALU.add).then_inc(sem["sQ2"], 1)

                @block.gpsimd
                def _(g):
                    g.wait_ge(sem["sXN"], ND)
                    g.wait_ge(sem["sST2"], NCCH)
                    for q in range(1, ND + 2, 2):
                        _emit_qload(g, q, sem["sQLo"])
                    for kt in range(NF):
                        q2 = ND + kt + 2
                        if q2 < ND + NF and q2 % 2 == 1:
                            _emit_qload(g, q2, sem["sQLo"])
                        g.wait_ge(sem["sQ2"], kt + 1)
                        g.dma_start(out=wq2_v[:, kt, :],
                                    in_=wqt[:, (kt % 2) * D:(kt % 2) * D + D]).then_inc(sem["sQ2S"], 16)
                    if dbg:
                        g.wait_ge(sem["sXR"], ND * NWOC)
                        g.dma_start(out=dbg_xres.ap().rearrange("(i p) t -> p i t", p=P),
                                    in_=x_res[:, :, :]).then_inc(sem["sOSD"], 16)
                        g.dma_start(out=dbg_xc.ap(), in_=xc_dram.ap()).then_inc(sem["sOSD"], 16)
                        g.dma_start(out=dbg_z.ap(), in_=z_dram.ap()).then_inc(sem["sOSD"], 16)
                        g.dma_start(out=dbg_xdbl.ap(), in_=cc1_out.ap()).then_inc(sem["sOSD"], 16)
                        g.dma_start(out=dbg_yg.ap(), in_=ygout.ap()).then_inc(sem["sOSD"], 16)
                        g.dma_start(out=dbg_thr.ap(), in_=gsc[:, 2:4]).then_inc(sem["sOSD"], 16)

            with ExitStack() as pc4:
                EC4 = pc4.enter_context
                hbuf = EC4(nc.sbuf_tensor("hbuf", [P, NF, CC], BF16))
                outst = EC4(nc.sbuf_tensor("outst", [P, 2, D], F32))
                wq2s = EC4(nc.sbuf_tensor("wq2s", [P, 2, D], BF16))
                ps_f1 = [EC4(nc.psum_tensor(f"ps_f1{k}", [P, CC], F32)) for k in range(2)]
                ps_f2 = [EC4(nc.psum_tensor(f"ps_f2{k}", [P, NW2], F32)) for k in range(2 * NF2N)]

                @block.gpsimd
                def _(g):
                    g.wait_ge(sem["sQ2S"], 16 * NF)
                    for cc in range(NCC):
                        for tp in range(NTP):
                            for kt in range(NF):
                                gidx = (cc * NTP + tp) * NF + kt
                                if gidx % 2 == 1:
                                    if gidx >= 2:
                                        g.wait_ge(sem["sF2K"], gidx - 1)
                                    g.dma_start(out=wq2s[:, gidx % 2, :],
                                                in_=wq2_v[:, kt, :]).then_inc(sW2L[1], 16)

                @block.sync
                def _(s):
                    s.wait_ge(sem["sQ2S"], 16 * NF)
                    for cc in range(NCC):
                        for tp in range(NTP):
                            for kt in range(NF):
                                gidx = (cc * NTP + tp) * NF + kt
                                if gidx % 2 == 0:
                                    if gidx >= 2:
                                        s.wait_ge(sem["sF2K"], gidx - 1)
                                    s.dma_start(out=wq2s[:, gidx % 2, :],
                                                in_=wq2_v[:, kt, :]).then_inc(sW2L[0], 16)
                            for tw in range(2):
                                ts_i = cc * NTS + tp * 2 + tw
                                s.wait_ge(sem["sOE"], NF2N * (ts_i + 1))
                                t0 = ts_i * P
                                s.dma_start(out=out.ap()[t0:t0 + P, :], in_=outst[:, ts_i % 2, :]).then_inc(sOS[ts_i % 2], 16)
                    nts_all = NCC * NTS
                    s.wait_ge(sOS[0], 16 * ((nts_all + 1) // 2))
                    s.wait_ge(sOS[1], 16 * (nts_all // 2))
                    if dbg:
                        s.wait_ge(sem["sOSD"], 16 * 10)

                @block.tensor
                def _(t):
                    t.wait_ge(sem["sXN"], ND)
                    t.wait_ge(sem["sQ1"], ND)
                    t.wait_ge(sem["sQ2"], NF)
                    for cc in range(NCC):
                        for ft in range(NF):
                            fg = cc * NF + ft
                            if fg >= 2:
                                t.wait_ge(sem["sHE"], fg - 1)
                            for kt in range(ND):
                                t.matmul(ps_f1[fg % 2][:, :], wq1[:, kt, ft * P:(ft + 1) * P],
                                         xn2[:, kt, cc * CC:(cc + 1) * CC],
                                         start=(kt == 0), stop=(kt == ND - 1)).then_inc(
                                             sem["sF1"], 1 if kt == ND - 1 else 0)
                        t.wait_ge(sem["sHE"], (cc + 1) * NF)
                        for tp in range(NTP):
                            if not (cc == 0 and tp == 0):
                                t.wait_ge(sem["sOE"], NF2N * (cc * NTS + 2 * tp))
                            for kt in range(NF):
                                gidx = (cc * NTP + tp) * NF + kt
                                t.wait_ge(sW2L[gidx % 2], 16 * (gidx // 2 + 1))
                                for tw in range(2):
                                    tcol = (tp * 2 + tw) * P
                                    for nw in range(NF2N):
                                        fk_i = t.matmul(ps_f2[tw * NF2N + nw][:, :],
                                                        hbuf[:, kt, tcol:tcol + P],
                                                        wq2s[:, gidx % 2, nw * NW2:(nw + 1) * NW2],
                                                        start=(kt == 0), stop=False)
                                fk_i.then_inc(sem["sF2K"], 1)
                            for tw in range(2):
                                ts_i = cc * NTS + tp * 2 + tw
                                tcol = (tp * 2 + tw) * P
                                for nw in range(NF2N):
                                    t.matmul(ps_f2[tw * NF2N + nw][:, :], onesr_bf[:, :],
                                             b2_bf[:, nw * NW2:(nw + 1) * NW2],
                                             start=False, stop=False)
                                    for q in range(NQS):
                                        dt_i = nw * NQS + q
                                        t.matmul(ps_f2[tw * NF2N + nw][:, :],
                                                 x_res[:, dt_i, cc * CC + tcol:cc * CC + tcol + P],
                                                 idshift[:, q, :],
                                                 start=False, stop=(q == NQS - 1)).then_inc(
                                                     sem["sF2"], 1 if q == NQS - 1 else 0)

                @block.scalar
                def _(a):
                    a.wait_ge(sem["sQ2S"], 16 * NF)
                    for cc in range(NCC):
                        if cc >= 1:
                            a.wait_ge(sem["sF2K"], cc * NTP * NF)
                        for ft in range(NF):
                            fg = cc * NF + ft
                            a.wait_ge(sem["sF1"], fg + 1)
                            a.activation(out=hbuf[:, ft, :], in_=ps_f1[fg % 2][:, :],
                                         func=AF.Relu, bias=b1c[:, ft:ft + 1]).then_inc(sem["sHE"], 1)
                        for tp in range(NTP):
                            for tw in range(2):
                                ts_i = cc * NTS + tp * 2 + tw
                                if ts_i >= 2:
                                    a.wait_ge(sOS[ts_i % 2], 16 * ((ts_i - 2) // 2 + 1))
                                for nw in range(NF2N):
                                    a.wait_ge(sem["sF2"], ts_i * NF2N + nw + 1)
                                    a.activation(out=outst[:, ts_i % 2, nw * NW2:(nw + 1) * NW2],
                                                 in_=ps_f2[tw * NF2N + nw][:, :],
                                                 func=AF.Identity).then_inc(sem["sOE"], 1)

    return nc


def _stage_inputs(ins, cfg):
    c = cfg
    x = np.ascontiguousarray(np.asarray(ins["L0_attn_x"], dtype=np.float32))
    W_in = np.asarray(ins["L0_attn_W_in"], dtype=np.float32)
    conv_w = np.asarray(ins["L0_attn_conv_w"], dtype=np.float32)
    conv_b = np.asarray(ins["L0_attn_conv_b"], dtype=np.float32)
    W_xp = np.asarray(ins["L0_attn_W_xproj"], dtype=np.float32)
    W_dt = np.asarray(ins["L0_attn_W_dt"], dtype=np.float32)
    b_dt = np.asarray(ins["L0_attn_b_dt"], dtype=np.float32)
    A_log = np.asarray(ins["L0_attn_A_log"], dtype=np.float32)
    Dp = np.asarray(ins["L0_attn_Dp"], dtype=np.float32)
    W_out = np.asarray(ins["L0_attn_W_out"], dtype=np.float32)
    W1 = np.asarray(ins["L0_ffn_W1"], dtype=np.float32)
    b1 = np.asarray(ins["L0_ffn_b1"], dtype=np.float32)
    W2 = np.asarray(ins["L0_ffn_W2"], dtype=np.float32)
    b2 = np.asarray(ins["L0_ffn_b2"], dtype=np.float32)

    DI2 = 2 * c.DI
    w2g = np.ascontiguousarray(
        W2.reshape(32, 32, 4, 32, 32).transpose(3, 2, 0, 1, 4).reshape(32 * 128, 1024))
    maps = []
    for core in range(8):
        b, k = core // 2, core % 2
        lo, hi = k * c.DI, (k + 1) * c.DI
        xT = np.ascontiguousarray(x[b].T)
        m = dict(
            xT=xT,
            xTh=np.ascontiguousarray(xT[:, k * c.TC:(k + 1) * c.TC]),
            w_in=np.ascontiguousarray(np.concatenate(
                [W_in[:, lo:hi], W_in[:, DI2 + lo:DI2 + hi]], axis=1)),
            conv_w=np.ascontiguousarray(conv_w[lo:hi, 0, :]),
            conv_b=np.ascontiguousarray(conv_b[lo:hi].reshape(c.NI, P).T),
            w_xp=np.ascontiguousarray(W_xp[lo:hi]),
            w_dt=np.ascontiguousarray(W_dt[0, lo:hi].reshape(c.NI, P).T),
            b_dt=np.ascontiguousarray(b_dt[lo:hi].reshape(c.NI, P).T),
            a_log=np.ascontiguousarray(A_log[lo:hi].reshape(c.NI, P, c.S).transpose(1, 0, 2).reshape(P, c.NI * c.S)),
            dp=np.ascontiguousarray(Dp[lo:hi].reshape(c.NI, P).T),
            w_out=np.ascontiguousarray(W_out),
            w1t=np.ascontiguousarray(W1.T),
            b1=np.ascontiguousarray(b1.reshape(c.NF, P).T),
            w2t=np.ascontiguousarray(W2.T),
            w2g=w2g,
            b2=np.ascontiguousarray(b2[None, :]),
            sel=np.ascontiguousarray(np.broadcast_to(
                np.array([1.0 - k, float(k)], dtype=np.float32), (128, 2))),
        )
        maps.append(m)
    return maps


def kernel(**inputs):
    cfg = Cfg()
    nc = build_program(cfg, dbg=False)
    in_maps = _stage_inputs(inputs, cfg)
    res = run_bass_kernel_spmd(nc, in_maps, core_ids=list(range(8)))
    B = inputs["L0_attn_x"].shape[0]
    outp = np.empty((B, cfg.T, cfg.D), dtype=np.float32)
    for core in range(8):
        b, k = core // 2, core % 2
        outp[b, k * cfg.TC:(k + 1) * cfg.TC, :] = res.results[core]["out"]
    return outp


# version 12
# speedup vs baseline: 1.1849x; 1.0318x over previous
import numpy as np
from contextlib import ExitStack

import concourse.bass as bass
import concourse.mybir as mybir
from concourse.bass_utils import run_bass_kernel_spmd

F32 = mybir.dt.float32
BF16 = mybir.dt.bfloat16
AF = mybir.ActivationFunctionType
ALU = mybir.AluOpType
P = 128


class Cfg:
    def __init__(self, T=4096, D=1024, DI=1024, S=16, DFF=4096, NCH=512, CC=512):
        self.T = T
        self.D = D
        self.DI = DI
        self.S = S
        self.DFF = DFF
        self.NCH = NCH
        self.CC = CC
        self.NJ = T // NCH
        self.ND = D // P
        self.NI = DI // P
        self.NWF = 2 * self.NI
        self.NF = DFF // P
        self.TH = T // 2
        self.TC = T // 2
        self.NPI = self.NI // 2
        self.NCC = self.TC // CC
        self.NTS = CC // P
        self.XD = 1 + 2 * S
        self.NW2 = min(512, D)
        self.NQS = self.NW2 // P
        self.NF2N = D // self.NW2


def build_program(cfg, dbg=False):
    c = cfg
    T, D, DI, S, DFF = c.T, c.D, c.DI, c.S, c.DFF
    NCH, NJ, ND, NI, NWF, NF = c.NCH, c.NJ, c.ND, c.NI, c.NWF, c.NF
    TH, TC, NPI, CC, NCC, NTS, XD = c.TH, c.TC, c.NPI, c.CC, c.NCC, c.NTS, c.XD
    NW2, NQS, NF2N = c.NW2, c.NQS, c.NF2N
    NK2 = 2 * NI
    NBC = 3
    NW = min(512, TH)
    NTHC = TH // NW
    NWC = min(512, TC)
    NCCH = TC // NWC
    WOC = min(512, TC)
    NWOC = TC // WOC
    NWIN_CS = 2 * DI // NCH
    NST = 2 * NPI * S
    NTP = NTS // 2
    PH_A, PH_C = 5, 10

    nc = bass.Bass("TRN2", target_bir_lowering=False, debug=False,
                   detect_race_conditions=False)

    xT = nc.dram_tensor("xT", [D, T], F32, kind="ExternalInput")
    xTh = nc.dram_tensor("xTh", [D, TC], F32, kind="ExternalInput")
    w_in = nc.dram_tensor("w_in", [D, 2 * DI], F32, kind="ExternalInput")
    conv_w = nc.dram_tensor("conv_w", [DI, 4], F32, kind="ExternalInput")
    conv_b = nc.dram_tensor("conv_b", [P, DI // P], F32, kind="ExternalInput")
    w_xp = nc.dram_tensor("w_xp", [DI, XD], F32, kind="ExternalInput")
    w_dt = nc.dram_tensor("w_dt", [P, DI // P], F32, kind="ExternalInput")
    b_dt = nc.dram_tensor("b_dt", [P, DI // P], F32, kind="ExternalInput")
    a_log = nc.dram_tensor("a_log", [P, (DI // P) * S], F32, kind="ExternalInput")
    dp = nc.dram_tensor("dp", [P, DI // P], F32, kind="ExternalInput")
    w_out = nc.dram_tensor("w_out", [2 * DI, D], F32, kind="ExternalInput")
    w1t = nc.dram_tensor("w1t", [D, DFF], F32, kind="ExternalInput")
    b1 = nc.dram_tensor("b1", [P, DFF // P], F32, kind="ExternalInput")
    w2t = nc.dram_tensor("w2t", [DFF, D], F32, kind="ExternalInput")
    b2 = nc.dram_tensor("b2", [1, D], F32, kind="ExternalInput")
    sel = nc.dram_tensor("sel", [P, 2], F32, kind="ExternalInput")
    w2g = nc.dram_tensor("w2g", [32 * P, 1024], F32, kind="ExternalInput")
    out = nc.dram_tensor("out", [TC, D], F32, kind="ExternalOutput")

    xc_dram = nc.dram_tensor("xc_dram", [DI, T], BF16, kind="Internal")
    z_dram = nc.dram_tensor("z_dram", [DI, T], BF16, kind="Internal")
    rowscr = nc.dram_tensor("rowscr", [1, T], F32, kind="Internal")
    cc1_in = nc.dram_tensor("cc1_in", [XD, T], F32, kind="Internal")
    cc1_out = nc.dram_tensor("cc1_out", [XD, T], F32, kind="Internal")
    xdbl_bf = nc.dram_tensor("xdbl_bf", [XD, T], BF16, kind="Internal")
    ygin = nc.dram_tensor("ygin", [2 * DI, TH], BF16, kind="Internal")
    ygoutA = nc.dram_tensor("ygoutA", [2 * DI, TH], BF16, kind="Internal")
    ygoutB = nc.dram_tensor("ygoutB", [2 * DI, TH], BF16, kind="Internal")
    wq2_dram = nc.dram_tensor("wq2_dram", [DFF, D], BF16, kind="Internal")
    g1d = nc.dram_tensor("g1d", [P, ND], F32, kind="Internal")
    g2D2 = nc.dram_tensor("g2D2", [P, 32], F32, kind="Internal")
    g2d3 = nc.dram_tensor("g2d3", [1, 4], F32, kind="Internal")
    chl_d = nc.dram_tensor("chl_d", [1, 3], F32, kind="Internal")
    thr_d = nc.dram_tensor("thr_d", [1, 2], F32, kind="Internal")
    row2_d = nc.dram_tensor("row2_d", [2, TC], F32, kind="Internal")

    if dbg:
        dbg_xc = nc.dram_tensor("dbg_xc", [DI, T], BF16, kind="ExternalOutput")
        dbg_z = nc.dram_tensor("dbg_z", [DI, T], BF16, kind="ExternalOutput")
        dbg_xdbl = nc.dram_tensor("dbg_xdbl", [XD, T], F32, kind="ExternalOutput")
        dbg_yg = nc.dram_tensor("dbg_yg", [2 * DI, T], BF16, kind="ExternalOutput")
        dbg_xres = nc.dram_tensor("dbg_xres", [D, TC], F32, kind="ExternalOutput")
        dbg_thr = nc.dram_tensor("dbg_thr", [1, 2], F32, kind="ExternalOutput")
        dbg_dl = nc.dram_tensor("dbg_dl", [P, 2 * TH], F32, kind="ExternalOutput")
        dbg_cr = nc.dram_tensor("dbg_cr", [P, NI * S * 2], F32, kind="ExternalOutput")
        dbg_xcp = nc.dram_tensor("dbg_xcp", [P, 2 * TH], F32, kind="ExternalOutput")
        dbg_zp = nc.dram_tensor("dbg_zp", [P, 2 * TH], F32, kind="ExternalOutput")

    w2g_v = w2g.ap().rearrange("(g p) t -> p g t", p=P)
    xT_v = xT.ap().rearrange("(i p) t -> p i t", p=P)
    xTh_v = xTh.ap().rearrange("(i p) t -> p i t", p=P)
    w_in_v = w_in.ap().rearrange("(i p) f -> p i f", p=P)
    w_xp_v = w_xp.ap().rearrange("(i p) f -> p i f", p=P)
    a_log_v = a_log.ap().rearrange("p (i s) -> p i s", i=DI // P)
    conv_w_v = conv_w.ap().rearrange("(i p) k -> p i k", p=P)
    xc_v = xc_dram.ap().rearrange("(i p) t -> p i t", p=P)
    z_v = z_dram.ap().rearrange("(i p) t -> p i t", p=P)
    w_out_v = w_out.ap().rearrange("(i p) f -> p i f", p=P)
    w1t_v = w1t.ap().rearrange("(i p) f -> p i f", p=P)
    w2t_v = w2t.ap().rearrange("(i p) f -> p i f", p=P)
    ygin_v = ygin.ap().rearrange("(hh i p) t -> p hh i t", p=P, hh=2)
    ygoutA_v = ygoutA.ap().rearrange("(i p) t -> p i t", p=P)
    ygoutB_v = ygoutB.ap().rearrange("(i p) t -> p i t", p=P)
    wq2_v = wq2_dram.ap().rearrange("(i p) f -> p i f", p=P)

    with ExitStack() as top:
        E = top.enter_context
        sem_names = [
            "sWL", "sIO", "sPRE", "sCSm", "sCSe", "sNA",
            "sXL0", "sXL1", "sSQ", "sST", "sEV1", "sVAR", "sRST", "sRW",
            "sRB0", "sRB1",
            "sWIN", "sEVC", "sZS0", "sZS1", "sCV", "sXCE", "sXCS0", "sXCS1",
            "sXD", "sXDE",
            "sGW0", "sGW1", "sGA", "sGR", "sTH", "sCD0", "sCD1", "sCC1",
            "sXBF", "sPH",
            "sBC0", "sBC1", "sBC2", "sSC", "sABC", "sZL0", "sZL1",
            "sA1", "sDL", "sDA", "sPR", "sYA", "sYG",
            "sSZ", "sYS0", "sYS1", "sAT", "sAT2", "sU",
            "sCL", "sWO", "sXR", "sSQ2", "sST2", "sEV2", "sVAR2", "sRS2",
            "sRW2", "sRB2", "sXN", "sQL", "sQ1", "sQ2", "sQ2S",
            "sF1", "sHE", "sW2L0", "sW2L1", "sF2K", "sF2", "sOE",
            "sOS0", "sOS1", "sOSD", "sYGL0", "sYGL1", "sYGC",
            "sG2L", "sG2A", "sG2S", "sG2E", "sG2D", "sG2X", "sG2C", "sCHB",
            "sYGT", "sQLo", "sW2L2",
        ]
        sem = {n: E(nc.semaphore(n)) for n in sem_names}
        sXL = [sem["sXL0"], sem["sXL1"]]
        sRB = [sem["sRB0"], sem["sRB1"]]
        sZSs = [sem["sZS0"], sem["sZS1"]]
        sXCSs = [sem["sXCS0"], sem["sXCS1"]]
        sGW = [sem["sGW0"], sem["sGW1"]]
        sCD = [sem["sCD0"], sem["sCD1"]]
        sBC = [sem["sBC0"], sem["sBC1"], sem["sBC2"]]
        sZL = [sem["sZL0"], sem["sZL1"]]
        sYSs = [sem["sYS0"], sem["sYS1"]]
        sW2L = [sem["sW2L0"], sem["sW2L1"], sem["sW2L2"]]
        sOS = [sem["sOS0"], sem["sOS1"]]
        sYGL = [sem["sYGL0"], sem["sYGL1"]]

        ones_bf = E(nc.sbuf_tensor("ones_bf", [P, 1], BF16))
        onesr_bf = E(nc.sbuf_tensor("onesr_bf", [1, P], BF16))
        ident_bf = E(nc.sbuf_tensor("ident_bf", [P, P], BF16))
        idshift = E(nc.sbuf_tensor("idshift", [P, NQS, NW2], BF16))
        iota_r = E(nc.sbuf_tensor("iota_r", [P, 1], F32))
        scr1 = E(nc.sbuf_tensor("scr1", [P, 1], F32))
        negA = E(nc.sbuf_tensor("negA", [P, NI, S], F32))
        carry = E(nc.sbuf_tensor("carry", [P, NI, S, 2], F32))
        wdt_c = E(nc.sbuf_tensor("wdt_c", [P, NI], F32))
        bdt_c = E(nc.sbuf_tensor("bdt_c", [P, NI], F32))
        dp_c = E(nc.sbuf_tensor("dp_c", [P, NI], F32))
        convb_c = E(nc.sbuf_tensor("convb_c", [P, NI], F32))
        thr_c = E(nc.sbuf_tensor("thr_c", [P, 2], F32))
        nthr_c = E(nc.sbuf_tensor("nthr_c", [P, 2], F32))
        chl_c = E(nc.sbuf_tensor("chl_c", [P, 3], F32))
        gsc = E(nc.sbuf_tensor("gsc", [1, 4], F32))
        eps_c = E(nc.sbuf_tensor("eps_c", [P, 1], F32))
        selc = E(nc.sbuf_tensor("selc", [P, 2], F32))

        block = E(nc.Block())

        with ExitStack() as pa:
            EA = pa.enter_context
            iota_c = EA(nc.sbuf_tensor("iota_c", [P, max(NW2, P)], F32))
            w_in_bf = EA(nc.sbuf_tensor("w_in_bf", [P, ND, 2 * DI], BF16))
            wxp_bf = EA(nc.sbuf_tensor("wxp_bf", [P, NI, XD], BF16))
            diag_bf = EA(nc.sbuf_tensor("diag_bf", [P, NI * 4, P], BF16))
            cwcol = EA(nc.sbuf_tensor("cwcol", [P, NI, 4], F32))
            xb = EA(nc.sbuf_tensor("xb", [P, 2, ND, NCH], BF16))
            sq = EA(nc.sbuf_tensor("sq", [P, ND, NCH], BF16))
            xc_pre = EA(nc.sbuf_tensor("xc_pre", [P, NI, 3 + T], BF16))
            z_stage = EA(nc.sbuf_tensor("z_stage", [P, 2, NI, NCH], BF16))
            xc_stage = EA(nc.sbuf_tensor("xc_stage", [P, 2, NI, NCH], BF16))
            rstd_bc = EA(nc.sbuf_tensor("rstd_bc", [P, 2, NCH], F32))
            m_row = EA(nc.sbuf_tensor("m_row", [1, 2, NCH], F32))
            e2_row = EA(nc.sbuf_tensor("e2_row", [1, 2, NCH], F32))
            var_row = EA(nc.sbuf_tensor("var_row", [1, 2, NCH], F32))
            rstd_row = EA(nc.sbuf_tensor("rstd_row", [1, 2, NCH], F32))
            negm_bf = EA(nc.sbuf_tensor("negm_bf", [1, 2, NCH], BF16))
            cs_row = EA(nc.sbuf_tensor("cs_row", [1, 2 * DI], F32))
            cs_bf = EA(nc.sbuf_tensor("cs_bf", [1, 2 * DI], BF16))
            xdbl_st = EA(nc.sbuf_tensor("xdbl_st", [XD, 2, NCH], F32))
            ps_sx = EA(nc.psum_tensor("ps_sx", [1, NCH], F32))
            ps_sq = EA(nc.psum_tensor("ps_sq", [1, NCH], F32))
            ps_win = [EA(nc.psum_tensor(f"ps_win{k}", [P, NCH], F32)) for k in range(2)]
            ps_cv = [EA(nc.psum_tensor(f"ps_cv{k}", [P, NCH], F32)) for k in range(2)]
            ps_xd = EA(nc.psum_tensor("ps_xd", [XD, NCH], F32))

            @block.gpsimd
            def _(g):
                g.dma_start(out=w_in_bf[:, :, :], in_=w_in_v).then_inc(sem["sWL"], 16)
                g.dma_start(out=wxp_bf[:, :, :], in_=w_xp_v).then_inc(sem["sWL"], 16)
                g.dma_start(out=convb_c[:, 0:NI], in_=conv_b.ap()).then_inc(sem["sWL"], 16)
                g.dma_start(out=wdt_c[:, 0:NI], in_=w_dt.ap()).then_inc(sem["sWL"], 16)
                g.dma_start(out=bdt_c[:, 0:NI], in_=b_dt.ap()).then_inc(sem["sWL"], 16)
                g.dma_start(out=dp_c[:, 0:NI], in_=dp.ap()).then_inc(sem["sWL"], 16)
                g.dma_start(out=cwcol[:, :, :], in_=conv_w_v).then_inc(sem["sWL"], 16)
                g.dma_start(out=negA[:, :, :], in_=a_log_v).then_inc(sem["sWL"], 16)
                g.dma_start(out=selc[:, :], in_=sel.ap()).then_inc(sem["sWL"], 16)
                g.iota(iota_c[:, :], pattern=[[1, iota_c.shape[1]]], base=0,
                       channel_multiplier=0, allow_small_or_imprecise_dtypes=True)
                g.iota(iota_r[:, :], pattern=[[0, 1]], base=0,
                       channel_multiplier=1, allow_small_or_imprecise_dtypes=True).then_inc(sem["sIO"], 1)
                for j in range(NJ):
                    if j >= 2:
                        g.wait_ge(sem["sWIN"], NWF * (j - 1))
                        g.wait_ge(sem["sSQ"], j - 1)
                    g.dma_start(out=xb[:, j % 2, :, :],
                                in_=xT_v[:, :, j * NCH:(j + 1) * NCH]).then_inc(sXL[j % 2], 16)
                    if j >= 1:
                        g.wait_ge(sem["sXCE"], NI * j)
                        g.dma_start(out=xc_v[:, :, (j - 1) * NCH:j * NCH],
                                    in_=xc_stage[:, (j - 1) % 2, :, :]).then_inc(sXCSs[(j - 1) % 2], 16)
                g.wait_ge(sem["sXCE"], NI * NJ)
                g.dma_start(out=xc_v[:, :, (NJ - 1) * NCH:NJ * NCH],
                            in_=xc_stage[:, (NJ - 1) % 2, :, :]).then_inc(sXCSs[(NJ - 1) % 2], 16)
                g.wait_ge(sCD[0], 16 * ((NJ + 1) // 2))
                g.wait_ge(sCD[1], 16 * (NJ // 2))
                g.collective_compute(
                    "AllReduce", ALU.add,
                    replica_groups=[[0, 1], [2, 3], [4, 5], [6, 7]],
                    ins=[cc1_in.ap()], outs=[cc1_out.ap()],
                ).then_inc(sem["sCC1"])
                g.wait_ge(sem["sCC1"], 1)
                g.dma_start(out=xdbl_bf.ap(), in_=cc1_out.ap(), max_dma_last_dim=8192).then_inc(sem["sXBF"], 16)

            @block.vector
            def _(v):
                v.wait_ge(sem["sIO"], 1)
                v.memset(ones_bf[:, :], 1.0)
                v.memset(onesr_bf[:, :], 1.0)
                v.tensor_tensor(out=ident_bf[:, :], in0=iota_c[:, 0:P],
                                in1=iota_r[:, :].broadcast_to([P, P]), op=ALU.is_equal)
                for q in range(NQS):
                    v.tensor_scalar(out=scr1[:, :], in0=iota_r[:, :], scalar1=float(P * q),
                                    scalar2=None, op0=ALU.add)
                    v.tensor_tensor(out=idshift[:, q, :], in0=iota_c[:, 0:NW2],
                                    in1=scr1[:, :].broadcast_to([P, NW2]), op=ALU.is_equal)
                for i in range(NI):
                    v.memset(xc_pre[:, i, 0:3], 0.0)
                v.memset(eps_c[:, :], 1e-5).then_inc(sem["sPRE"], 1)
                v.wait_ge(sem["sWL"], 16 * 9)
                for i in range(NI):
                    for k in range(4):
                        v.tensor_tensor(out=diag_bf[:, i * 4 + k, :], in0=ident_bf[:, :],
                                        in1=cwcol[:, i, k:k + 1].broadcast_to([P, P]),
                                        op=ALU.mult).then_inc(sem["sPRE"], 1)
                for j in range(NJ):
                    jj = j % 2
                    v.wait_ge(sem["sEV1"], j + 1)
                    v.tensor_tensor(out=var_row[:, jj, :], in0=m_row[:, jj, :], in1=m_row[:, jj, :], op=ALU.mult)
                    v.tensor_tensor(out=var_row[:, jj, :], in0=e2_row[:, jj, :], in1=var_row[:, jj, :], op=ALU.subtract).then_inc(sem["sVAR"], 1)
                    for ft in range(NWF):
                        ftg = j * NWF + ft
                        v.wait_ge(sem["sWIN"], ftg + 1)
                        v.wait_ge(sRB[j % 2], 16 * (j // 2 + 1))
                        if ft == NI and j >= 2:
                            v.wait_ge(sZSs[j % 2], 16 * ((j - 2) // 2 + 1))
                        dest = xc_pre[:, ft, 3 + j * NCH:3 + (j + 1) * NCH] if ft < NI \
                            else z_stage[:, jj, ft - NI, :]
                        v.tensor_tensor(out=dest, in0=ps_win[ftg % 2][:, :],
                                        in1=rstd_bc[:, jj, :], op=ALU.mult).then_inc(sem["sEVC"], 1)

            @block.scalar
            def _(a):
                for nchk in range(NWIN_CS):
                    a.wait_ge(sem["sCSm"], nchk + 1)
                    a.activation(out=cs_row[:, nchk * NCH:(nchk + 1) * NCH],
                                 in_=ps_win[nchk % 2][0:1, 0:NCH], func=AF.Identity).then_inc(sem["sCSe"], 1)
                a.activation(out=cs_bf[:, :], in_=cs_row[:, :], func=AF.Identity).then_inc(sem["sCSe"], 1)

            @block.tensor
            def _(t):
                t.wait_ge(sem["sWL"], 16 * 9)
                t.wait_ge(sem["sPRE"], 1)
                for nchk in range(NWIN_CS):
                    if nchk >= 2:
                        t.wait_ge(sem["sCSe"], nchk - 1)
                    for kt in range(ND):
                        t.matmul(ps_win[nchk % 2][0:1, 0:NCH], ones_bf[:, :],
                                 w_in_bf[:, kt, nchk * NCH:(nchk + 1) * NCH],
                                 start=(kt == 0), stop=(kt == ND - 1)).then_inc(
                                     sem["sCSm"], 1 if kt == ND - 1 else 0)
                for j in range(NJ):
                    jj = j % 2
                    t.wait_ge(sem["sSQ"], j + 1)
                    if j >= 1:
                        t.wait_ge(sem["sEV1"], j)
                    for kt in range(ND):
                        t.matmul(ps_sx[:, :], ones_bf[:, :], xb[:, jj, kt, :],
                                 start=(kt == 0), stop=(kt == ND - 1))
                    for kt in range(ND):
                        t.matmul(ps_sq[:, :], ones_bf[:, :], sq[:, kt, :],
                                 start=(kt == 0), stop=(kt == ND - 1)).then_inc(
                                     sem["sST"], 1 if kt == ND - 1 else 0)
                    t.wait_ge(sem["sRST"], j + 1)
                    if j == 0:
                        t.wait_ge(sem["sCSe"], NWIN_CS + 1)
                    for ft in range(NWF):
                        ftg = j * NWF + ft
                        if ftg >= 2:
                            t.wait_ge(sem["sEVC"], ftg - 1)
                        for kt in range(ND):
                            t.matmul(ps_win[ftg % 2][:, :], w_in_bf[:, kt, ft * P:(ft + 1) * P],
                                     xb[:, jj, kt, :], start=(kt == 0), stop=False)
                        t.matmul(ps_win[ftg % 2][:, :], cs_bf[0:1, ft * P:(ft + 1) * P],
                                 negm_bf[0:1, jj, :], start=False, stop=True).then_inc(sem["sWIN"], 1)
                    if j == 0:
                        t.wait_ge(sem["sPRE"], 1 + 4 * NI)
                    for i in range(NI):
                        ci = j * NI + i
                        t.wait_ge(sem["sEVC"], j * NWF + i + 1)
                        if ci >= 2:
                            t.wait_ge(sem["sXCE"], ci - 1)
                        for k in range(4):
                            t.matmul(ps_cv[ci % 2][:, :], diag_bf[:, i * 4 + k, :],
                                     xc_pre[:, i, j * NCH + k:j * NCH + k + NCH],
                                     start=(k == 0), stop=(k == 3)).then_inc(
                                         sem["sCV"], 1 if k == 3 else 0)
                    t.wait_ge(sem["sXCE"], NI * (j + 1))
                    if j >= 2:
                        t.wait_ge(sem["sXDE"], j - 1)
                    for i in range(NI):
                        t.matmul(ps_xd[:, :], wxp_bf[:, i, :], xc_stage[:, jj, i, :],
                                 start=(i == 0), stop=(i == NI - 1)).then_inc(
                                     sem["sXD"], 1 if i == NI - 1 else 0)

            @block.scalar
            def _(a):
                for j in range(NJ):
                    jj = j % 2
                    a.wait_ge(sXL[j % 2], 16 * (j // 2 + 1))
                    if j >= 1:
                        a.wait_ge(sem["sST"], j)
                    a.activation(out=sq[:, :, :], in_=xb[:, jj, :, :], func=AF.Square).then_inc(sem["sSQ"], 1)
                    a.wait_ge(sem["sST"], j + 1)
                    if j >= 2:
                        a.wait_ge(sem["sWIN"], NWF * (j - 1))
                    a.activation(out=m_row[:, jj, :], in_=ps_sx[:, :], func=AF.Identity, scale=1.0 / D)
                    a.activation(out=e2_row[:, jj, :], in_=ps_sq[:, :], func=AF.Identity, scale=1.0 / D).then_inc(sem["sEV1"], 1)
                    a.wait_ge(sem["sVAR"], j + 1)
                    if j == 0:
                        a.wait_ge(sem["sPRE"], 1)
                    if j >= 2:
                        a.wait_ge(sem["sRW"], 16 * (j - 1))
                    a.activation(out=rstd_row[:, jj, :], in_=var_row[:, jj, :], func=AF.Ln, bias=eps_c[0:1, :])
                    a.activation(out=rstd_row[:, jj, :], in_=rstd_row[:, jj, :], func=AF.Exp, scale=-0.5)
                    a.activation(out=negm_bf[:, jj, :], in_=m_row[:, jj, :], func=AF.Identity, scale=-1.0).then_inc(sem["sRST"], 1)
                    for i in range(NI):
                        ci = j * NI + i
                        a.wait_ge(sem["sCV"], ci + 1)
                        if j >= 2 and i == 0:
                            a.wait_ge(sXCSs[j % 2], 16 * ((j - 2) // 2 + 1))
                            a.wait_ge(sem["sXD"], j - 1)
                        a.activation(out=xc_stage[:, jj, i, :], in_=ps_cv[ci % 2][:, :],
                                     func=AF.Silu, bias=convb_c[:, i:i + 1]).then_inc(sem["sXCE"], 1)
                    a.wait_ge(sem["sXD"], j + 1)
                    if j >= 2:
                        a.wait_ge(sCD[j % 2], 16 * ((j - 2) // 2 + 1))
                    a.activation(out=xdbl_st[:, jj, :], in_=ps_xd[:, :], func=AF.Identity).then_inc(sem["sXDE"], 1)

            @block.sync
            def _(s):
                for j in range(NJ):
                    jj = j % 2
                    s.wait_ge(sem["sRST"], j + 1)
                    s.dma_start(out=rowscr.ap()[:, j * NCH:(j + 1) * NCH],
                                in_=rstd_row[:, jj, :]).then_inc(sem["sRW"], 16)
                    s.wait_ge(sem["sRW"], 16 * (j + 1))
                    if j >= 2:
                        s.wait_ge(sem["sEVC"], NWF * (j - 1))
                    s.dma_start(out=rstd_bc[:, jj, :],
                                in_=rowscr.ap()[:, j * NCH:(j + 1) * NCH].broadcast_to([P, NCH])).then_inc(sRB[j % 2], 16)
                    s.wait_ge(sem["sEVC"], NWF * (j + 1))
                    s.dma_start(out=z_v[:, :, j * NCH:(j + 1) * NCH],
                                in_=z_stage[:, jj, :, :]).then_inc(sZSs[j % 2], 16)
                    s.wait_ge(sem["sXDE"], j + 1)
                    s.dma_start(out=cc1_in.ap()[:, j * NCH:(j + 1) * NCH],
                                in_=xdbl_st[:, jj, :]).then_inc(sCD[j % 2], 16)

        GL = 1024
        with ExitStack() as pa2:
            EA2 = pa2.enter_context
            gtile = EA2(nc.sbuf_tensor("gtile", [P, 2, DFF], F32))
            gabs = EA2(nc.sbuf_tensor("gabs", [P, DFF], F32))
            gcol1 = EA2(nc.sbuf_tensor("gcol1", [P, ND], F32))
            grow = EA2(nc.sbuf_tensor("grow", [1, P * ND], F32))
            g2wg = EA2(nc.sbuf_tensor("g2wg", [P, 2, GL], F32))
            g2wa = EA2(nc.sbuf_tensor("g2wa", [P, 2, GL], F32))
            g2scr = EA2(nc.sbuf_tensor("g2scr", [P, 2, GL], F32))
            g2ones = EA2(nc.sbuf_tensor("g2ones", [P, GL], F32))
            g2s1 = EA2(nc.sbuf_tensor("g2s1", [P, 32], F32))
            g2st2 = EA2(nc.sbuf_tensor("g2st2", [4, GL], F32))
            g2scr2 = EA2(nc.sbuf_tensor("g2scr2", [4, GL], F32))
            g2dum = EA2(nc.sbuf_tensor("g2dum", [4, GL], F32))
            g2s2r = EA2(nc.sbuf_tensor("g2s2r", [4, 1], F32))
            g2st3 = EA2(nc.sbuf_tensor("g2st3", [1, 4], F32))
            g2scr3 = EA2(nc.sbuf_tensor("g2scr3", [1, 4], F32))
            g2sc = EA2(nc.sbuf_tensor("g2sc", [1, 12], F32))

            @block.sync
            def _(s):
                s.wait_ge(sem["sWIN"], NWF * NJ)
                s.wait_ge(sem["sXD"], NJ)
                s.wait_ge(sXCSs[0], 16 * ((NJ + 1) // 2))
                s.wait_ge(sXCSs[1], 16 * (NJ // 2))
                s.wait_ge(sCD[0], 16 * ((NJ + 1) // 2))
                s.wait_ge(sCD[1], 16 * (NJ // 2))
                for kt in range(ND):
                    if kt >= 2:
                        s.wait_ge(sem["sGA"], kt - 1)
                    s.dma_start(out=gtile[:, kt % 2, :], in_=w1t_v[:, kt, :]).then_inc(sGW[kt % 2], 16)
                s.wait_ge(sem["sGA"], ND)
                s.dma_start(out=g1d.ap(), in_=gcol1[:, :]).then_inc(sem["sGR"], 16)
                s.wait_ge(sem["sGR"], 16)
                s.dma_start(out=grow[:, 0:P * ND],
                            in_=g1d.ap().rearrange("p n -> (p n)")[None, :]).then_inc(sem["sGR"], 16)
                s.wait_ge(sem["sTH"], 1)
                s.dma_start(out=thr_d.ap()[:, 0:1], in_=gsc[:, 2:3]).then_inc(sem["sGR"], 16)
                s.wait_ge(sem["sGR"], 48)
                s.dma_start(out=thr_c[:, 0:1],
                            in_=thr_d.ap()[:, 0:1].broadcast_to([P, 1])).then_inc(sem["sTH"], 16)
                for g in range(32):
                    if g >= 2:
                        s.wait_ge(sem["sG2A"], g - 1)
                    s.dma_start(out=g2wg[:, g % 2, :], in_=w2g_v[:, g, :]).then_inc(sem["sG2L"], 16)
                s.wait_ge(sem["sG2E"], 34)
                s.dma_start(out=g2D2.ap(), in_=g2s1[:, :]).then_inc(sem["sG2D"], 16)
                s.wait_ge(sem["sG2D"], 16)
                s.dma_start(out=g2st2[:, :],
                            in_=g2D2.ap().rearrange("(j r) c -> j (r c)", j=4)).then_inc(sem["sG2D"], 16)
                s.wait_ge(sem["sG2X"], 1)
                s.dma_start(out=g2d3.ap().rearrange("o f -> f o"), in_=g2s2r[:, :]).then_inc(sem["sG2D"], 16)
                s.wait_ge(sem["sG2D"], 48)
                s.dma_start(out=g2st3[:, :], in_=g2d3.ap()).then_inc(sem["sG2D"], 16)
                s.wait_ge(sem["sG2C"], 1)
                s.dma_start(out=chl_d.ap(), in_=g2sc[:, 9:12]).then_inc(sem["sCHB"], 16)
                s.wait_ge(sem["sCHB"], 16)
                s.dma_start(out=chl_c[:, :], in_=chl_d.ap().broadcast_to([P, 3])).then_inc(sem["sCHB"], 16)

            @block.scalar
            def _(a):
                for kt in range(ND):
                    a.wait_ge(sGW[kt % 2], 16 * (kt // 2 + 1))
                    act = a.activation(out=gabs[:, :], in_=gtile[:, kt % 2, :], func=AF.Abs,
                                       accum_out=gcol1[:, kt:kt + 1])
                    if kt < ND - 1:
                        act.then_inc(sem["sGA"], 1)
                a.activation(out=gabs[:, :], in_=gtile[:, (ND - 1) % 2, :], func=AF.Abs,
                             accum_out=gcol1[:, ND - 1:ND]).then_inc(sem["sGA"], 1)
                for g in range(32):
                    a.wait_ge(sem["sG2L"], 16 * (g + 1))
                    if g >= 2:
                        a.wait_ge(sem["sG2S"], g - 1)
                    a.activation(out=g2wa[:, g % 2, :], in_=g2wg[:, g % 2, :],
                                 func=AF.Abs).then_inc(sem["sG2A"], 1)
                a.wait_ge(sem["sWL"], 16 * 9)
                a.activation(out=negA[:, :, :], in_=negA[:, :, :], func=AF.Exp).then_inc(sem["sNA"], 1)
                a.wait_ge(sem["sTH"], 17)
                a.activation(out=nthr_c[:, 0:1], in_=thr_c[:, 0:1], func=AF.Identity, scale=-1.0).then_inc(sem["sTH"], 1)

            @block.vector
            def _(v):
                v.memset(g2ones[:, :], 1.0)
                v.wait_ge(sem["sGR"], 32)
                v.reduce_sum(gsc[:, 0:1], grow[:, 0:P * ND], axis=mybir.AxisListType.X)
                v.reduce_sum(gsc[:, 0:1], grow[:, 0:P * ND], axis=mybir.AxisListType.X)
                v.tensor_scalar(out=gsc[:, 2:3], in0=gsc[:, 0:1], scalar1=0.5 / (D * DFF),
                                scalar2=0.5e-5, op0=ALU.mult, op1=ALU.add)
                v.reduce_sum(gsc[:, 1:2], grow[:, 0:P * ND], axis=mybir.AxisListType.X)
                v.tensor_scalar(out=gsc[:, 2:3], in0=gsc[:, 0:1], scalar1=0.5 / (D * DFF),
                                scalar2=0.5e-5, op0=ALU.mult, op1=ALU.add).then_inc(sem["sTH"], 1)
                for g in range(32):
                    v.wait_ge(sem["sG2A"], g + 1)
                    v.tensor_tensor_scan(out=g2scr[:, g % 2, :], data0=g2ones[:, :],
                                         data1=g2wa[:, g % 2, :], initial=0.0,
                                         op0=ALU.mult, op1=ALU.add).then_inc(sem["sG2S"], 1)
                    if g >= 1:
                        v.tensor_copy(out=g2s1[:, g - 1:g],
                                      in_=g2scr[:, (g - 1) % 2, GL - 1:GL]).then_inc(sem["sG2E"], 1)
                v.tensor_copy(out=g2dum[0:4, :], in_=g2scr[0:4, 1, :])
                v.tensor_copy(out=g2s1[:, 31:32], in_=g2scr[:, 1, GL - 1:GL]).then_inc(sem["sG2E"], 1)
                v.tensor_copy(out=g2dum[0:4, :], in_=g2scr[0:4, 0, :])
                v.tensor_copy(out=g2s1[:, 30:31], in_=g2scr[:, 0, GL - 1:GL]).then_inc(sem["sG2E"], 1)
                v.tensor_copy(out=g2s1[:, 31:32], in_=g2scr[:, 1, GL - 1:GL]).then_inc(sem["sG2E"], 1)
                v.wait_ge(sem["sG2D"], 32)
                v.tensor_tensor_scan(out=g2scr2[:, :], data0=g2ones[0:4, :],
                                     data1=g2st2[:, :], initial=0.0,
                                     op0=ALU.mult, op1=ALU.add)
                v.tensor_copy(out=g2dum[:, :], in_=g2scr2[:, :])
                v.tensor_copy(out=g2s2r[:, :], in_=g2scr2[:, GL - 1:GL])
                v.tensor_copy(out=g2dum[:, :], in_=g2scr2[:, :])
                v.tensor_copy(out=g2s2r[:, :], in_=g2scr2[:, GL - 1:GL]).then_inc(sem["sG2X"], 1)
                v.wait_ge(sem["sG2D"], 64)
                v.tensor_tensor_scan(out=g2scr3[:, :], data0=g2ones[0:1, 0:4],
                                     data1=g2st3[:, :], initial=0.0,
                                     op0=ALU.mult, op1=ALU.add)

                def spacer():
                    v.tensor_copy(out=g2dum[0:1, :], in_=g2scr2[0:1, :])
                spacer()
                v.tensor_copy(out=g2sc[:, 0:1], in_=g2scr3[:, 3:4])
                spacer()
                v.tensor_copy(out=g2sc[:, 0:1], in_=g2scr3[:, 3:4])
                spacer()
                v.tensor_scalar(out=g2sc[:, 1:2], in0=g2sc[:, 0:1], scalar1=float(2.0 ** -22),
                                scalar2=None, op0=ALU.mult)
                spacer()
                v.tensor_scalar(out=g2sc[:, 2:3], in0=g2sc[:, 1:2], scalar1=float(np.float32(1e-5)),
                                scalar2=None, op0=ALU.add)
                spacer()
                v.tensor_scalar(out=g2sc[:, 3:4], in0=g2sc[:, 2:3], scalar1=0.5,
                                scalar2=None, op0=ALU.mult)
                spacer()
                v.tensor_scalar(out=g2sc[:, 5:6], in0=g2ones[0:1, 0:1], scalar1=float(2.0 ** -25),
                                scalar2=None, op0=ALU.add)
                spacer()
                v.tensor_scalar(out=g2sc[:, 8:9], in0=g2sc[:, 5:6], scalar1=-1.0,
                                scalar2=None, op0=ALU.add)
                spacer()
                v.tensor_scalar(out=g2sc[:, 8:9], in0=g2sc[:, 8:9], scalar1=-1.0,
                                scalar2=float(2.0 ** -25), op0=ALU.mult, op1=ALU.add)
                spacer()
                v.tensor_tensor(out=g2sc[:, 4:5], in0=g2sc[:, 2:3], in1=g2sc[:, 8:9], op=ALU.mult)
                spacer()
                v.tensor_tensor(out=g2sc[:, 9:10], in0=g2sc[:, 3:4], in1=g2sc[:, 4:5], op=ALU.add)
                spacer()
                v.tensor_tensor(out=g2sc[:, 6:7], in0=g2sc[:, 3:4], in1=g2sc[:, 9:10], op=ALU.subtract)
                spacer()
                v.tensor_tensor(out=g2sc[:, 10:11], in0=g2sc[:, 6:7], in1=g2sc[:, 4:5], op=ALU.add)
                spacer()
                v.tensor_scalar(out=g2sc[:, 11:12], in0=g2sc[:, 10:11], scalar1=-1.0,
                                scalar2=None, op0=ALU.mult)
                spacer()
                v.tensor_tensor(out=g2sc[:, 9:10], in0=g2sc[:, 3:4], in1=g2sc[:, 4:5], op=ALU.add)
                v.tensor_tensor(out=g2sc[:, 10:11], in0=g2sc[:, 6:7], in1=g2sc[:, 4:5], op=ALU.add)
                v.tensor_scalar(out=g2sc[:, 11:12], in0=g2sc[:, 10:11], scalar1=-1.0,
                                scalar2=None, op0=ALU.mult).then_inc(sem["sG2C"], 1)
                v.wait_ge(sem["sNA"], 1)
                v.tensor_scalar(out=negA[:, :, :], in0=negA[:, :, :], scalar1=-1.0,
                                scalar2=None, op0=ALU.mult).then_inc(sem["sNA"], 1)

            @block.sync
            def _(s):
                s.wait_ge(sZSs[0], 16 * ((NJ + 1) // 2))
                s.wait_ge(sZSs[1], 16 * (NJ // 2))
                s.wait_ge(sem["sTH"], 18)
                s.sem_inc(sem["sPH"], 1)
            @block.gpsimd
            def _(g):
                g.wait_ge(sem["sXBF"], 16)
                g.wait_ge(sXCSs[0], 16 * ((NJ + 1) // 2))
                g.wait_ge(sXCSs[1], 16 * (NJ // 2))
                g.sem_inc(sem["sPH"], 1)
            @block.vector
            def _(v):
                v.wait_ge(sem["sNA"], 2)
                v.wait_ge(sem["sEVC"], NWF * NJ)
                v.sem_inc(sem["sPH"], 1)
            @block.scalar
            def _(a):
                a.wait_ge(sem["sXDE"], NJ)
                a.wait_ge(sem["sGA"], ND)
                a.wait_ge(sem["sTH"], 18)
                a.sem_inc(sem["sPH"], 1)
            @block.tensor
            def _(t):
                t.wait_ge(sem["sXD"], NJ)
                t.sem_inc(sem["sPH"], 1)

        with ExitStack() as pb:
            EB = pb.enter_context
            b_bc = EB(nc.sbuf_tensor("b_bc", [P, NBC, TH], BF16))
            c_bc = EB(nc.sbuf_tensor("c_bc", [P, NBC, TH], BF16))
            a_bc = EB(nc.sbuf_tensor("a_bc", [P, TH], F32))
            delta = EB(nc.sbuf_tensor("delta", [P, 2, 2, TH], F32))
            arg_s = EB(nc.sbuf_tensor("arg_s", [P, 2, TH], F32))
            t_s = EB(nc.sbuf_tensor("t_s", [P, 2, TH], F32))
            dxc = EB(nc.sbuf_tensor("dxc", [P, 2, TH], BF16))
            dA = EB(nc.sbuf_tensor("dA", [P, 2, TH], F32))
            u_s = EB(nc.sbuf_tensor("u_s", [P, 2, TH], BF16))
            h_s = EB(nc.sbuf_tensor("h_s", [P, 2, 2, TH], BF16))
            xc_pair = EB(nc.sbuf_tensor("xc_pair", [P, 2, 2, TH], BF16))
            z_pair = EB(nc.sbuf_tensor("z_pair", [P, 2, 2, TH], BF16))
            ybuf = EB(nc.sbuf_tensor("ybuf", [P, TH], F32))
            yg_s = EB(nc.sbuf_tensor("yg_s", [P, 2, 2, TH], BF16))
            ps_y = [EB(nc.psum_tensor(f"ps_y{k}", [P, TH], F32)) for k in range(2)]

            def step_of(h, pi, s):
                return (h * NPI + pi) * S + s

            @block.sync
            def _(s):
                s.wait_ge(sem["sPH"], PH_A)
                for h in range(2):
                    t0 = h * TH
                    if h == 1:
                        s.wait_ge(sem["sA1"], NPI)
                    s.dma_start(out=a_bc[:, :],
                                in_=cc1_out.ap()[0:1, t0:t0 + TH].broadcast_to([P, TH])).then_inc(sem["sABC"], 16)
                    for pi in range(NPI):
                        gp = h * NPI + pi
                        if gp >= 2:
                            s.wait_ge(sem["sYG"], 2 * (gp - 1))
                        if dbg and gp == 4:
                            s.wait_ge(sem["sOSD"], 48)
                        s.dma_start(out=xc_pair[:, gp % 2, :, :],
                                    in_=xc_v[:, 2 * pi:2 * pi + 2, t0:t0 + TH]).then_inc(sZL[gp % 2], 16)
                        s.dma_start(out=z_pair[:, gp % 2, :, :],
                                    in_=z_v[:, 2 * pi:2 * pi + 2, t0:t0 + TH]).then_inc(sZL[gp % 2], 16)
                        for st in range(S):
                            stp = step_of(h, pi, st)
                            if stp >= NBC:
                                s.wait_ge(sem["sPR"], stp - NBC + 1)
                            s.dma_start(out=b_bc[:, stp % NBC, :],
                                        in_=xdbl_bf.ap()[1 + st:2 + st, t0:t0 + TH].broadcast_to([P, TH])).then_inc(sBC[stp % NBC], 16)
                            s.dma_start(out=c_bc[:, stp % NBC, :],
                                        in_=xdbl_bf.ap()[1 + S + st:2 + S + st, t0:t0 + TH].broadcast_to([P, TH])).then_inc(sBC[stp % NBC], 16)

            @block.scalar
            def _(a):
                a.wait_ge(sem["sPH"], PH_A)
                for h in range(2):
                    for pi in range(NPI):
                        gp = h * NPI + pi
                        a.wait_ge(sem["sABC"], 16 * (h + 1))
                        if gp >= 1:
                            a.wait_ge(sem["sDL"], gp)
                        for iw in range(2):
                            i = 2 * pi + iw
                            a.activation(out=arg_s[:, iw, :], in_=a_bc[:, :], func=AF.Identity,
                                         scale=wdt_c[:, i:i + 1], bias=bdt_c[:, i:i + 1])
                            a.activation(out=t_s[:, iw, :], in_=arg_s[:, iw, :], func=AF.Abs)
                            a.activation(out=t_s[:, iw, :], in_=t_s[:, iw, :], func=AF.Exp, scale=-1.0)
                            ln_i = a.activation(out=t_s[:, iw, :], in_=t_s[:, iw, :], func=AF.Ln, bias=1.0)
                        ln_i.then_inc(sem["sA1"], 1)
                        a.wait_ge(sZL[gp % 2], 32 * (gp // 2 + 1))
                        a.activation(out=z_pair[:, gp % 2, :, :], in_=z_pair[:, gp % 2, :, :],
                                     func=AF.Silu).then_inc(sem["sSZ"], 1)
                        a.wait_ge(sem["sDL"], gp + 1)
                        for st in range(S):
                            stp = step_of(h, pi, st)
                            for iw in range(2):
                                sub = 2 * stp + iw
                                if sub >= 2:
                                    a.wait_ge(sem["sSC"], sub - 1)
                                da_i = a.activation(out=dA[:, sub % 2, :], in_=delta[:, gp % 2, iw, :],
                                                    func=AF.Exp, scale=negA[:, 2 * pi + iw, st:st + 1])
                            da_i.then_inc(sem["sDA"], 1)

            @block.vector
            def _(v):
                v.wait_ge(sem["sPH"], PH_A)
                for h in range(2):
                    for pi in range(NPI):
                        gp = h * NPI + pi
                        v.wait_ge(sem["sA1"], gp + 1)
                        if gp >= 2:
                            v.wait_ge(sem["sDA"], S * (gp - 1))
                        if dbg and gp == 4:
                            v.wait_ge(sem["sOSD"], 48)
                        for iw in range(2):
                            v.scalar_tensor_tensor(out=delta[:, gp % 2, iw, :], in0=arg_s[:, iw, :],
                                                   scalar=0.0, in1=t_s[:, iw, :], op0=ALU.max, op1=ALU.add)
                        v.wait_ge(sZL[gp % 2], 32 * (gp // 2 + 1))
                        for iw in range(2):
                            dxc_i = v.tensor_tensor(out=dxc[:, iw, :], in0=delta[:, gp % 2, iw, :],
                                                    in1=xc_pair[:, gp % 2, iw, :], op=ALU.mult)
                        dxc_i.then_inc(sem["sDL"], 1)
                        for st in range(S):
                            stp = step_of(h, pi, st)
                            v.wait_ge(sBC[stp % NBC], 32 * (stp // NBC + 1))
                            v.wait_ge(sem["sDA"], stp + 1)
                            if stp >= 2:
                                v.wait_ge(sem["sYA"], stp - 1)
                            for iw in range(2):
                                i = 2 * pi + iw
                                v.tensor_tensor(out=u_s[:, iw, :], in0=dxc[:, iw, :],
                                                in1=b_bc[:, stp % NBC, :], op=ALU.mult)
                                v.tensor_tensor_scan(out=h_s[:, stp % 2, iw, :],
                                                     data0=dA[:, (2 * stp + iw) % 2, :], data1=u_s[:, iw, :],
                                                     initial=(0.0 if h == 0 else carry[:, i, st, 1:2]),
                                                     op0=ALU.mult, op1=ALU.add).then_inc(sem["sSC"], 1)
                            for iw in range(2):
                                i = 2 * pi + iw
                                if h == 0:
                                    v.tensor_copy(out=carry[:, i, st, :],
                                                  in_=h_s[:, stp % 2, iw, TH - 2:TH])
                                cm_i = v.tensor_tensor(out=h_s[:, stp % 2, iw, :], in0=h_s[:, stp % 2, iw, :],
                                                       in1=c_bc[:, stp % NBC, :], op=ALU.mult)
                            cm_i.then_inc(sem["sPR"], 1)
                        v.wait_ge(sem["sYA"], S * (gp + 1))
                        v.wait_ge(sem["sSZ"], gp + 1)
                        if gp >= 2:
                            v.wait_ge(sYSs[gp % 2], 16 * ((gp - 2) // 2 + 1))
                        for iw in range(2):
                            i = 2 * pi + iw
                            v.scalar_tensor_tensor(out=ybuf[:, :], in0=xc_pair[:, gp % 2, iw, :],
                                                   scalar=dp_c[:, i:i + 1], in1=ps_y[iw][:, :],
                                                   op0=ALU.mult, op1=ALU.add)
                            v.tensor_tensor(out=yg_s[:, gp % 2, iw, :], in0=ybuf[:, :],
                                            in1=z_pair[:, gp % 2, iw, :], op=ALU.mult).then_inc(sem["sYG"], 1)

            @block.tensor
            def _(t):
                t.wait_ge(sem["sPH"], PH_A)
                for h in range(2):
                    for pi in range(NPI):
                        gp = h * NPI + pi
                        for st in range(S):
                            stp = step_of(h, pi, st)
                            t.wait_ge(sem["sPR"], stp + 1)
                            if st == 0:
                                t.wait_ge(sem["sYG"], 2 * gp)
                            for iw in range(2):
                                for nw in range(NTHC):
                                    ya_i = t.matmul(ps_y[iw][:, nw * NW:(nw + 1) * NW], ident_bf[:, :],
                                                    h_s[:, stp % 2, iw, nw * NW:(nw + 1) * NW],
                                                    start=(st == 0), stop=(st == S - 1))
                            ya_i.then_inc(sem["sYA"], 1)

            @block.gpsimd
            def _(g):
                g.wait_ge(sem["sPH"], PH_A)
                if dbg:
                    g.wait_ge(sem["sDL"], 3)
                    g.dma_start(out=dbg_dl.ap(), in_=delta[:, 0, :, :].rearrange("p a t -> p (a t)")).then_inc(sem["sOSD"], 16)
                    g.wait_ge(sem["sSZ"], 3)
                    g.dma_start(out=dbg_xcp.ap(), in_=xc_pair[:, 0, :, :].rearrange("p a t -> p (a t)")).then_inc(sem["sOSD"], 16)
                    g.dma_start(out=dbg_zp.ap(), in_=z_pair[:, 0, :, :].rearrange("p a t -> p (a t)")).then_inc(sem["sOSD"], 16)
                for h in range(2):
                    if dbg and h == 1:
                        g.dma_start(out=dbg_cr.ap(), in_=carry[:, :, :, :].rearrange("p i s c -> p (i s c)")).then_inc(sem["sOSD"], 16)
                    for pi in range(NPI):
                        gp = h * NPI + pi
                        g.wait_ge(sem["sYG"], 2 * (gp + 1))
                        g.dma_start(out=ygin_v[:, h, 2 * pi:2 * pi + 2, :],
                                    in_=yg_s[:, gp % 2, :, :]).then_inc(sYSs[gp % 2], 16)
                    g.wait_ge(sYSs[0], 16 * (2 * (h + 1)))
                    g.wait_ge(sYSs[1], 16 * (2 * (h + 1)))
                    g.collective_compute(
                        "AllGather", ALU.bypass,
                        replica_groups=[[0, 1], [2, 3], [4, 5], [6, 7]],
                        ins=[ygin.ap()[h * DI:(h + 1) * DI, :]],
                        outs=[(ygoutA if h == 0 else ygoutB).ap()],
                    ).then_inc(sem["sAT" if h == 0 else "sAT2"])

        @block.sync
        def _(s):
            for q in range(NBC):
                nq = len([x for x in range(NST) if x % NBC == q])
                s.wait_ge(sBC[q], 32 * nq)
            s.sem_inc(sem["sPH"], 1)
        @block.gpsimd
        def _(g):
            g.sem_inc(sem["sPH"], 1)
        @block.vector
        def _(v):
            v.wait_ge(sem["sYG"], 2 * 2 * NPI)
            v.sem_inc(sem["sPH"], 1)
        @block.scalar
        def _(a):
            a.wait_ge(sem["sDA"], NST)
            a.sem_inc(sem["sPH"], 1)
        @block.tensor
        def _(t):
            t.wait_ge(sem["sYA"], NST)
            t.sem_inc(sem["sPH"], 1)

        with ExitStack() as pc:
            EC = pc.enter_context
            x_res = EC(nc.sbuf_tensor("x_res", [P, ND, TC], BF16))
            b1c = EC(nc.sbuf_tensor("b1c", [P, NF], F32))
            b2_bf = EC(nc.sbuf_tensor("b2_bf", [1, D], BF16))
            ps_s2x = EC(nc.psum_tensor("ps_s2x", [1, NWC], F32))
            ps_s2q = EC(nc.psum_tensor("ps_s2q", [1, NWC], F32))

            with ExitStack() as pc1:
                EC1 = pc1.enter_context
                wout_bf = EC1(nc.sbuf_tensor("wout_bf", [P, NK2, D], BF16))
                ygs = EC1(nc.sbuf_tensor("ygs", [P, NK2, TC], BF16))
                xh_bf = EC1(nc.sbuf_tensor("xh_bf", [P, ND, TC], BF16))
                ygts = EC1(nc.sbuf_tensor("ygts", [P, TC], BF16))
                ps_wo = [EC1(nc.psum_tensor(f"ps_wo{k}", [P, WOC], F32)) for k in range(2)]

                @block.gpsimd
                def _(g):
                    g.wait_ge(sem["sPH"], PH_C)
                    g.dma_start(out=wout_bf[:, :, :], in_=w_out_v).then_inc(sem["sCL"], 16)
                    g.dma_start(out=xh_bf[:, :, :], in_=xTh_v).then_inc(sem["sCL"], 16)
                    g.dma_start(out=b2_bf[:, :], in_=b2.ap()).then_inc(sem["sCL"], 16)
                    g.dma_start(out=b1c[:, :], in_=b1.ap()).then_inc(sem["sCL"], 16)

                @block.scalar
                def _(a):
                    a.wait_ge(sem["sPH"], PH_C)
                    a.wait_ge(sem["sAT2"], 1)
                    for kc in range(NK2):
                        if kc >= 1:
                            a.wait_ge(sem["sYGC"], kc)
                        a.dma_start(out=ygts[:, :], in_=ygoutB_v[:, kc, :]).then_inc(sem["sYGT"], 16)

                @block.sync
                def _(s):
                    s.wait_ge(sem["sPH"], PH_C)
                    s.wait_ge(sem["sAT"], 1)
                    for kc in range(NK2):
                        s.dma_start(out=ygs[:, kc, :], in_=ygoutA_v[:, kc, :]).then_inc(sYGL[kc % 2], 16)

                @block.vector
                def _(v):
                    v.wait_ge(sem["sPH"], PH_C)
                    v.wait_ge(sem["sWL"], 16 * 9)
                    for kc in range(NK2):
                        v.wait_ge(sYGL[kc % 2], 16 * (kc // 2 + 1))
                        v.wait_ge(sem["sYGT"], 16 * (kc + 1))
                        v.tensor_scalar(out=ygs[:, kc, :], in0=ygs[:, kc, :], scalar1=selc[:, 0:1],
                                        scalar2=None, op0=ALU.mult)
                        v.scalar_tensor_tensor(out=ygs[:, kc, :], in0=ygts[:, :],
                                               scalar=selc[:, 1:2], in1=ygs[:, kc, :],
                                               op0=ALU.mult, op1=ALU.add).then_inc(sem["sYGC"], 1)
                    v.wait_ge(sem["sCL"], 16 * 4)
                    for mt in range(ND):
                        for nch in range(NWOC):
                            wo = mt * NWOC + nch
                            v.wait_ge(sem["sWO"], wo + 1)
                            v.tensor_tensor(out=x_res[:, mt, nch * WOC:(nch + 1) * WOC],
                                            in0=ps_wo[wo % 2][:, :],
                                            in1=xh_bf[:, mt, nch * WOC:(nch + 1) * WOC],
                                            op=ALU.add).then_inc(sem["sXR"], 1)

                @block.tensor
                def _(t):
                    t.wait_ge(sem["sPH"], PH_C)
                    t.wait_ge(sem["sCL"], 16 * 4)
                    t.wait_ge(sem["sYGC"], NK2)
                    for mt in range(ND):
                        for nch in range(NWOC):
                            wo = mt * NWOC + nch
                            if wo >= 2:
                                t.wait_ge(sem["sXR"], wo - 1)
                            for kc in range(NK2):
                                t.matmul(ps_wo[wo % 2][:, :], wout_bf[:, kc, mt * P:(mt + 1) * P],
                                         ygs[:, kc, nch * WOC:(nch + 1) * WOC],
                                         start=(kc == 0), stop=(kc == NK2 - 1)).then_inc(
                                             sem["sWO"], 1 if kc == NK2 - 1 else 0)

            with ExitStack() as pc2:
                EC2 = pc2.enter_context
                xn2 = EC(nc.sbuf_tensor("xn2", [P, ND, TC], BF16))
                sq2 = EC2(nc.sbuf_tensor("sq2", [P, ND, TC], BF16))
                m2_bc = EC2(nc.sbuf_tensor("m2_bc", [P, TC], F32))
                r2_bc = EC2(nc.sbuf_tensor("r2_bc", [P, TC], F32))
                m2r = EC2(nc.sbuf_tensor("m2r", [1, 2, NWC], F32))
                e22r = EC2(nc.sbuf_tensor("e22r", [1, 2, NWC], F32))
                v2r = EC2(nc.sbuf_tensor("v2r", [1, NWC], F32))
                r2r = EC2(nc.sbuf_tensor("r2r", [1, 2, NWC], F32))
                tsub2 = EC2(nc.sbuf_tensor("tsub2", [P, TC], F32))

                @block.scalar
                def _(a):
                    a.wait_ge(sem["sXR"], ND * NWOC)
                    a.activation(out=sq2[:, :, :], in_=x_res[:, :, :], func=AF.Square).then_inc(sem["sSQ2"], 1)
                    for nch in range(NCCH):
                        pp = nch % 2
                        a.wait_ge(sem["sST2"], nch + 1)
                        if nch >= 2:
                            a.wait_ge(sem["sRW2"], 32 * (nch - 1))
                        a.activation(out=m2r[:, pp, :], in_=ps_s2x[:, :], func=AF.Identity, scale=1.0 / D)
                        a.activation(out=e22r[:, pp, :], in_=ps_s2q[:, :], func=AF.Identity, scale=1.0 / D).then_inc(sem["sEV2"], 1)
                        a.wait_ge(sem["sVAR2"], nch + 1)
                        a.activation(out=r2r[:, pp, :], in_=v2r[:, :], func=AF.Ln, bias=eps_c[0:1, :])
                        a.activation(out=r2r[:, pp, :], in_=r2r[:, pp, :], func=AF.Exp, scale=-0.5).then_inc(sem["sRS2"], 1)

                @block.vector
                def _(v):
                    for nch in range(NCCH):
                        pp = nch % 2
                        v.wait_ge(sem["sEV2"], nch + 1)
                        v.tensor_tensor(out=v2r[:, :], in0=m2r[:, pp, :], in1=m2r[:, pp, :], op=ALU.mult)
                        v.tensor_tensor(out=v2r[:, :], in0=e22r[:, pp, :], in1=v2r[:, :], op=ALU.subtract).then_inc(sem["sVAR2"], 1)
                    v.wait_ge(sem["sRB2"], 32)
                    for i in range(ND):
                        v.tensor_tensor(out=tsub2[:, :], in0=x_res[:, i, :], in1=m2_bc[:, :], op=ALU.subtract)
                        v.tensor_tensor(out=xn2[:, i, :], in0=tsub2[:, :], in1=r2_bc[:, :], op=ALU.mult).then_inc(sem["sXN"], 1)

                @block.tensor
                def _(t):
                    t.wait_ge(sem["sSQ2"], 1)
                    for nch in range(NCCH):
                        if nch >= 1:
                            t.wait_ge(sem["sEV2"], nch)
                        for kt in range(ND):
                            t.matmul(ps_s2x[:, :], ones_bf[:, :], x_res[:, kt, nch * NWC:(nch + 1) * NWC],
                                     start=(kt == 0), stop=(kt == ND - 1))
                        for kt in range(ND):
                            t.matmul(ps_s2q[:, :], ones_bf[:, :], sq2[:, kt, nch * NWC:(nch + 1) * NWC],
                                     start=(kt == 0), stop=(kt == ND - 1)).then_inc(
                                         sem["sST2"], 1 if kt == ND - 1 else 0)

                @block.sync
                def _(s):
                    for nch in range(NCCH):
                        pp = nch % 2
                        s.wait_ge(sem["sRS2"], nch + 1)
                        s.dma_start(out=row2_d.ap()[0:1, nch * NWC:(nch + 1) * NWC],
                                    in_=m2r[:, pp, :]).then_inc(sem["sRW2"], 16)
                        s.dma_start(out=row2_d.ap()[1:2, nch * NWC:(nch + 1) * NWC],
                                    in_=r2r[:, pp, :]).then_inc(sem["sRW2"], 16)
                    s.wait_ge(sem["sRW2"], 32 * NCCH)
                    s.dma_start(out=m2_bc[:, :], in_=row2_d.ap()[0:1, :].broadcast_to([P, TC])).then_inc(sem["sRB2"], 16)
                    s.dma_start(out=r2_bc[:, :], in_=row2_d.ap()[1:2, :].broadcast_to([P, TC])).then_inc(sem["sRB2"], 16)

            wq1 = EC(nc.sbuf_tensor("wq1", [P, ND, DFF], BF16))
            with ExitStack() as pc3:
                EC3 = pc3.enter_context
                wtile = EC3(nc.sbuf_tensor("wtile", [P, 2, DFF], F32))
                mp_s = EC3(nc.sbuf_tensor("mp_s", [P, DFF], F32))
                wqt = EC3(nc.sbuf_tensor("wqt", [P, DFF], BF16))

                def _emit_qload(eng, q, sem_ld):
                    if q >= 2:
                        if q - 2 < ND:
                            eng.wait_ge(sem["sQ1"], q - 1)
                        else:
                            eng.wait_ge(sem["sQ2"], q - ND - 1)
                    if q < ND:
                        eng.dma_start(out=wtile[:, q % 2, :], in_=w1t_v[:, q, :]).then_inc(sem_ld, 16)
                    else:
                        eng.dma_start(out=wtile[:, q % 2, 0:D], in_=w2t_v[:, q - ND, :]).then_inc(sem_ld, 16)

                def _wait_qload(v, q):
                    if q % 2 == 0:
                        v.wait_ge(sem["sQL"], 16 * (q // 2 + 1))
                    else:
                        v.wait_ge(sem["sQLo"], 16 * ((q - 1) // 2 + 1))

                @block.sync
                def _(s):
                    s.wait_ge(sem["sXN"], ND)
                    s.wait_ge(sem["sST2"], NCCH)
                    for q in range(0, ND + NF, 2):
                        _emit_qload(s, q, sem["sQL"])

                @block.vector
                def _(v):
                    v.wait_ge(sem["sTH"], 18)
                    for kt in range(ND):
                        _wait_qload(v, kt)
                        v.tensor_scalar(out=mp_s[:, :], in0=wtile[:, kt % 2, :], scalar1=thr_c[:, 0:1],
                                        scalar2=None, op0=ALU.is_ge)
                        v.scalar_tensor_tensor(out=mp_s[:, :], in0=wtile[:, kt % 2, :], scalar=nthr_c[:, 0:1],
                                               in1=mp_s[:, :], op0=ALU.is_gt, op1=ALU.add)
                        v.tensor_scalar(out=wq1[:, kt, :], in0=mp_s[:, :], scalar1=-1.0,
                                        scalar2=None, op0=ALU.add).then_inc(sem["sQ1"], 1)
                    v.wait_ge(sem["sCHB"], 32)
                    for kt in range(NF):
                        q = ND + kt
                        _wait_qload(v, q)
                        if kt >= 2:
                            v.wait_ge(sem["sQ2S"], 16 * (kt - 1))
                        v.scalar_tensor_tensor(out=mp_s[:, 0:D], in0=wtile[:, q % 2, 0:D],
                                               scalar=chl_c[:, 0:1], op0=ALU.subtract,
                                               in1=chl_c[:, 1:2].broadcast_to([P, D]), op1=ALU.is_gt)
                        v.scalar_tensor_tensor(out=mp_s[:, D:2 * D], in0=wtile[:, q % 2, 0:D],
                                               scalar=chl_c[:, 0:1], op0=ALU.add,
                                               in1=chl_c[:, 2:3].broadcast_to([P, D]), op1=ALU.is_gt)
                        v.scalar_tensor_tensor(out=wqt[:, (kt % 2) * D:(kt % 2) * D + D], in0=mp_s[:, 0:D],
                                               scalar=-1.0, op0=ALU.add,
                                               in1=mp_s[:, D:2 * D], op1=ALU.add).then_inc(sem["sQ2"], 1)

                @block.gpsimd
                def _(g):
                    g.wait_ge(sem["sXN"], ND)
                    g.wait_ge(sem["sST2"], NCCH)
                    for q in range(1, ND + 2, 2):
                        _emit_qload(g, q, sem["sQLo"])
                    for kt in range(NF):
                        q2 = ND + kt + 2
                        if q2 < ND + NF and q2 % 2 == 1:
                            _emit_qload(g, q2, sem["sQLo"])
                        g.wait_ge(sem["sQ2"], kt + 1)
                        g.dma_start(out=wq2_v[:, kt, :],
                                    in_=wqt[:, (kt % 2) * D:(kt % 2) * D + D]).then_inc(sem["sQ2S"], 16)
                    if dbg:
                        g.wait_ge(sem["sXR"], ND * NWOC)
                        g.dma_start(out=dbg_xres.ap().rearrange("(i p) t -> p i t", p=P),
                                    in_=x_res[:, :, :]).then_inc(sem["sOSD"], 16)
                        g.dma_start(out=dbg_xc.ap(), in_=xc_dram.ap()).then_inc(sem["sOSD"], 16)
                        g.dma_start(out=dbg_z.ap(), in_=z_dram.ap()).then_inc(sem["sOSD"], 16)
                        g.dma_start(out=dbg_xdbl.ap(), in_=cc1_out.ap()).then_inc(sem["sOSD"], 16)
                        g.dma_start(out=dbg_yg.ap()[:, 0:TH], in_=ygoutA.ap()).then_inc(sem["sOSD"], 16)
                        g.dma_start(out=dbg_yg.ap()[:, TH:2 * TH], in_=ygoutB.ap()).then_inc(sem["sOSD"], 16)
                        g.dma_start(out=dbg_thr.ap(), in_=gsc[:, 2:4]).then_inc(sem["sOSD"], 16)

            with ExitStack() as pc4:
                EC4 = pc4.enter_context
                hbuf = EC4(nc.sbuf_tensor("hbuf", [P, NF, CC], BF16))
                outst = EC4(nc.sbuf_tensor("outst", [P, 2, D], F32))
                wq2s = EC4(nc.sbuf_tensor("wq2s", [P, 2, D], BF16))
                ps_f1 = [EC4(nc.psum_tensor(f"ps_f1{k}", [P, CC], F32)) for k in range(2)]
                ps_f2 = [EC4(nc.psum_tensor(f"ps_f2{k}", [P, NW2], F32)) for k in range(2 * NF2N)]

                @block.gpsimd
                def _(g):
                    g.wait_ge(sem["sQ2S"], 16 * NF)
                    for cc in range(NCC):
                        for tp in range(NTP):
                            for kt in range(NF):
                                gidx = (cc * NTP + tp) * NF + kt
                                if gidx % 2 == 1:
                                    if gidx >= 2:
                                        g.wait_ge(sem["sF2K"], gidx - 1)
                                    g.dma_start(out=wq2s[:, gidx % 2, :],
                                                in_=wq2_v[:, kt, :]).then_inc(sW2L[1], 16)

                @block.sync
                def _(s):
                    s.wait_ge(sem["sQ2S"], 16 * NF)
                    for cc in range(NCC):
                        for tp in range(NTP):
                            for kt in range(NF):
                                gidx = (cc * NTP + tp) * NF + kt
                                if gidx % 2 == 0:
                                    if gidx >= 2:
                                        s.wait_ge(sem["sF2K"], gidx - 1)
                                    s.dma_start(out=wq2s[:, gidx % 2, :],
                                                in_=wq2_v[:, kt, :]).then_inc(sW2L[0], 16)
                            for tw in range(2):
                                ts_i = cc * NTS + tp * 2 + tw
                                s.wait_ge(sem["sOE"], NF2N * (ts_i + 1))
                                t0 = ts_i * P
                                s.dma_start(out=out.ap()[t0:t0 + P, :], in_=outst[:, ts_i % 2, :]).then_inc(sOS[ts_i % 2], 16)
                    nts_all = NCC * NTS
                    s.wait_ge(sOS[0], 16 * ((nts_all + 1) // 2))
                    s.wait_ge(sOS[1], 16 * (nts_all // 2))
                    if dbg:
                        s.wait_ge(sem["sOSD"], 16 * 11)

                @block.tensor
                def _(t):
                    t.wait_ge(sem["sXN"], ND)
                    t.wait_ge(sem["sQ1"], ND)
                    t.wait_ge(sem["sQ2"], NF)
                    for cc in range(NCC):
                        for ft in range(NF):
                            fg = cc * NF + ft
                            if fg >= 2:
                                t.wait_ge(sem["sHE"], fg - 1)
                            for kt in range(ND):
                                t.matmul(ps_f1[fg % 2][:, :], wq1[:, kt, ft * P:(ft + 1) * P],
                                         xn2[:, kt, cc * CC:(cc + 1) * CC],
                                         start=(kt == 0), stop=(kt == ND - 1)).then_inc(
                                             sem["sF1"], 1 if kt == ND - 1 else 0)
                        for tp in range(NTP):
                            if not (cc == 0 and tp == 0):
                                t.wait_ge(sem["sOE"], NF2N * (cc * NTS + 2 * tp))
                            for kt in range(NF):
                                gidx = (cc * NTP + tp) * NF + kt
                                t.wait_ge(sem["sHE"], cc * NF + kt + 1)
                                t.wait_ge(sW2L[gidx % 2], 16 * (gidx // 2 + 1))
                                for tw in range(2):
                                    tcol = (tp * 2 + tw) * P
                                    for nw in range(NF2N):
                                        fk_i = t.matmul(ps_f2[tw * NF2N + nw][:, :],
                                                        hbuf[:, kt, tcol:tcol + P],
                                                        wq2s[:, gidx % 2, nw * NW2:(nw + 1) * NW2],
                                                        start=(kt == 0), stop=False)
                                fk_i.then_inc(sem["sF2K"], 1)
                            for tw in range(2):
                                ts_i = cc * NTS + tp * 2 + tw
                                tcol = (tp * 2 + tw) * P
                                for nw in range(NF2N):
                                    t.matmul(ps_f2[tw * NF2N + nw][:, :], onesr_bf[:, :],
                                             b2_bf[:, nw * NW2:(nw + 1) * NW2],
                                             start=False, stop=False)
                                    for q in range(NQS):
                                        dt_i = nw * NQS + q
                                        t.matmul(ps_f2[tw * NF2N + nw][:, :],
                                                 x_res[:, dt_i, cc * CC + tcol:cc * CC + tcol + P],
                                                 idshift[:, q, :],
                                                 start=False, stop=(q == NQS - 1)).then_inc(
                                                     sem["sF2"], 1 if q == NQS - 1 else 0)

                @block.scalar
                def _(a):
                    a.wait_ge(sem["sQ2S"], 16 * NF)
                    for cc in range(NCC):
                        if cc >= 1:
                            a.wait_ge(sem["sF2K"], cc * NTP * NF)
                        for ft in range(NF):
                            fg = cc * NF + ft
                            a.wait_ge(sem["sF1"], fg + 1)
                            a.activation(out=hbuf[:, ft, :], in_=ps_f1[fg % 2][:, :],
                                         func=AF.Relu, bias=b1c[:, ft:ft + 1]).then_inc(sem["sHE"], 1)
                        for tp in range(NTP):
                            for tw in range(2):
                                ts_i = cc * NTS + tp * 2 + tw
                                if ts_i >= 2:
                                    a.wait_ge(sOS[ts_i % 2], 16 * ((ts_i - 2) // 2 + 1))
                                for nw in range(NF2N):
                                    a.wait_ge(sem["sF2"], ts_i * NF2N + nw + 1)
                                    a.activation(out=outst[:, ts_i % 2, nw * NW2:(nw + 1) * NW2],
                                                 in_=ps_f2[tw * NF2N + nw][:, :],
                                                 func=AF.Identity).then_inc(sem["sOE"], 1)

    return nc


def _stage_inputs(ins, cfg):
    c = cfg
    x = np.ascontiguousarray(np.asarray(ins["L0_attn_x"], dtype=np.float32))
    W_in = np.asarray(ins["L0_attn_W_in"], dtype=np.float32)
    conv_w = np.asarray(ins["L0_attn_conv_w"], dtype=np.float32)
    conv_b = np.asarray(ins["L0_attn_conv_b"], dtype=np.float32)
    W_xp = np.asarray(ins["L0_attn_W_xproj"], dtype=np.float32)
    W_dt = np.asarray(ins["L0_attn_W_dt"], dtype=np.float32)
    b_dt = np.asarray(ins["L0_attn_b_dt"], dtype=np.float32)
    A_log = np.asarray(ins["L0_attn_A_log"], dtype=np.float32)
    Dp = np.asarray(ins["L0_attn_Dp"], dtype=np.float32)
    W_out = np.asarray(ins["L0_attn_W_out"], dtype=np.float32)
    W1 = np.asarray(ins["L0_ffn_W1"], dtype=np.float32)
    b1 = np.asarray(ins["L0_ffn_b1"], dtype=np.float32)
    W2 = np.asarray(ins["L0_ffn_W2"], dtype=np.float32)
    b2 = np.asarray(ins["L0_ffn_b2"], dtype=np.float32)

    DI2 = 2 * c.DI
    w2g = np.ascontiguousarray(
        W2.reshape(32, 32, 4, 32, 32).transpose(3, 2, 0, 1, 4).reshape(32 * 128, 1024))
    maps = []
    for core in range(8):
        b, k = core // 2, core % 2
        lo, hi = k * c.DI, (k + 1) * c.DI
        xT = np.ascontiguousarray(x[b].T)
        m = dict(
            xT=xT,
            xTh=np.ascontiguousarray(xT[:, k * c.TC:(k + 1) * c.TC]),
            w_in=np.ascontiguousarray(np.concatenate(
                [W_in[:, lo:hi], W_in[:, DI2 + lo:DI2 + hi]], axis=1)),
            conv_w=np.ascontiguousarray(conv_w[lo:hi, 0, :]),
            conv_b=np.ascontiguousarray(conv_b[lo:hi].reshape(c.NI, P).T),
            w_xp=np.ascontiguousarray(W_xp[lo:hi]),
            w_dt=np.ascontiguousarray(W_dt[0, lo:hi].reshape(c.NI, P).T),
            b_dt=np.ascontiguousarray(b_dt[lo:hi].reshape(c.NI, P).T),
            a_log=np.ascontiguousarray(A_log[lo:hi].reshape(c.NI, P, c.S).transpose(1, 0, 2).reshape(P, c.NI * c.S)),
            dp=np.ascontiguousarray(Dp[lo:hi].reshape(c.NI, P).T),
            w_out=np.ascontiguousarray(W_out),
            w1t=np.ascontiguousarray(W1.T),
            b1=np.ascontiguousarray(b1.reshape(c.NF, P).T),
            w2t=np.ascontiguousarray(W2.T),
            w2g=w2g,
            b2=np.ascontiguousarray(b2[None, :]),
            sel=np.ascontiguousarray(np.broadcast_to(
                np.array([1.0 - k, float(k)], dtype=np.float32), (128, 2))),
        )
        maps.append(m)
    return maps


def kernel(**inputs):
    cfg = Cfg()
    nc = build_program(cfg, dbg=False)
    in_maps = _stage_inputs(inputs, cfg)
    res = run_bass_kernel_spmd(nc, in_maps, core_ids=list(range(8)))
    B = inputs["L0_attn_x"].shape[0]
    outp = np.empty((B, cfg.T, cfg.D), dtype=np.float32)
    for core in range(8):
        b, k = core // 2, core % 2
        outp[b, k * cfg.TC:(k + 1) * cfg.TC, :] = res.results[core]["out"]
    return outp
